# version 25
# baseline (speedup 1.0000x reference)
import numpy as np
import ml_dtypes
from contextlib import ExitStack

import concourse.bass as bass
import concourse.mybir as mybir
from concourse.bass_utils import run_bass_kernel_spmd

F32 = mybir.dt.float32
BF16 = mybir.dt.bfloat16
AF = mybir.ActivationFunctionType
ALU = mybir.AluOpType
AX = mybir.AxisListType

T_TOK = 4096
C = 1024
NCH = 8
G1 = 512
NG1 = T_TOK // G1
NB2 = 4
NSLOT = 16
NEG = -1.0e30
EPS = 1e-5
SCALE = 1.0 / 32.0
KRING = 3


def slot_kcap(j):
    p, hi = j // 2, j % 2
    return 256 * (16 - p) if hi else 256 * (p + 1)


def slot_tile(j, parity):
    p, hi = j // 2, j % 2
    return (30 - 2 * p + parity) if hi else (2 * p + parity)


def score_chunks(kcap):
    out, o = [], 0
    while o < kcap:
        w = min(512, kcap - o)
        out.append((o, w))
        o += w
    return out


class Ch:

    def __init__(self, name):
        self.name, self.v = name, 0

    def tok(self):
        return [(self.name, self.v)]

    def sem_names(self):
        return [self.name]


class DCh:

    def __init__(self, name, r):
        self.name, self.r = name, r
        self.c = [0] * r
        self.seq = 0

    def snap(self):
        return [(f"{self.name}{i}", self.c[i]) for i in range(self.r) if self.c[i] > 0]

    def sem_names(self):
        return [f"{self.name}{i}" for i in range(self.r)]


class Plan:
    ENGS = ("T", "V", "S", "Y", "G")

    def __init__(self):
        self.items = {e: [] for e in self.ENGS}

    def op(self, eng, fn, waits=(), inc=None):
        ws = []
        for t in waits:
            if t:
                ws.extend(t)
        tok, semname, amt = None, None, 0
        if isinstance(inc, DCh):
            idx = inc.seq % inc.r
            inc.seq += 1
            semname = f"{inc.name}{idx}"
            if inc.c[idx] > 0:
                ws.append((semname, inc.c[idx]))
            inc.c[idx] += 16
            amt = 16
            tok = [(semname, inc.c[idx])]
        elif isinstance(inc, Ch):
            inc.v += 1
            semname, amt = inc.name, 1
            tok = [(inc.name, inc.v)]
        ws = tuple((n, v) for (n, v) in ws if v and v > 0)
        self.items[eng].append((ws, fn, semname, amt))
        return tok

    def replay(self, eng, engobj, sems):
        for ws, fn, semname, amt in self.items[eng]:
            seen = set()
            for n, v in ws:
                key = n
                if key in seen:
                    v = max(v, dict(ws).get(n, v))
                    continue
                seen.add(key)
                vv = max(val for (nn, val) in ws if nn == n)
                engobj.wait_ge(sems[n], vv)
            ins = fn(engobj)
            if semname is not None:
                ins.then_inc(sems[semname], amt)



_DBG = {}


def build_nc():
    nc = bass.Bass("TRN2", target_bir_lowering=False)
    es = ExitStack()

    d = {}
    for nm, shp, dt in [
        ("xfm", [C, T_TOK], F32), ("xcols", [C, 2048], F32), ("xrows", [2048, C], F32),
        ("wqT", [C, C], F32), ("wkT", [C, C], F32), ("wvT", [C, C], F32),
        ("wker", [C, C], F32), ("aT", [16, C], F32), ("bmat", [16, C], F32),
        ("biasb", [128, C], F32), ("maskt", [128, 256], F32),
        ("identb", [128, 128], BF16), ("identf", [128, 128], F32),
        ("onesrow", [1, 128], F32), ("onescol", [128, 1], F32),
        ("invCrow", [1, 128], F32),
    ]:
        d[nm] = nc.dram_tensor(nm, shp, dt, kind="ExternalInput").ap()
    out_h = nc.dram_tensor("out", [2048, C], F32, kind="ExternalOutput").ap()
    kT_h = nc.dram_tensor("kT_h", [NG1, 128, NCH * G1], BF16, kind="Internal").ap()
    qT_h = nc.dram_tensor("qT_h", [NB2, 128, NCH * G1], BF16, kind="Internal").ap()
    ff_h = nc.dram_tensor("ff_h", [NSLOT, 128, C], F32, kind="Internal").ap()
    scr_h = nc.dram_tensor("scr_h", [128, 1], F32, kind="Internal").ap()
    scrb_h = nc.dram_tensor("scrb_h", [1, 4], F32, kind="Internal").ap()

    chT, chV, chS, chP = Ch("sT"), Ch("sV"), Ch("sS"), Ch("sP")
    chY, chG, chQ = DCh("sY", 8), DCh("sG", 8), DCh("sQ", 8)
    chans = [chT, chV, chS, chP, chY, chG, chQ]
    P = Plan()

    sb_off = {}

    def _rec(n, scope, fn):
        b0 = nc.sbuf_base
        h = fn()
        sb_off[n] = (b0, nc.sbuf_base, scope)
        return h

    def sb(n, s, t):
        return _rec(n, "es", lambda: es.enter_context(nc.sbuf_tensor(n, s, t)))

    v_res = sb("v_res", [128, 32 * C], BF16)
    identb = sb("identb_s", [128, 128], BF16)
    identf = sb("identf_s", [128, 128], F32)
    onesrow = sb("onesrow_s", [1, 128], F32)
    invCrow = sb("invCrow_s", [1, 128], F32)
    onescol = sb("onescol_s", [128, 1], BF16)
    aTs = sb("aT_s", [16, C], BF16)
    bms = sb("bm_s", [16, C], BF16)
    scol = sb("scol", [128, 1], F32)
    iscol = sb("iscol", [128, 1], F32)
    n8scol = sb("n8scol", [128, 1], F32)
    pw = ExitStack()
    weff = pw.enter_context(nc.sbuf_tensor("weff", [128, NCH * C], BF16, side="right"))

    t_idb = P.op("Y", lambda e: e.dma_start(out=identb[:, :], in_=d["identb"][:, :]), inc=chY)
    t_idf = P.op("Y", lambda e: e.dma_start(out=identf[:, :], in_=d["identf"][:, :]), inc=chY)
    t_or = P.op("Y", lambda e: e.dma_start(out=onesrow[:, :], in_=d["onesrow"][:, :]), inc=chY)
    t_ic = P.op("Y", lambda e: e.dma_start(out=invCrow[:, :], in_=d["invCrow"][:, :]), inc=chY)
    t_oc = P.op("G", lambda e: e.dma_start(out=onescol[:, :], in_=d["onescol"][:, :]), inc=chG)
    t_aT = P.op("G", lambda e: e.dma_start(out=aTs[:, :], in_=d["aT"][:, :]), inc=chG)
    t_bm = P.op("G", lambda e: e.dma_start(out=bms[:, :], in_=d["bmat"][:, :]), inc=chG)

    p1 = ExitStack()

    def sb1(n, s, t):
        return _rec(n, "p1", lambda: p1.enter_context(nc.sbuf_tensor(n, s, t)))

    wkb = sb1("wkb", [128, NCH * C], BF16)
    wvb = sb1("wvb", [128, NCH * C], BF16)
    xg = [sb1(f"xg{i}", [128, NCH * G1], BF16) for i in range(3)]
    kst01 = sb1("kst01", [128, 2 * NCH * G1], BF16)
    kstage = [kst01[:, i * NCH * G1:(i + 1) * NCH * G1] for i in range(2)]
    x32g0 = kst01.bitcast(F32)
    sqb = [sb1(f"sq{i}", [128, G1], BF16) for i in range(2)]
    strow = sb1("strow", [1, 4 * G1], F32)
    mbc_s = sb1("mbc_s", [128, G1], F32)
    rbc_s = sb1("rbc_s", [128, G1], F32)
    xntmp = sb1("xntmp", [128, G1], F32)
    xntmp_b = sb1("xntmp_b", [128, G1], F32)
    xnt = [xntmp, xntmp_b]
    w32 = [sb1(f"w32_{i}", [128, C], F32) for i in range(2)]
    lad_u = sb1("lad_u", [128, G1], F32)
    lad_a = sb1("lad_a", [128, G1], F32)
    lad_b = sb1("lad_b", [128, G1], F32)
    ab_sb = sb1("ab_sb", [128, NCH * C], BF16)
    gcol = sb1("gcol", [128, 16], F32)
    grow = sb1("grow", [1, 256], F32)
    _DBG["p1_headroom"] = nc.sbuf_top - nc.sbuf_base

    xg0_Y = []
    for h in range(2):
        xg0_Y.append(P.op("Y", lambda e, h=h: e.dma_start(
            out=x32g0[:, h * 4 * G1:(h + 1) * 4 * G1].rearrange("p (a t) -> p a t", a=4),
            in_=d["xfm"].rearrange("(a p) t -> p a t", p=128)[:, 4 * h:4 * (h + 1), 0:G1]),
            inc=chY))
    cast_tok = {}
    for cc in range(NCH):
        if cc % 2 == 0:
            cast_tok[cc] = P.op("V", lambda e, cc=cc: e.tensor_copy(
                out=xg[0][:, cc * G1:(cc + 1) * G1], in_=x32g0[:, cc * G1:(cc + 1) * G1]),
                waits=[xg0_Y[cc // 4]], inc=chV)
        else:
            cast_tok[cc] = P.op("S", lambda e, cc=cc: e.copy(
                out=xg[0][:, cc * G1:(cc + 1) * G1], in_=x32g0[:, cc * G1:(cc + 1) * G1]),
                waits=[xg0_Y[cc // 4]], inc=chS)

    wk_Y, wk_cast = {}, {}
    wk_Y[0] = P.op("Y", lambda e: e.dma_start(
        out=x32g0[:, :].rearrange("p (a t) -> p a t", a=NCH),
        in_=d["wkT"].rearrange("(a p) x -> p a x", p=128)[:, :, 0:512]),
        waits=[cast_tok[cc] for cc in range(NCH)], inc=chY)

    def e_wk_tail():
        for h in range(2):
            if h == 1:
                wk_Y[1] = P.op("Y", lambda e: e.dma_start(
                    out=x32g0[:, :].rearrange("p (a t) -> p a t", a=NCH),
                    in_=d["wkT"].rearrange("(a p) x -> p a x", p=128)[:, :, 512:1024]),
                    waits=[wk_cast[0]], inc=chY)
            wk_cast[h] = P.op("S", lambda e, h=h: e.copy(
                out=wkb[:, :].rearrange("p (a x) -> p a x", a=NCH)[:, :, h * 512:(h + 1) * 512],
                in_=x32g0[:, :].rearrange("p (a t) -> p a t", a=NCH)),
                waits=[wk_Y[h]], inc=chS)

    wv32 = ab_sb.bitcast(F32)
    wv_cast = {}
    wv_Y = {}
    wv_Y[0] = P.op("Y", lambda e: e.dma_start(
        out=wv32[:, :].rearrange("p (a t) -> p a t", a=NCH),
        in_=d["wvT"].rearrange("(a p) x -> p a x", p=128)[:, :, 0:512]), inc=chY)

    def e_wv_tail():
        for h in range(2):
            if h == 1:
                wv_Y[1] = P.op("Y", lambda e: e.dma_start(
                    out=wv32[:, :].rearrange("p (a t) -> p a t", a=NCH),
                    in_=d["wvT"].rearrange("(a p) x -> p a x", p=128)[:, :, 512:1024]),
                    waits=wv_cast[0], inc=chY)
            toks = []
            for q in range(2):
                dst = wvb[:, :].rearrange("p (a x) -> p a x", a=NCH)[
                    :, 4 * q:4 * (q + 1), h * 512:(h + 1) * 512]
                srcv = wv32[:, :].rearrange("p (a t) -> p a t", a=NCH)[:, 4 * q:4 * (q + 1), :]
                toks.append(P.op("V", lambda e, dst=dst, srcv=srcv: e.tensor_copy(
                    out=dst, in_=srcv), waits=[wv_Y[h]], inc=chV))
            wv_cast[h] = toks
        for i in range(6):
            w = [] if i < 2 else list(wv_cast[1])
            qa_y[i] = P.op("Y", lambda e, i=i: e.dma_start(
                out=scan_buf(i), in_=d["wker"][i * 128:(i + 1) * 128, :]),
                waits=w, inc=chY)
        return wv_cast[0] + wv_cast[1]

    def scan_buf(i):
        if i in (0, 1, 6, 7):
            return w32[i % 2][:, :]
        return wv32[:, (i - 2) * C:(i - 1) * C]

    qa_y = {}
    qmax_done = {}
    y_bc = []
    v_ab_tok = {}

    def ps1(n, s, t=F32):
        return p1.enter_context(nc.psum_tensor(n, s, t))

    T0 = ps1("T0", [128, G1])
    T1 = ps1("T1", [128, G1])
    psK = [ps1(f"psK{i}", [128, G1]) for i in range(2)]
    psV = [[ps1(f"psV{i}_{h}", [128, 512]) for h in range(2)] for i in range(2)]
    s1r, s2r, tr1, tr2 = (strow[0:1, i * G1:(i + 1) * G1] for i in range(4))

    def e_scan():
        for i in range(NCH):
            if i >= 6:
                qa_y[i] = P.op("Y", lambda e, i=i: e.dma_start(
                    out=scan_buf(i), in_=d["wker"][i * 128:(i + 1) * 128, :]),
                    waits=[qmax_done[i - 6]], inc=chY)
            qmax_done[i] = P.op("V", lambda e, i=i: e.reduce_max(
                gcol[:, i:i + 1], scan_buf(i), axis=AX.X, apply_absolute_value=True),
                waits=[qa_y[i]], inc=chV)
        v_am = P.op("V", lambda e: e.reduce_max(gcol[:, 8:9], gcol[:, 0:8], axis=AX.X),
                    waits=[qmax_done[NCH - 1]], inc=chV)
        y_sc1 = P.op("Y", lambda e: e.dma_start(out=scr_h[:, :], in_=gcol[:, 8:9]),
                     waits=[v_am], inc=chY)
        y_sc2 = P.op("Y", lambda e: e.dma_start(
            out=grow[0:1, 0:128], in_=scr_h.rearrange("p one -> one p")),
            waits=[y_sc1], inc=chY)
        v_gm = P.op("V", lambda e: e.reduce_max(grow[0:1, 128:129], grow[0:1, 0:128],
                                                axis=AX.X), waits=[y_sc2], inc=chV)
        v_s = P.op("V", lambda e: e.tensor_scalar(out=grow[0:1, 129:130], in0=grow[0:1, 128:129],
                                                  scalar1=1.0 / 7.0, scalar2=None, op0=ALU.mult),
                   waits=[v_gm], inc=chV)
        v_rc = P.op("V", lambda e: e.reciprocal(out=grow[0:1, 130:131], in_=grow[0:1, 128:129]),
                    waits=[v_gm], inc=chV)
        P.op("V", lambda e: e.tensor_scalar(out=grow[0:1, 131:132], in0=grow[0:1, 130:131],
                                            scalar1=7.0, scalar2=None, op0=ALU.mult),
             waits=[v_rc], inc=chV)
        v_n8 = P.op("V", lambda e: e.tensor_scalar(out=grow[0:1, 132:133], in0=grow[0:1, 129:130],
                                                   scalar1=-8.0, scalar2=None, op0=ALU.mult),
                    waits=[v_s], inc=chV)
        y_sc3 = P.op("Y", lambda e: e.dma_start(out=scrb_h[0:1, :], in_=grow[0:1, 129:133]),
                     waits=[v_n8], inc=chY)
        for cidx, dst in ((0, scol), (2, iscol), (3, n8scol)):
            y_bc.append(P.op("Y", lambda e, cidx=cidx, dst=dst: e.dma_start(
                out=dst[:, :], in_=scrb_h[0:1, cidx:cidx + 1].broadcast_to([128, 1])),
                waits=[y_sc3], inc=chY))

    def e_ab_iter(i):
        w1 = [t_aT, t_bm]
        if i == 0:
            w1.append(vmm_S[1])
        if i - 1 in v_ab_tok:
            w1.append(v_ab_tok[i - 1])
        t1 = P.op("T", lambda e, i=i: e.matmul(
            psV[0][0][:, :], aTs[:, i * 128:(i + 1) * 128], bms[:, 0:512],
            start=True, stop=True), waits=w1, inc=chT)
        t2 = P.op("T", lambda e, i=i: e.matmul(
            psV[0][1][:, :], aTs[:, i * 128:(i + 1) * 128], bms[:, 512:1024],
            start=True, stop=True), inc=chT)
        wv_ = [t2, y_bc[2], qmax_done[min(2 + i // 2, 5)]] + wv_cast[1]
        P.op("V", lambda e, i=i: e.tensor_scalar(
            out=ab_sb[:, i * C:i * C + 512], in0=psV[0][0][:, :],
            scalar1=n8scol[:, 0:1], scalar2=None, op0=ALU.add), waits=wv_, inc=chV)
        v_ab_tok[i] = P.op("V", lambda e, i=i: e.tensor_scalar(
            out=ab_sb[:, i * C + 512:(i + 1) * C], in0=psV[0][1][:, :],
            scalar1=n8scol[:, 0:1], scalar2=None, op0=ALU.add), inc=chV)

    xg_dma, stats_T, math_V, bcast_T, bcastev_V, xn_V = {}, {}, {}, {}, {}, {}
    math_S, xn_cc = {}, {}
    vmm_T, kst_Y, kst_S, vmm_S = {}, {}, {}, {}
    sq_T, sq_S = {}, {}
    quant_V, qw_Y = {}, {}

    def e_xg(g):
        w = [xg_dma_gate.get(g)]
        xg_dma[g] = P.op("G", lambda e, g=g: e.dma_start(
            out=xg[g % 3][:, :].rearrange("p (a t) -> p a t", a=NCH),
            in_=d["xfm"].rearrange("(a p) t -> p a t", p=128)[:, :, g * G1:(g + 1) * G1]),
            waits=w, inc=chG)

    xg_dma_gate = {}

    def e_stats(g, bankfree):
        for cc in range(NCH):
            wsq = [cast_tok[cc]] if g == 0 else [xg_dma[g]]
            key = (g, cc - 2) if cc >= 2 else (g - 1, cc + 6)
            if key in sq_T:
                wsq += [sq_T[key], sq_S[key]]
            sq_S[(g, cc)] = P.op("S", lambda e, g=g, cc=cc: e.activation(
                out=sqb[cc % 2][:, :], in_=xg[g % 3][:, cc * G1:(cc + 1) * G1],
                func=AF.Square), waits=wsq, inc=chS)
            w = ([cast_tok[cc]] if g == 0 else [xg_dma[g]]) + [t_oc]
            if cc == 0:
                w.append(bankfree)
            P.op("T", lambda e, g=g, cc=cc: e.matmul(
                T0[0:1, :], onescol[:, 0:1], xg[g % 3][:, cc * G1:(cc + 1) * G1],
                start=(cc == 0), stop=(cc == NCH - 1), skip_group_check=True),
                waits=w, inc=chT)
            sq_T[(g, cc)] = P.op("T", lambda e, g=g, cc=cc: e.matmul(
                T1[0:1, :], onescol[:, 0:1], sqb[cc % 2][:, :],
                start=(cc == 0), stop=(cc == NCH - 1), skip_group_check=True),
                waits=[sq_S[(g, cc)]], inc=chT)
        stats_T[g] = chT.tok()

    def e_math(g, rows, prev=()):
        s1, s2, t1_, t2_ = rows
        v0 = P.op("V", lambda e, s1=s1: e.tensor_copy(out=s1, in_=T0[0:1, :]),
                  waits=[stats_T[g]] + list(prev), inc=chV)
        v1 = P.op("V", lambda e, s2=s2: e.tensor_copy(out=s2, in_=T1[0:1, :]), inc=chV)
        v2 = P.op("V", lambda e, t1_=t1_, s1=s1: e.tensor_tensor(
            out=t1_, in0=s1, in1=s1, op=ALU.mult), waits=[v0], inc=chV)
        v3 = P.op("V", lambda e, t1_=t1_, s2=s2: e.scalar_tensor_tensor(
            out=t1_, in0=s2, scalar=float(C), in1=t1_, op0=ALU.mult, op1=ALU.subtract),
            waits=[v1, v2], inc=chV)
        v4 = P.op("V", lambda e, t1_=t1_: e.tensor_scalar(
            out=t1_, in0=t1_, scalar1=1.0 / (C * C), scalar2=EPS,
            op0=ALU.mult, op1=ALU.add), waits=[v3], inc=chV)
        s0 = P.op("S", lambda e, t1_=t1_: e.activation(out=t1_, in_=t1_, func=AF.Ln),
                  waits=[v4], inc=chS)
        s1x = P.op("S", lambda e, t2_=t2_, t1_=t1_: e.activation(
            out=t2_, in_=t1_, func=AF.Exp, scale=-0.5), waits=[s0], inc=chS)
        math_S[g] = s1x
        return P.op("V", lambda e: e.engine_nop(), waits=[s1x], inc=chV)

    def e_bcast(g):
        P.op("T", lambda e: e.matmul(T0[:, :], invCrow[0:1, :], s1r,
                                     start=True, stop=True),
             waits=[math_V[g], stats_T[g], t_ic, t_or], inc=chT)
        bcast_T[g] = P.op("T", lambda e: e.matmul(T1[:, :], onesrow[0:1, :], tr2,
                                                  start=True, stop=True), inc=chT)

    def e_xn(g):
        P.op("V", lambda e: e.tensor_copy(out=mbc_s[:, :], in_=T0[:, :]),
             waits=[bcast_T[g]], inc=chV)
        ev = P.op("V", lambda e: e.tensor_copy(out=rbc_s[:, :], in_=T1[:, :]),
                  inc=chV)
        bcastev_V[g] = ev
        mul_t = {}
        for c0 in range(0, NCH, 2):
            subs = []
            for cc in (c0, c0 + 1):
                w = [ev] if cc < 2 else [mul_t[cc - 2]]
                if g == 0:
                    w.append(cast_tok[cc])
                subs.append(P.op("V", lambda e, g=g, cc=cc: e.tensor_tensor(
                    out=xnt[cc % 2][:, :], in0=xg[g % 3][:, cc * G1:(cc + 1) * G1],
                    in1=mbc_s[:, :], op=ALU.subtract), waits=w, inc=chV))
            for k, cc in enumerate((c0, c0 + 1)):
                mul_t[cc] = P.op("V", lambda e, g=g, cc=cc: e.tensor_tensor(
                    out=xg[g % 3][:, cc * G1:(cc + 1) * G1], in0=xnt[cc % 2][:, :],
                    in1=rbc_s[:, :], op=ALU.mult), waits=[subs[k]], inc=chV)
        xn_V[g] = mul_t[NCH - 1]
        xn_cc[g] = dict(mul_t)

    def e_kproj(g, extra=None):
        ev = {}
        for dc in range(NCH):
            for cc in range(NCH):
                w = [xn_cc[g][cc]] if dc == 0 else []
                if cc == 0:
                    if dc == 0:
                        w.append(wk_cast[0])
                        if g >= 2:
                            w.append(kst_Y[g - 2])
                    if dc == 4:
                        w.append(wk_cast[1])
                    if dc >= 2:
                        w.append(ev[dc - 2])
                    elif g >= 1:
                        w.append(kst_S[g - 1])
                P.op("T", lambda e, g=g, dc=dc, cc=cc: e.matmul(
                    psK[dc % 2][:, :],
                    wkb[:, :].rearrange("p (a x) -> p a x", a=NCH)[:, cc, dc * 128:(dc + 1) * 128],
                    xg[g % 3][:, cc * G1:(cc + 1) * G1],
                    start=(cc == 0), stop=(cc == NCH - 1)), waits=w, inc=chT)
            tmm = chT.tok()
            wk_ev = [tmm] + ([kst_S[g - 2]] if dc == 0 and g >= 2 else [])
            if g <= 1 and dc == 0:
                wk_ev.append(wk_cast[1])
            ev[dc] = P.op("S", lambda e, g=g, dc=dc: e.copy(
                out=kstage[g % 2][:, dc * G1:(dc + 1) * G1], in_=psK[dc % 2][:, :]),
                waits=wk_ev, inc=chS)
            if extra is not None:
                extra(dc)
        kst_S[g] = ev[NCH - 1]
        kst_Y[g] = P.op("Y", lambda e, g=g: e.dma_start(
            out=kT_h[g, :, :], in_=kstage[g % 2][:, :]),
            waits=[ev[NCH - 1]], inc=chY)

    def e_vproj(g):
        evs = {}
        for tt in range(4):
            for h in range(2):
                for cc in range(NCH):
                    w = [xn_cc[g][cc]] if (tt == 0 and h == 0) else []
                    if cc == 0 and h == 0:
                        if tt >= 2:
                            w.append(evs[tt - 2])
                        elif g >= 1:
                            w.append(vmm_S[g - 1])
                        if tt == 0:
                            w += wv_done
                            if g == 2:
                                w.append(v_ab_tok[NCH - 1])
                    P.op("T", lambda e, g=g, tt=tt, h=h, cc=cc: e.matmul(
                        psV[tt % 2][h][:, :],
                        xg[g % 3][:, cc * G1 + tt * 128: cc * G1 + (tt + 1) * 128],
                        wvb[:, :].rearrange("p (a x) -> p a x", a=NCH)[:, cc, h * 512:(h + 1) * 512],
                        start=(cc == 0), stop=(cc == NCH - 1)), waits=w, inc=chT)
            tmm = chT.tok()
            base = (4 * g + tt) * C
            P.op("S", lambda e, tt=tt, base=base: e.copy(
                out=v_res[:, base:base + 512], in_=psV[tt % 2][0][:, :]),
                waits=[tmm], inc=chS)
            evs[tt] = P.op("S", lambda e, tt=tt, base=base: e.copy(
                out=v_res[:, base + 512:base + 1024], in_=psV[tt % 2][1][:, :]), inc=chS)
        vmm_T[g] = chT.tok()
        vmm_S[g] = evs[3]

    def e_qdma(i):
        w = [quant_V[i - 2]] if i >= 2 else [qmax_done[i + 6]]
        qw_Y[i] = P.op("Y", lambda e, i=i: e.dma_start(
            out=w32[i % 2][:, :], in_=d["wker"][i * 128:(i + 1) * 128, :]),
            waits=w, inc=chY)

    def e_quant(i):
        vq = chV.tok()
        for hh in range(2):
            w = [qw_Y[i], vq]
            if i == 0 and hh == 0:
                w += [v_ab_tok[NCH - 1], y_bc[0], y_bc[1]]
            vq = P.op("V", lambda e, i=i, hh=hh: e.tensor_scalar(
                out=lad_u[:, :], in0=w32[i % 2][:, hh * 512:(hh + 1) * 512],
                scalar1=iscol[:, 0:1], scalar2=8.5, op0=ALU.mult, op1=ALU.add),
                waits=w, inc=chV)
            vq = P.op("V", lambda e: e.tensor_scalar(
                out=lad_a[:, :], in0=lad_u[:, :], scalar1=8.0, scalar2=8.0,
                op0=ALU.is_ge, op1=ALU.mult), waits=[vq], inc=chV)
            for st in (4.0, 2.0, 1.0):
                vq = P.op("V", lambda e: e.tensor_tensor(out=lad_b[:, :], in0=lad_u[:, :],
                                                         in1=lad_a[:, :], op=ALU.subtract),
                          waits=[vq], inc=chV)
                vq = P.op("V", lambda e, st=st: e.tensor_scalar(
                    out=lad_b[:, :], in0=lad_b[:, :], scalar1=st, scalar2=st,
                    op0=ALU.is_ge, op1=ALU.mult), waits=[vq], inc=chV)
                vq = P.op("V", lambda e: e.tensor_tensor(out=lad_a[:, :], in0=lad_a[:, :],
                                                         in1=lad_b[:, :], op=ALU.add),
                          waits=[vq], inc=chV)
            vq = P.op("V", lambda e, i=i, hh=hh: e.scalar_tensor_tensor(
                out=weff[:, i * C + hh * 512: i * C + (hh + 1) * 512], in0=lad_a[:, :],
                scalar=scol[:, 0:1], in1=ab_sb[:, i * C + hh * 512: i * C + (hh + 1) * 512],
                op0=ALU.mult, op1=ALU.add), waits=[vq], inc=chV)
        quant_V[i] = vq

    e_stats(0, None)
    math_V[0] = e_math(0, (s1r, s2r, tr1, tr2))
    e_wk_tail()
    e_bcast(0); e_xn(0)
    e_xg(1)
    WQ32R = [None, None]
    wq_h0_Y = [None]
    wq_h1_Y = [None]
    kp7tok = [None]

    def wq_hook(dc):
        if dc == NCH - 1:
            kp7tok[0] = chT.tok()
            wq_h0_Y[0] = P.op("Y", lambda e: e.dma_start(
                out=WQ32R[0][:, :].rearrange("p (a t) -> p a t", a=NCH),
                in_=d["wqT"].rearrange("(a p) x -> p a x", p=128)[:, :, 0:512]),
                waits=[kp7tok[0]], inc=chY)
            wq_h1_Y[0] = P.op("Y", lambda e: e.dma_start(
                out=WQ32R[1][:, :].rearrange("p (a t) -> p a t", a=NCH),
                in_=d["wqT"].rearrange("(a p) x -> p a x", p=128)[:, :, 512:1024]),
                waits=[quant_V[NCH - 1]], inc=chY)

    quant_sched = {2: [0], 3: [1], 4: [2], 5: [3], 6: [4, 5], 7: [6, 7]}
    qdma_sched = {3: [2], 4: [3], 5: [4], 6: [5], 7: [6, 7]}
    e_stats(1, bcastev_V[0])
    math_V[1] = e_math(1, (s1r, s2r, tr1, tr2), prev=[math_V[0], math_S[0]])
    e_bcast(1); e_xn(1)
    wv_done = e_wv_tail()
    e_kproj(0)
    e_scan()
    e_qdma(0); e_qdma(1)
    for g in range(1, NG1 + 1):
        if g + 1 <= NG1 - 1:
            xg_dma_gate[g + 1] = vmm_T[g - 2] if g >= 2 else None
            e_xg(g + 1)
        for qi in qdma_sched.get(g, []):
            e_qdma(qi)
        e_vproj(g - 1)
        if g >= 3:
            for qi in quant_sched.get(g, []):
                e_quant(qi)
        if g <= NG1 - 1:
            ex = e_ab_iter if g == 2 else (wq_hook if g == 7 else None)
            e_kproj(g, extra=ex)
            if g + 1 <= NG1 - 1:
                e_stats(g + 1, bcastev_V[g])
                math_V[g + 1] = e_math(g + 1, (s1r, s2r, tr1, tr2),
                                       prev=[math_V[g], math_S[g]])
                e_bcast(g + 1)
                e_xn(g + 1)
        if g == 2:
            for qi in quant_sched.get(g, []):
                e_quant(qi)
    p1_last = {
        "wkb": kp7tok[0], "wvb": vmm_T[NG1 - 1],
        "xg0": vmm_T[6] + xn_V[6], "xg1": vmm_T[7] + xn_V[7],
        "xg2": vmm_T[5] + xn_V[5],
        "kst_even": kst_Y[6] + kst_S[6], "kst_odd": kst_Y[7] + kst_S[7],
        "sqb": stats_T[7] + sq_S[(7, 7)],
        "strow": bcast_T[7] + math_V[7],
        "xnrow": xn_V[7],
        "w32": quant_V[7], "lad": quant_V[7], "ab": quant_V[7],
        "psT01": bcastev_V[7], "psK": kst_S[7], "psV": vmm_S[7],
    }
    bar1 = (chT.tok() + chV.tok() + chS.tok() + chY.snap() + chG.snap() + chQ.snap())
    p1.close()

    p2a = ExitStack()

    def sb2(n, s, t):
        return _rec(n, "p2a", lambda: p2a.enter_context(nc.sbuf_tensor(n, s, t)))

    def ps2(n, s, t=F32):
        return p2a.enter_context(nc.psum_tensor(n, s, t))

    wq32 = sb2("wq32", [128, 4 * C], F32)
    wqb = sb2("wqb", [128, NCH * C], BF16)
    xga = [sb2(f"xga{i}", [128, NCH * G1], BF16) for i in range(3)]
    strow2 = sb2("strow2", [1, 4 * G1], F32)
    mbc2 = sb2("mbc2", [128, G1], F32)
    rbc2 = sb2("rbc2", [128, G1], F32)
    xntmp2 = sb2("xntmp2", [128, G1], F32)
    xntmp2_b = sb2("xntmp2_b", [128, G1], F32)
    sqa = [sb2(f"sqa{i}", [128, G1], BF16) for i in range(2)]
    xnt2 = [xntmp2, xntmp2_b]
    qstage = [sb2(f"qst{i}", [128, NCH * G1], BF16) for i in range(2)]
    ffstage = [sb2(f"ffst{i}", [128, C], F32) for i in range(2)]
    wq32b = sb2("wq32b", [128, 4 * C], F32)
    biasb = sb2("biasb_s", [128, C], F32)
    U0 = ps2("U0", [128, G1])
    U1 = ps2("U1", [128, G1])
    psQ = [ps2(f"psQ{i}", [128, G1]) for i in range(2)]
    psF = [[ps2(f"psF{i}_{h}", [128, 512]) for h in range(2)] for i in range(2)]
    s1a, s2a, ta1, ta2 = (strow2[0:1, i * G1:(i + 1) * G1] for i in range(4))
    WQ32R[0] = wq32
    WQ32R[1] = wq32b
    KRESR = [None]
    QTILER = [None, None]
    MASKTR = [None]
    kres_early = []
    qt_early = {}
    t_mk_early = [None]

    def _rng(n):
        return sb_off[n][0:2]
    assert _rng("wq32") == _rng("wkb"), (sb_off["wq32"], sb_off["wkb"])
    assert _rng("wqb") == _rng("wvb")
    for i in range(3):
        assert _rng(f"xga{i}") == _rng(f"xg{i}")
    assert _rng("strow2") == (_rng("kst01")[0], _rng("kst01")[0] + 8192)
    assert _rng("xntmp2_b")[1] == _rng("kst01")[1]
    assert _rng("sqa0") == _rng("sq0") and _rng("sqa1") == _rng("sq1")
    assert _rng("qst0") == _rng("strow")
    assert (_rng("qst1")[0], _rng("qst1")[1]) == (_rng("mbc_s")[0], _rng("xntmp_b")[1])
    assert _rng("ffst0") == _rng("w32_0") and _rng("ffst1") == _rng("w32_1")
    assert _rng("wq32b")[0] >= _rng("lad_u")[0] and _rng("wq32b")[1] <= _rng("ab_sb")[1]
    assert _rng("biasb_s")[1] <= _rng("ab_sb")[1]

    wq_cast = {}

    def e_wq_casts():
        for h in range(2):
            stg = wq32 if h == 0 else wq32b
            for q in range(2):
                dst = wqb[:, :].rearrange("p (a x) -> p a x", a=NCH)[
                    :, 4 * q:4 * (q + 1), h * 512:(h + 1) * 512]
                srcv = stg[:, :].rearrange("p (a t) -> p a t", a=NCH)[:, 4 * q:4 * (q + 1), :]
                tk = P.op("S", lambda e, dst=dst, srcv=srcv: e.copy(out=dst, in_=srcv),
                          waits=[wq_h0_Y[0] if h == 0 else wq_h1_Y[0],
                                 p1_last["wvb"] if (h == 0 and q == 0) else None], inc=chS)
            wq_cast[h] = tk

    t_bb = P.op("Y", lambda e: e.dma_start(out=biasb[:, :], in_=d["biasb"][:, :]),
                waits=[p1_last["lad"]], inc=chY)

    xga_dma, stats2_T, math2_V, bcast2_T, bcastev2_V, xn2_V = {}, {}, {}, {}, {}, {}
    fmm_T, qst_Y, qst_S, ff_Y, ff_V = {}, {}, {}, {}, {}
    math2_S, ffev, xn2_cc = {}, {}, {}
    sq2_T, sq2_S = {}, {}
    xga_gate = {}

    SL = (0, 2, 1, 0)
    SLGATE = {0: p1_last["xg0"], 2: p1_last["xg2"], 1: p1_last["xg1"]}

    def e_xga(b):
        w = [SLGATE[SL[b]] if b <= 2 else None, xga_gate.get(b)]
        xga_dma[b] = P.op("G", lambda e, b=b: e.dma_start(
            out=xga[SL[b]][:, :].rearrange("p (a t) -> p a t", a=NCH),
            in_=d["xcols"].rearrange("(a p) t -> p a t", p=128)[:, :, b * G1:(b + 1) * G1]),
            waits=w, inc=chG)

    def e_stats2(b, bankfree):
        for cc in range(NCH):
            wsq = [xga_dma[b]]
            if b == 0 and cc <= 1:
                wsq.append(p1_last["sqb"])
            key = (b, cc - 2) if cc >= 2 else (b - 1, cc + 6)
            if key in sq2_T:
                wsq += [sq2_T[key], sq2_S[key]]
            sq2_S[(b, cc)] = P.op("S", lambda e, b=b, cc=cc: e.activation(
                out=sqa[cc % 2][:, :], in_=xga[SL[b]][:, cc * G1:(cc + 1) * G1],
                func=AF.Square), waits=wsq, inc=chS)
            w = [xga_dma[b]]
            if cc == 0:
                w.append(bankfree)
            P.op("T", lambda e, b=b, cc=cc: e.matmul(
                U0[0:1, :], onescol[:, 0:1], xga[SL[b]][:, cc * G1:(cc + 1) * G1],
                start=(cc == 0), stop=(cc == NCH - 1), skip_group_check=True),
                waits=w, inc=chT)
            sq2_T[(b, cc)] = P.op("T", lambda e, b=b, cc=cc: e.matmul(
                U1[0:1, :], onescol[:, 0:1], sqa[cc % 2][:, :],
                start=(cc == 0), stop=(cc == NCH - 1), skip_group_check=True),
                waits=[sq2_S[(b, cc)]] + ([bankfree] if b == 0 and cc == 0 else []),
                inc=chT)
        stats2_T[b] = chT.tok()

    def e_math2(b, prev=()):
        reg = [p1_last["kst_even"]] if b == 0 else []
        v0 = P.op("V", lambda e: e.tensor_copy(out=s1a, in_=U0[0:1, :]),
                  waits=[stats2_T[b]] + list(prev) + reg, inc=chV)
        v1 = P.op("V", lambda e: e.tensor_copy(out=s2a, in_=U1[0:1, :]),
                  waits=reg, inc=chV)
        v2 = P.op("V", lambda e: e.tensor_tensor(
            out=ta1, in0=s1a, in1=s1a, op=ALU.mult), waits=[v0], inc=chV)
        v3 = P.op("V", lambda e: e.scalar_tensor_tensor(
            out=ta1, in0=s2a, scalar=float(C), in1=ta1, op0=ALU.mult, op1=ALU.subtract),
            waits=[v1, v2], inc=chV)
        v4 = P.op("V", lambda e: e.tensor_scalar(
            out=ta1, in0=ta1, scalar1=1.0 / (C * C), scalar2=EPS,
            op0=ALU.mult, op1=ALU.add), waits=[v3], inc=chV)
        s0 = P.op("S", lambda e: e.activation(out=ta1, in_=ta1, func=AF.Ln),
                  waits=[v4], inc=chS)
        s1x = P.op("S", lambda e: e.activation(
            out=ta2, in_=ta1, func=AF.Exp, scale=-0.5), waits=[s0], inc=chS)
        math2_S[b] = s1x
        math2_V[b] = P.op("V", lambda e: e.engine_nop(), waits=[s1x], inc=chV)

    def e_bcast2(b):
        P.op("T", lambda e: e.matmul(U0[:, :], invCrow[0:1, :], s1a,
                                     start=True, stop=True),
             waits=[math2_V[b], stats2_T[b], t_ic], inc=chT)
        bcast2_T[b] = P.op("T", lambda e: e.matmul(U1[:, :], onesrow[0:1, :], ta2,
                                                   start=True, stop=True), inc=chT)

    def e_xn2(b):
        reg = [p1_last["kst_odd"]] if b == 0 else []
        P.op("V", lambda e: e.tensor_copy(out=mbc2[:, :], in_=U0[:, :]),
             waits=[bcast2_T[b]] + reg, inc=chV)
        ev = P.op("V", lambda e: e.tensor_copy(out=rbc2[:, :], in_=U1[:, :]),
                  waits=reg, inc=chV)
        bcastev2_V[b] = ev
        mul_t = {}
        for c0 in range(0, NCH, 2):
            subs = []
            for cc in (c0, c0 + 1):
                w = ([ev] if cc < 2 else [mul_t[cc - 2]]) + (reg if cc <= 1 else [])
                subs.append(P.op("V", lambda e, b=b, cc=cc: e.tensor_tensor(
                    out=xnt2[cc % 2][:, :], in0=xga[SL[b]][:, cc * G1:(cc + 1) * G1],
                    in1=mbc2[:, :], op=ALU.subtract), waits=w, inc=chV))
            for k, cc in enumerate((c0, c0 + 1)):
                mul_t[cc] = P.op("V", lambda e, b=b, cc=cc: e.tensor_tensor(
                    out=xga[SL[b]][:, cc * G1:(cc + 1) * G1], in0=xnt2[cc % 2][:, :],
                    in1=rbc2[:, :], op=ALU.mult), waits=[subs[k]], inc=chV)
        xn2_V[b] = mul_t[NCH - 1]
        xn2_cc[b] = dict(mul_t)

    def e_qproj(b):
        ev = {}
        for dc in range(NCH):
            for cc in range(NCH):
                w = [xn2_cc[b][cc]] if dc == 0 else []
                if cc == 0:
                    if dc == 0:
                        w.append(wq_cast[0])
                        if b == 0:
                            w.append(p1_last["psK"])
                        if b >= 2:
                            w.append(qst_Y[b - 2])
                    if dc == 4:
                        w.append(wq_cast[1])
                    if dc >= 2:
                        w.append(ev[dc - 2])
                    elif b >= 1:
                        w.append(qst_S[b - 1])
                P.op("T", lambda e, b=b, dc=dc, cc=cc: e.matmul(
                    psQ[dc % 2][:, :],
                    wqb[:, :].rearrange("p (a x) -> p a x", a=NCH)[:, cc, dc * 128:(dc + 1) * 128],
                    xga[SL[b]][:, cc * G1:(cc + 1) * G1],
                    start=(cc == 0), stop=(cc == NCH - 1)), waits=w, inc=chT)
            tmm = chT.tok()
            wse = [tmm] + ([qst_S[b - 2]] if dc == 0 and b >= 2 else [])
            if b <= 1 and dc == 0:
                wse.append(p1_last["strow"] if b == 0 else p1_last["xnrow"])
            ev[dc] = P.op("S", lambda e, b=b, dc=dc: e.activation(
                out=qstage[b % 2][:, :].rearrange(
                    "p (o a t) -> p o a t", o=4, a=NCH)[:, :, dc, :],
                in_=psQ[dc % 2][:, :].rearrange("p (o t) -> p o t", o=4),
                func=AF.Copy, scale=SCALE),
                waits=wse, inc=chS)
        qst_S[b] = ev[NCH - 1]
        qst_Y[b] = P.op("Y", lambda e, b=b: e.dma_start(
            out=qT_h[b, :, :], in_=qstage[b % 2][:, :]),
            waits=[ev[NCH - 1]], inc=chY)

    def e_ffn(b):
        evs = {}
        for tt in range(4):
            j = 4 * b + tt
            for h in range(2):
                for cc in range(NCH):
                    w = [xn2_cc[b][cc]] if (tt == 0 and h == 0) else []
                    if cc == 0 and h == 0:
                        if b == 0 and tt == 0:
                            w += [quant_V[NCH - 1], p1_last["psV"]]
                        if tt >= 2:
                            w.append(evs[tt - 2])
                        elif b >= 1:
                            w.append(ff_V[b - 1])
                    P.op("T", lambda e, b=b, tt=tt, h=h, cc=cc: e.matmul(
                        psF[tt % 2][h][:, :],
                        xga[SL[b]][:, cc * G1 + tt * 128: cc * G1 + (tt + 1) * 128],
                        weff[:, :].rearrange("p (a x) -> p a x", a=NCH)[:, cc, h * 512:(h + 1) * 512],
                        start=(cc == 0), stop=(cc == NCH - 1)), waits=w, inc=chT)
            tmm = chT.tok()
            wv_ = [tmm, t_bb]
            if j == 0:
                wv_.append(p1_last["w32"])
            if j >= 2:
                wv_.append(ff_Y[j - 2])
            if j >= 2:
                wv_.append(ffev[j - 2])
            P.op("V", lambda e, tt=tt: e.tensor_tensor(
                out=ffstage[tt % 2][:, 0:512], in0=psF[tt % 2][0][:, :],
                in1=biasb[:, 0:512], op=ALU.add), waits=wv_, inc=chV)
            evs[tt] = P.op("V", lambda e, tt=tt: e.tensor_tensor(
                out=ffstage[tt % 2][:, 512:1024], in0=psF[tt % 2][1][:, :],
                in1=biasb[:, 512:1024], op=ALU.add), inc=chV)
            ffev[j] = evs[tt]
            ff_Y[j] = P.op("G", lambda e, j=j, tt=tt: e.dma_start(
                out=ff_h[j, :, :], in_=ffstage[tt % 2][:, :]),
                waits=[evs[tt]], inc=chG)
        fmm_T[b] = chT.tok()
        ff_V[b] = evs[3]

    e_xga(1)
    e_xga(0)
    e_stats2(0, p1_last["psT01"]); e_math2(0); e_bcast2(0); e_xn2(0)
    e_wq_casts()
    kres_early.append(P.op("Y", lambda e: e.dma_start(
        out=KRESR[0][:, 0:NCH * 512], in_=kT_h[0, :, :]),
        waits=[wq_cast[0], kst_Y[0]], inc=chY))
    kres_early.append(P.op("Y", lambda e: e.dma_start(
        out=KRESR[0][:, NCH * 512:], in_=kT_h[1, :, :]),
        waits=[kst_Y[1]], inc=chY))
    e_stats2(1, bcastev2_V[0]); e_math2(1, prev=[math2_V[0], math2_S[0]])
    for b in range(NB2):
        if b >= 1:
            e_ffn(b - 1)
        e_qproj(b)
        if b == NB2 - 1:
            qp3_T = chT.tok()
            ORD_ = sorted(range(NSLOT), key=lambda j: -slot_kcap(j))
            for si_ in range(2):
                j_ = ORD_[si_]
                qt_early[si_] = P.op("Y", lambda e, si_=si_, j_=j_: e.dma_start(
                    out=QTILER[si_][:, :],
                    in_=qT_h[j_ // 4, :, :][:, (j_ % 4) * 1024:(j_ % 4 + 1) * 1024]),
                    waits=[qp3_T, qst_Y[j_ // 4]], inc=chY)
            t_mk_early[0] = P.op("Y", lambda e: e.dma_start(
                out=MASKTR[0][:, :], in_=d["maskt"][:, :]), waits=[qp3_T], inc=chY)
        if b + 1 <= NB2 - 1:
            e_bcast2(b + 1)
            e_xn2(b + 1)
        if b + 2 <= NB2 - 1:
            xga_gate[b + 2] = fmm_T[b - 1] if b >= 1 else None
            e_xga(b + 2)
            e_stats2(b + 2, bcastev2_V[b + 1])
            e_math2(b + 2, prev=[math2_V[b + 1], math2_S[b + 1]])
    e_ffn(NB2 - 1)
    bar2 = (chT.tok() + chV.tok() + chS.tok() + chY.snap() + chG.snap() + chQ.snap())
    p2a.close()

    pw.close()
    p2b = ExitStack()

    def sb3(n, s, t):
        return _rec(n, "p2b", lambda: p2b.enter_context(nc.sbuf_tensor(n, s, t)))

    def ps3(n, s, t=F32):
        return p2b.enter_context(nc.psum_tensor(n, s, t))

    k_res = sb3("k_res", [128, 2 * NCH * 512], BF16)
    qtile = [sb3(f"qtile{i}", [128, NCH * 128], BF16) for i in range(4)]
    maskt = sb3("maskt_s", [128, 256], F32)
    scores = [[sb3(f"scores{p}_{i}", [128, 4096], BF16) for i in range(2)]
              for p in range(2)]
    pbuf = [sb3(f"pbuf{i}", [128, 4096], BF16) for i in range(2)]
    kbuf = [sb3(f"kbuf{i}", [128, NCH * 512], BF16) for i in range(KRING)]
    fftile = [sb3(f"fft{i}", [128, C], F32) for i in range(3)]
    xrtile = [sb3(f"xrt{i}", [128, C], F32) for i in range(3)]
    outsb = [sb3(f"outsb{i}", [128, C], F32) for i in range(2)]
    ptb = [sb3(f"ptb{i}", [128, 128], BF16) for i in range(3)]
    mxv = sb3("mxv", [128, 1], F32)
    nbias = sb3("nbias", [128, 1], F32)
    lsum = sb3("lsum", [128, 4], F32)
    invl = [sb3(f"invl{i}", [128, 1], F32) for i in range(2)]

    psS = [ps3(f"psS{a}", [128, 512]) for a in range(2)]
    psT_ = [ps3(f"psT{i}", [128, 128], BF16) for i in range(2)]
    psAV = [[ps3(f"psAV{i}_{h}", [128, 512]) for h in range(2)] for i in range(2)]

    ORDER = sorted(range(NSLOT), key=lambda j: -slot_kcap(j))
    NP2 = NSLOT // 2

    def pr_slots(k):
        return ORDER[2 * k], ORDER[2 * k + 1]

    KQ = []
    for k in range(NP2):
        jA, _ = pr_slots(k)
        for ci, (off, wdt) in enumerate(score_chunks(slot_kcap(jA))):
            if off >= 1024:
                KQ.append((k, ci, off, wdt))
    kq_pos = {(k, ci): q for q, (k, ci, _, _) in enumerate(KQ)}
    kdma, kcons = {}, {}
    kq_emit = [0]
    mxcols = [sb3(f"mxcols{i}", [128, 8], F32) for i in range(2)]

    KRESR[0] = k_res
    QTILER[0], QTILER[1] = qtile[0], qtile[1]
    MASKTR[0] = maskt
    assert _rng("k_res") == _rng("wq32")
    assert _rng("qtile0")[0] == _rng("wqb")[0] and _rng("maskt_s")[1] <= _rng("wqb")[1]

    kres_dma = kres_early
    t_mk = t_mk_early[0]

    def pump_k(upto):
        while kq_emit[0] < min(upto, len(KQ)):
            q = kq_emit[0]
            k, ci, off, wdt = KQ[q]
            g = off // 512
            buf = q % KRING
            w = [bar2] if q < KRING else [kcons[q - KRING]]
            qname, qch = "Y", chY
            if wdt == 512:
                v = P.op(qname, lambda e, g=g, buf=buf: e.dma_start(
                    out=kbuf[buf][:, :], in_=kT_h[g, :, :]), waits=w, inc=qch)
            else:
                v = P.op(qname, lambda e, g=g, buf=buf: e.dma_start(
                    out=kbuf[buf][:, :].rearrange("p (a t) -> p a t", a=NCH)[:, :, 0:256],
                    in_=kT_h[g, :, :].rearrange("p (a t) -> p a t", a=NCH)[:, :, 0:256]),
                    waits=w, inc=qch)
            kdma[q] = (v, buf)
            kq_emit[0] += 1

    qt_dma, ff_dma, xr_dma = {}, {}, {}
    qt_cons, ff_cons, xr_cons = {}, {}, {}
    sc_mx_V, exp_S, inv_V, epi_V, mx_fin, exp_half = {}, {}, {}, {}, {}, {}
    trans_cons, av_T, out_G, sc_fin_T = {}, {}, {}, {}
    sc_prev = {}
    ptbg = {}

    def e_sdma_qt(k):
        for a in range(2):
            j = pr_slots(k)[a]
            si = 2 * k + a
            if si in qt_early:
                qt_dma[si] = qt_early[si]
                continue
            w0 = [qp3_T, qst_Y[j // 4]] if k < 2 else []
            qt_dma[si] = P.op("Y", lambda e, si=si, j=j: e.dma_start(
                out=qtile[si % 4][:, :],
                in_=qT_h[j // 4, :, :][:, (j % 4) * 1024:(j % 4 + 1) * 1024]),
                waits=w0 + ([qt_cons[si - 4]] if si >= 4 else []), inc=chY)

    def e_sdma_fx(k):
        w0 = [bar2] if k < 2 else []
        for a in range(2):
            j = pr_slots(k)[a]
            si = 2 * k + a
            ff_dma[si] = P.op("G", lambda e, si=si, j=j: e.dma_start(
                out=fftile[si % 3][:, :], in_=ff_h[j, :, :]),
                waits=w0 + ([ff_cons[si - 3]] if si >= 3 else []), inc=chG)
            xr_dma[si] = P.op("G", lambda e, si=si, j=j: e.dma_start(
                out=xrtile[si % 3][:, :], in_=d["xrows"][j * 128:(j + 1) * 128, :]),
                waits=w0 + ([xr_cons[si - 3]] if si >= 3 else []), inc=chG)

    def e_scores(k):
        jA, jB = pr_slots(k)
        kcapA, kcapB = slot_kcap(jA), slot_kcap(jB)
        chunks = score_chunks(kcapA)
        nchA = len(chunks)
        nchB = len(score_chunks(kcapB))
        kp = k % 2
        mx_prevA = mx_fin.get(2 * (k - 1))
        mx_prevB = mx_fin.get(2 * (k - 1) + 1)
        for ci, (off, wdt) in enumerate(chunks):
            if off < 1024:
                q = None
                v = kres_dma[0] if off < 512 else (kres_dma[0] + kres_dma[1])
                buf = -1
            else:
                q = kq_pos[(k, ci)]
                pump_k(q + 1)
                v, buf = kdma[q]
            for a, (jx, kcx, mx_prev) in enumerate(
                    [(jA, kcapA, mx_prevA), (jB, kcapB, mx_prevB)]):
                wdx = min(wdt, kcx - off)
                if wdx <= 0:
                    continue
                si = 2 * k + a
                for dc in range(NCH):
                    w = []
                    if dc == 0:
                        w.append(v)
                        prev = sc_prev.get(a)
                        if prev is not None:
                            w.append(prev)
                        if ci == 0:
                            w.append(qt_dma[si])
                            if k == 0:
                                w.append(bcastev2_V[NB2 - 1])
                    if buf < 0:
                        kap = k_res[:, :].rearrange("p (g a t) -> p g a t", g=2, a=NCH)[
                            :, off // 512, dc, 0:wdx]
                    else:
                        kap = kbuf[buf][:, :].rearrange("p (a t) -> p a t", a=NCH)[
                            :, dc, 0:wdx]
                    P.op("T", lambda e, si=si, a=a, dc=dc, wdx=wdx, kap=kap: e.matmul(
                        psS[a][:, 0:wdx],
                        qtile[si % 4][:, dc * 128:(dc + 1) * 128],
                        kap,
                        start=(dc == 0), stop=(dc == NCH - 1)), waits=w, inc=chT)
                if a == 1 and q is not None:
                    kcons[q] = chT.tok()
                wev = [chT.tok()]
                if ci == 0 and k >= 1:
                    wev.append(sc_mx_V.get(2 * (k - 1) + a))
                if ci == 0 and k >= 2:
                    wev.append(exp_S.get(2 * (k - 2) + a))
                nchx = nchA if a == 0 else nchB
                if ci < nchx - 1:
                    ev = P.op("V", lambda e, kp=kp, a=a, off=off, wdx=wdx: e.tensor_copy(
                        out=scores[kp][a][:, off:off + wdx], in_=psS[a][:, 0:wdx]),
                        waits=wev, inc=chV)
                    sc_prev[a] = ev
                    P.op("V", lambda e, kp=kp, a=a, ci=ci, off=off, wdx=wdx: e.reduce_max(
                        mxcols[a][:, ci:ci + 1], scores[kp][a][:, off:off + wdx], axis=AX.X),
                        waits=[ev, mx_prev], inc=chV)
                else:
                    evl = []
                    if wdx > 256:
                        evl.append(P.op("V", lambda e, kp=kp, a=a, off=off, wdx=wdx:
                                        e.tensor_copy(
                            out=scores[kp][a][:, off:off + wdx - 256],
                            in_=psS[a][:, 0:wdx - 256]), waits=wev, inc=chV))
                    evl.append(P.op("V", lambda e, kp=kp, a=a, kcx=kcx, wdx=wdx:
                                    e.tensor_tensor(
                        out=scores[kp][a][:, kcx - 256:kcx],
                        in0=psS[a][:, wdx - 256:wdx],
                        in1=maskt[:, :], op=ALU.add),
                        waits=wev + [t_mk, mx_prev], inc=chV))
                    ev = evl[-1]
                    sc_prev[a] = ev
                    P.op("V", lambda e, kp=kp, a=a, ci=ci, off=off, wdx=wdx: e.reduce_max(
                        mxcols[a][:, ci:ci + 1], scores[kp][a][:, off:off + wdx], axis=AX.X),
                        waits=evl + [mx_prev], inc=chV)
                if a == 0:
                    sc_mx_V[2 * k] = chV.tok()
                else:
                    sc_mx_V[2 * k + 1] = chV.tok()
            if kcapB - off <= 0 and q is not None:
                kcons[q] = chT.tok()
            if q is not None:
                pump_k(q + KRING + 1)
        sc_fin_T[k] = chT.tok()
        qt_cons[2 * k] = sc_fin_T[k]
        qt_cons[2 * k + 1] = sc_fin_T[k]

    def e_sm_pre(si):
        a = si % 2
        kcap = slot_kcap(ORDER[si])
        nch = len(score_chunks(kcap))
        v1 = P.op("V", lambda e, a=a, nch=nch: e.reduce_max(
            nbias[:, :], mxcols[a][:, 0:nch], axis=AX.X, negate=True),
            waits=[sc_mx_V[si], exp_S.get(si - 1)], inc=chV)
        mx_fin[si] = v1
        w = [v1, exp_S.get(si - 1)]
        if si >= 2:
            w.append(trans_cons.get(si - 2))
        half = (kcap // 256) * 128
        kp = (si // 2) % 2
        exp_half[si] = P.op("S", lambda e, kp=kp, a=a, half=half: e.activation(
            out=pbuf[a][:, 0:half], in_=scores[kp][a][:, 0:half], func=AF.Exp,
            bias=nbias[:, 0:1], scale=1.0, accum_out=lsum[:, 2 * a:2 * a + 1]), waits=w, inc=chS)
        exp_S[si] = P.op("S", lambda e, kp=kp, a=a, half=half, kcap=kcap: e.activation(
            out=pbuf[a][:, half:kcap], in_=scores[kp][a][:, half:kcap], func=AF.Exp,
            bias=nbias[:, 0:1], scale=1.0, accum_out=lsum[:, 2 * a + 1:2 * a + 2]), inc=chS)

    def e_sm_post(si):
        a = si % 2
        vsum = P.op("V", lambda e, a=a: e.tensor_tensor(
            out=lsum[:, 2 * a:2 * a + 1], in0=lsum[:, 2 * a:2 * a + 1],
            in1=lsum[:, 2 * a + 1:2 * a + 2], op=ALU.add),
            waits=[exp_S[si], inv_V.get(si - 1)], inc=chV)
        inv_V[si] = P.op("V", lambda e, si=si, a=a: e.reciprocal(
            out=invl[si % 2][:, :], in_=lsum[:, 2 * a:2 * a + 1]),
            waits=[vsum, epi_V.get(si - 2)], inc=chV)

    def e_trans_av(si):
        a = si % 2
        kcap = slot_kcap(ORDER[si])
        nk = kcap // 128
        ptb_ev = {}
        av_lastk = {}

        half_nk = (kcap // 256)
        def em_tr(kc):
            w = []
            if kc == 0:
                w = [exp_half[si], t_idb]
                if si == 0:
                    w.append(qst_S[NB2 - 1])
            elif kc == half_nk:
                w = [exp_S[si]]
            tt = P.op("T", lambda e, a=a, kc=kc: e.transpose(
                psT_[kc % 2][:, :], pbuf[a][:, kc * 128:(kc + 1) * 128], identb[:, :]),
                waits=w, inc=chT)
            wv = [tt, ptbg.get(kc % 3)]
            if kc >= 3:
                wv.append(av_lastk[kc - 3])
            ptb_ev[kc] = P.op("S", lambda e, kc=kc: e.copy(
                out=ptb[kc % 3][:, :], in_=psT_[kc % 2][:, :]), waits=wv, inc=chS)
            ptbg[kc % 3] = ptb_ev[kc]

        def em_av(kc):
            for h in range(2):
                w2 = [ptb_ev[kc]]
                if kc == 0 and h == 0 and si == 0:
                    w2.append(ffev[4 * NB2 - 1])
                if kc == 0 and h == 0 and si >= 2:
                    w2.append(epi_V.get(si - 2))
                av_lastk[kc] = P.op("T", lambda e, si=si, kc=kc, h=h, nk=nk: e.matmul(
                    psAV[si % 2][h][:, :], ptb[kc % 3][:, :],
                    v_res[:, kc * C + h * 512: kc * C + (h + 1) * 512],
                    start=(kc == 0), stop=(kc == nk - 1), skip_group_check=True),
                    waits=w2, inc=chT)

        em_tr(0)
        for kc in range(1, nk):
            em_tr(kc)
            em_av(kc - 1)
        em_av(nk - 1)
        trans_cons[si] = chT.tok()
        av_T[si] = chT.tok()

    def e_epilogue(si):
        j = ORDER[si]
        w = [av_T[si], inv_V[si], ff_dma[si], xr_dma[si]]
        if si >= 2:
            w.append(out_G[si - 2])
        P.op("V", lambda e, si=si: e.scalar_tensor_tensor(
            out=outsb[si % 2][:, 0:512], in0=psAV[si % 2][0][:, :], scalar=invl[si % 2][:, 0:1],
            in1=fftile[si % 3][:, 0:512], op0=ALU.mult, op1=ALU.add), waits=w, inc=chV)
        v2 = P.op("V", lambda e, si=si: e.scalar_tensor_tensor(
            out=outsb[si % 2][:, 512:1024], in0=psAV[si % 2][1][:, :], scalar=invl[si % 2][:, 0:1],
            in1=fftile[si % 3][:, 512:1024], op0=ALU.mult, op1=ALU.add), inc=chV)
        ff_cons[si] = v2
        v3 = P.op("V", lambda e, si=si: e.tensor_tensor(
            out=outsb[si % 2][:, :], in0=outsb[si % 2][:, :], in1=xrtile[si % 3][:, :],
            op=ALU.add), waits=[v2], inc=chV)
        xr_cons[si] = v3
        epi_V[si] = v3
        out_G[si] = P.op("G", lambda e, j=j, si=si: e.dma_start(
            out=out_h[j * 128:(j + 1) * 128, :], in_=outsb[si % 2][:, :]),
            waits=[v3], inc=chG)

    e_sdma_qt(0)
    e_sdma_fx(0)
    e_scores(0)
    for k in range(NP2):
        e_sm_pre(2 * k)
        e_sm_pre(2 * k + 1)
        if k + 1 < NP2:
            e_sdma_qt(k + 1)
            e_scores(k + 1)
        e_sm_post(2 * k)
        e_sm_post(2 * k + 1)
        e_trans_av(2 * k)
        e_epilogue(2 * k)
        if k + 1 < NP2:
            e_sdma_fx(k + 1)
        e_trans_av(2 * k + 1)
        e_epilogue(2 * k + 1)

    fin = chG.snap() + chY.snap() + chQ.snap()
    P.op("G", lambda e: e.nop(), waits=[fin])
    p2b.close()

    sems = {}
    for c in chans:
        for n in c.sem_names():
            sems[n] = es.enter_context(nc.semaphore(n))
    block = es.enter_context(nc.Block())

    @block.tensor
    def _(t):
        P.replay("T", t, sems)

    @block.vector
    def _(v):
        P.replay("V", v, sems)

    @block.scalar
    def _(s):
        P.replay("S", s, sems)

    @block.sync
    def _(sy):
        P.replay("Y", sy, sems)

    @block.gpsimd
    def _(g):
        P.replay("G", g, sems)

    es.close()
    return nc



_NC_CACHE = {}


def _get_nc():
    if "nc" not in _NC_CACHE:
        _NC_CACHE["nc"] = build_nc()
    return _NC_CACHE["nc"]


def kernel(L0_attn_x, L0_attn_Wq, L0_attn_Wk, L0_attn_Wv,
           L0_ffn_kernel, L0_ffn_bias, L0_ffn_a, L0_ffn_b):
    x = np.ascontiguousarray(np.asarray(L0_attn_x), dtype=np.float32)
    Wq = np.asarray(L0_attn_Wq, dtype=np.float32)
    Wk = np.asarray(L0_attn_Wk, dtype=np.float32)
    Wv = np.asarray(L0_attn_Wv, dtype=np.float32)
    Wker = np.ascontiguousarray(np.asarray(L0_ffn_kernel), dtype=np.float32)
    bias = np.asarray(L0_ffn_bias, dtype=np.float32)
    a_m = np.asarray(L0_ffn_a, dtype=np.float32)
    b_m = np.ascontiguousarray(np.asarray(L0_ffn_b), dtype=np.float32)

    wqT = np.ascontiguousarray(Wq.T)
    wkT = np.ascontiguousarray(Wk.T)
    wvT = np.ascontiguousarray(Wv.T)
    aT = np.ascontiguousarray(a_m.T)
    biasb = np.ascontiguousarray(np.broadcast_to(bias[None, :], (128, C)))
    identb = np.eye(128, dtype=ml_dtypes.bfloat16)
    identf = np.eye(128, dtype=np.float32)
    onesrow = np.ones((1, 128), np.float32)
    onescol = np.ones((128, 1), np.float32)

    tri = np.tril(np.ones((128, 128), np.float32))
    diag_m = np.where(tri > 0, 0.0, NEG).astype(np.float32)
    full_m = np.full((128, 128), NEG, np.float32)
    zero_m = np.zeros((128, 128), np.float32)
    mask_par = [np.ascontiguousarray(np.concatenate([diag_m, full_m], axis=1)),
                np.ascontiguousarray(np.concatenate([zero_m, diag_m], axis=1))]

    nc = _get_nc()
    in_maps, rows_per_core = [], []
    for core in range(8):
        b, par = core // 2, core % 2
        tiles = [slot_tile(j, par) for j in range(NSLOT)]
        own = np.concatenate([np.arange(128 * t, 128 * (t + 1)) for t in tiles])
        rows_per_core.append(own)
        xseq = x[b]
        in_maps.append({
            "xfm": np.ascontiguousarray(xseq.T),
            "xcols": np.ascontiguousarray(xseq[own].T),
            "xrows": np.ascontiguousarray(xseq[own]),
            "wqT": wqT, "wkT": wkT, "wvT": wvT,
            "wker": Wker, "aT": aT, "bmat": b_m,
            "biasb": biasb, "maskt": mask_par[par],
            "identb": identb, "identf": identf,
            "onesrow": onesrow, "onescol": onescol,
            "invCrow": np.full((1, 128), 1.0 / C, np.float32),
        })

    res = run_bass_kernel_spmd(nc, in_maps, core_ids=list(range(8)))
    out = np.empty((4, T_TOK, C), np.float32)
    for core in range(8):
        out[core // 2, rows_per_core[core]] = res.results[core]["out"]
    return out


# version 26
# speedup vs baseline: 1.1638x; 1.1638x over previous
import numpy as np
import ml_dtypes
from contextlib import ExitStack

import concourse.bass as bass
import concourse.mybir as mybir
from concourse.bass_utils import run_bass_kernel_spmd

F32 = mybir.dt.float32
BF16 = mybir.dt.bfloat16
AF = mybir.ActivationFunctionType
ALU = mybir.AluOpType
AX = mybir.AxisListType

T_TOK = 4096
C = 1024
NCH = 8
G1 = 512
NG1 = T_TOK // G1
NB2 = 4
NSLOT = 16
NEG = -1.0e30
EPS = 1e-5
SCALE = 1.0 / 32.0
KRING = 3


def slot_kcap(j):
    p, hi = j // 2, j % 2
    return 256 * (16 - p) if hi else 256 * (p + 1)


def slot_tile(j, parity):
    p, hi = j // 2, j % 2
    return (30 - 2 * p + parity) if hi else (2 * p + parity)


def score_chunks(kcap):
    out, o = [], 0
    while o < kcap:
        w = min(512, kcap - o)
        out.append((o, w))
        o += w
    return out


class Ch:

    def __init__(self, name):
        self.name, self.v = name, 0

    def tok(self):
        return [(self.name, self.v)]

    def sem_names(self):
        return [self.name]


class DCh:

    def __init__(self, name, r):
        self.name, self.r = name, r
        self.c = [0] * r
        self.seq = 0

    def snap(self):
        return [(f"{self.name}{i}", self.c[i]) for i in range(self.r) if self.c[i] > 0]

    def sem_names(self):
        return [f"{self.name}{i}" for i in range(self.r)]


class Plan:
    ENGS = ("T", "V", "S", "Y", "G")

    def __init__(self):
        self.items = {e: [] for e in self.ENGS}

    def op(self, eng, fn, waits=(), inc=None):
        ws = []
        for t in waits:
            if t:
                ws.extend(t)
        tok, semname, amt = None, None, 0
        if isinstance(inc, DCh):
            idx = inc.seq % inc.r
            inc.seq += 1
            semname = f"{inc.name}{idx}"
            if inc.c[idx] > 0:
                ws.append((semname, inc.c[idx]))
            inc.c[idx] += 16
            amt = 16
            tok = [(semname, inc.c[idx])]
        elif isinstance(inc, Ch):
            inc.v += 1
            semname, amt = inc.name, 1
            tok = [(inc.name, inc.v)]
        ws = tuple((n, v) for (n, v) in ws if v and v > 0)
        self.items[eng].append((ws, fn, semname, amt))
        return tok

    def replay(self, eng, engobj, sems):
        for ws, fn, semname, amt in self.items[eng]:
            seen = set()
            for n, v in ws:
                key = n
                if key in seen:
                    v = max(v, dict(ws).get(n, v))
                    continue
                seen.add(key)
                vv = max(val for (nn, val) in ws if nn == n)
                engobj.wait_ge(sems[n], vv)
            ins = fn(engobj)
            if semname is not None:
                ins.then_inc(sems[semname], amt)



_DBG = {}


def build_nc():
    nc = bass.Bass("TRN2", target_bir_lowering=False)
    es = ExitStack()

    d = {}
    for nm, shp, dt in [
        ("xfm", [C, T_TOK], F32), ("xcols", [C, 2048], F32), ("xrows", [2048, C], F32),
        ("wqT", [C, C], F32), ("wkT", [C, C], F32), ("wvT", [C, C], F32),
        ("wker", [C, C], F32), ("aT", [16, C], F32), ("bmat", [16, C], F32),
        ("biasb", [128, C], F32), ("maskt", [128, 256], F32),
        ("identb", [128, 128], BF16), ("identf", [128, 128], F32),
        ("onesrow", [1, 128], F32), ("onescol", [128, 1], F32),
        ("invCrow", [1, 128], F32),
    ]:
        d[nm] = nc.dram_tensor(nm, shp, dt, kind="ExternalInput").ap()
    out_h = nc.dram_tensor("out", [2048, C], F32, kind="ExternalOutput").ap()
    kT_h = nc.dram_tensor("kT_h", [NG1, 128, NCH * G1], BF16, kind="Internal").ap()
    qT_h = nc.dram_tensor("qT_h", [NB2, 128, NCH * G1], BF16, kind="Internal").ap()
    ff_h = nc.dram_tensor("ff_h", [NSLOT, 128, C], F32, kind="Internal").ap()
    scr_h = nc.dram_tensor("scr_h", [128, 1], F32, kind="Internal").ap()
    scrb_h = nc.dram_tensor("scrb_h", [1, 4], F32, kind="Internal").ap()

    chT, chV, chS, chP = Ch("sT"), Ch("sV"), Ch("sS"), Ch("sP")
    chY, chG, chQ = DCh("sY", 8), DCh("sG", 8), DCh("sQ", 8)
    chans = [chT, chV, chS, chP, chY, chG, chQ]
    P = Plan()

    sb_off = {}

    def _rec(n, scope, fn):
        b0 = nc.sbuf_base
        h = fn()
        sb_off[n] = (b0, nc.sbuf_base, scope)
        return h

    def sb(n, s, t):
        return _rec(n, "es", lambda: es.enter_context(nc.sbuf_tensor(n, s, t)))

    v_res = sb("v_res", [128, 32 * C], BF16)
    identb = sb("identb_s", [128, 128], BF16)
    identf = sb("identf_s", [128, 128], F32)
    onesrow = sb("onesrow_s", [1, 128], F32)
    invCrow = sb("invCrow_s", [1, 128], F32)
    onescol = sb("onescol_s", [128, 1], BF16)
    aTs = sb("aT_s", [16, C], BF16)
    bms = sb("bm_s", [16, C], BF16)
    scol = sb("scol", [128, 1], F32)
    iscol = sb("iscol", [128, 1], F32)
    n8scol = sb("n8scol", [128, 1], F32)
    pw = ExitStack()
    weff = pw.enter_context(nc.sbuf_tensor("weff", [128, NCH * C], BF16, side="right"))

    t_idb = P.op("Y", lambda e: e.dma_start(out=identb[:, :], in_=d["identb"][:, :]), inc=chY)
    t_idf = P.op("Y", lambda e: e.dma_start(out=identf[:, :], in_=d["identf"][:, :]), inc=chY)
    t_or = P.op("Y", lambda e: e.dma_start(out=onesrow[:, :], in_=d["onesrow"][:, :]), inc=chY)
    t_ic = P.op("Y", lambda e: e.dma_start(out=invCrow[:, :], in_=d["invCrow"][:, :]), inc=chY)
    t_oc = P.op("G", lambda e: e.dma_start(out=onescol[:, :], in_=d["onescol"][:, :]), inc=chG)
    t_aT = P.op("G", lambda e: e.dma_start(out=aTs[:, :], in_=d["aT"][:, :]), inc=chG)
    t_bm = P.op("G", lambda e: e.dma_start(out=bms[:, :], in_=d["bmat"][:, :]), inc=chG)

    p1 = ExitStack()

    def sb1(n, s, t):
        return _rec(n, "p1", lambda: p1.enter_context(nc.sbuf_tensor(n, s, t)))

    wkb = sb1("wkb", [128, NCH * C], BF16)
    wvb = sb1("wvb", [128, NCH * C], BF16)
    xg = [sb1(f"xg{i}", [128, NCH * G1], BF16) for i in range(3)]
    kst01 = sb1("kst01", [128, 2 * NCH * G1], BF16)
    kstage = [kst01[:, i * NCH * G1:(i + 1) * NCH * G1] for i in range(2)]
    x32g0 = kst01.bitcast(F32)
    sqb = [sb1(f"sq{i}", [128, G1], BF16) for i in range(2)]
    strow = sb1("strow", [1, 4 * G1], F32)
    mbc_s = sb1("mbc_s", [128, G1], F32)
    rbc_s = sb1("rbc_s", [128, G1], F32)
    xntmp = sb1("xntmp", [128, G1], F32)
    xntmp_b = sb1("xntmp_b", [128, G1], F32)
    xnt = [xntmp, xntmp_b]
    w32 = [sb1(f"w32_{i}", [128, C], F32) for i in range(2)]
    lad_u = sb1("lad_u", [128, G1], F32)
    lad_a = sb1("lad_a", [128, G1], F32)
    lad_b = sb1("lad_b", [128, G1], F32)
    ab_sb = sb1("ab_sb", [128, NCH * C], BF16)
    gcol = sb1("gcol", [128, 16], F32)
    grow = sb1("grow", [1, 256], F32)
    _DBG["p1_headroom"] = nc.sbuf_top - nc.sbuf_base

    xg0_Y = []
    for h in range(2):
        xg0_Y.append(P.op("Y", lambda e, h=h: e.dma_start(
            out=x32g0[:, h * 4 * G1:(h + 1) * 4 * G1].rearrange("p (a t) -> p a t", a=4),
            in_=d["xfm"].rearrange("(a p) t -> p a t", p=128)[:, 4 * h:4 * (h + 1), 0:G1]),
            inc=chY))
    cast_tok = {}
    for cc in range(NCH):
        if cc % 2 == 0:
            cast_tok[cc] = P.op("V", lambda e, cc=cc: e.tensor_copy(
                out=xg[0][:, cc * G1:(cc + 1) * G1], in_=x32g0[:, cc * G1:(cc + 1) * G1]),
                waits=[xg0_Y[cc // 4]], inc=chV)
        else:
            cast_tok[cc] = P.op("S", lambda e, cc=cc: e.copy(
                out=xg[0][:, cc * G1:(cc + 1) * G1], in_=x32g0[:, cc * G1:(cc + 1) * G1]),
                waits=[xg0_Y[cc // 4]], inc=chS)

    wk_Y, wk_cast = {}, {}
    wk_Y[0] = P.op("Y", lambda e: e.dma_start(
        out=x32g0[:, :].rearrange("p (a t) -> p a t", a=NCH),
        in_=d["wkT"].rearrange("(a p) x -> p a x", p=128)[:, :, 0:512]),
        waits=[cast_tok[cc] for cc in range(NCH)], inc=chY)

    def e_wk_tail():
        for h in range(2):
            if h == 1:
                wk_Y[1] = P.op("Y", lambda e: e.dma_start(
                    out=x32g0[:, :].rearrange("p (a t) -> p a t", a=NCH),
                    in_=d["wkT"].rearrange("(a p) x -> p a x", p=128)[:, :, 512:1024]),
                    waits=[wk_cast[0]], inc=chY)
            wk_cast[h] = P.op("S", lambda e, h=h: e.copy(
                out=wkb[:, :].rearrange("p (a x) -> p a x", a=NCH)[:, :, h * 512:(h + 1) * 512],
                in_=x32g0[:, :].rearrange("p (a t) -> p a t", a=NCH)),
                waits=[wk_Y[h]], inc=chS)

    wv32 = ab_sb.bitcast(F32)
    wv_cast = {}
    wv_Y = {}
    wv_Y[0] = P.op("Y", lambda e: e.dma_start(
        out=wv32[:, :].rearrange("p (a t) -> p a t", a=NCH),
        in_=d["wvT"].rearrange("(a p) x -> p a x", p=128)[:, :, 0:512]), inc=chY)

    def e_wv_tail():
        for h in range(2):
            if h == 1:
                wv_Y[1] = P.op("Y", lambda e: e.dma_start(
                    out=wv32[:, :].rearrange("p (a t) -> p a t", a=NCH),
                    in_=d["wvT"].rearrange("(a p) x -> p a x", p=128)[:, :, 512:1024]),
                    waits=wv_cast[0], inc=chY)
            toks = []
            for q in range(2):
                dst = wvb[:, :].rearrange("p (a x) -> p a x", a=NCH)[
                    :, 4 * q:4 * (q + 1), h * 512:(h + 1) * 512]
                srcv = wv32[:, :].rearrange("p (a t) -> p a t", a=NCH)[:, 4 * q:4 * (q + 1), :]
                toks.append(P.op("V", lambda e, dst=dst, srcv=srcv: e.tensor_copy(
                    out=dst, in_=srcv), waits=[wv_Y[h]], inc=chV))
            wv_cast[h] = toks
        for i in range(6):
            w = [] if i < 2 else list(wv_cast[1])
            qa_y[i] = P.op("Y", lambda e, i=i: e.dma_start(
                out=scan_buf(i), in_=d["wker"][i * 128:(i + 1) * 128, :]),
                waits=w, inc=chY)
        return wv_cast[0] + wv_cast[1]

    def scan_buf(i):
        if i in (0, 1, 6, 7):
            return w32[i % 2][:, :]
        return wv32[:, (i - 2) * C:(i - 1) * C]

    qa_y = {}
    qmax_done = {}
    y_bc = []
    v_ab_tok = {}

    def ps1(n, s, t=F32):
        return p1.enter_context(nc.psum_tensor(n, s, t))

    T0 = ps1("T0", [128, G1])
    T1 = ps1("T1", [128, G1])
    psK = [ps1(f"psK{i}", [128, G1]) for i in range(2)]
    psV = [[ps1(f"psV{i}_{h}", [128, 512]) for h in range(2)] for i in range(2)]
    s1r, s2r, tr1, tr2 = (strow[0:1, i * G1:(i + 1) * G1] for i in range(4))

    def e_scan():
        for i in range(NCH):
            if i >= 6:
                qa_y[i] = P.op("Y", lambda e, i=i: e.dma_start(
                    out=scan_buf(i), in_=d["wker"][i * 128:(i + 1) * 128, :]),
                    waits=[qmax_done[i - 6]], inc=chY)
            qmax_done[i] = P.op("V", lambda e, i=i: e.reduce_max(
                gcol[:, i:i + 1], scan_buf(i), axis=AX.X, apply_absolute_value=True),
                waits=[qa_y[i]], inc=chV)
        v_am = P.op("V", lambda e: e.reduce_max(gcol[:, 8:9], gcol[:, 0:8], axis=AX.X),
                    waits=[qmax_done[NCH - 1]], inc=chV)
        y_sc1 = P.op("Y", lambda e: e.dma_start(out=scr_h[:, :], in_=gcol[:, 8:9]),
                     waits=[v_am], inc=chY)
        y_sc2 = P.op("Y", lambda e: e.dma_start(
            out=grow[0:1, 0:128], in_=scr_h.rearrange("p one -> one p")),
            waits=[y_sc1], inc=chY)
        v_gm = P.op("V", lambda e: e.reduce_max(grow[0:1, 128:129], grow[0:1, 0:128],
                                                axis=AX.X), waits=[y_sc2], inc=chV)
        v_s = P.op("V", lambda e: e.tensor_scalar(out=grow[0:1, 129:130], in0=grow[0:1, 128:129],
                                                  scalar1=1.0 / 7.0, scalar2=None, op0=ALU.mult),
                   waits=[v_gm], inc=chV)
        v_rc = P.op("V", lambda e: e.reciprocal(out=grow[0:1, 130:131], in_=grow[0:1, 128:129]),
                    waits=[v_gm], inc=chV)
        P.op("V", lambda e: e.tensor_scalar(out=grow[0:1, 131:132], in0=grow[0:1, 130:131],
                                            scalar1=7.0, scalar2=None, op0=ALU.mult),
             waits=[v_rc], inc=chV)
        v_n8 = P.op("V", lambda e: e.tensor_scalar(out=grow[0:1, 132:133], in0=grow[0:1, 129:130],
                                                   scalar1=-8.0, scalar2=None, op0=ALU.mult),
                    waits=[v_s], inc=chV)
        y_sc3 = P.op("Y", lambda e: e.dma_start(out=scrb_h[0:1, :], in_=grow[0:1, 129:133]),
                     waits=[v_n8], inc=chY)
        for cidx, dst in ((0, scol), (2, iscol), (3, n8scol)):
            y_bc.append(P.op("Y", lambda e, cidx=cidx, dst=dst: e.dma_start(
                out=dst[:, :], in_=scrb_h[0:1, cidx:cidx + 1].broadcast_to([128, 1])),
                waits=[y_sc3], inc=chY))

    def e_ab_iter(i):
        w1 = [t_aT, t_bm]
        if i == 0:
            w1.append(vmm_S[1])
        if i - 1 in v_ab_tok:
            w1.append(v_ab_tok[i - 1])
        t1 = P.op("T", lambda e, i=i: e.matmul(
            psV[0][0][:, :], aTs[:, i * 128:(i + 1) * 128], bms[:, 0:512],
            start=True, stop=True), waits=w1, inc=chT)
        t2 = P.op("T", lambda e, i=i: e.matmul(
            psV[0][1][:, :], aTs[:, i * 128:(i + 1) * 128], bms[:, 512:1024],
            start=True, stop=True), inc=chT)
        wv_ = [t2, y_bc[2], qmax_done[min(2 + i // 2, 5)]] + wv_cast[1]
        P.op("V", lambda e, i=i: e.tensor_scalar(
            out=ab_sb[:, i * C:i * C + 512], in0=psV[0][0][:, :],
            scalar1=n8scol[:, 0:1], scalar2=None, op0=ALU.add), waits=wv_, inc=chV)
        v_ab_tok[i] = P.op("V", lambda e, i=i: e.tensor_scalar(
            out=ab_sb[:, i * C + 512:(i + 1) * C], in0=psV[0][1][:, :],
            scalar1=n8scol[:, 0:1], scalar2=None, op0=ALU.add), inc=chV)

    xg_dma, stats_T, math_V, bcast_T, bcastev_V, xn_V = {}, {}, {}, {}, {}, {}
    math_S, xn_cc = {}, {}
    vmm_T, kst_Y, kst_S, vmm_S = {}, {}, {}, {}
    sq_T, sq_S = {}, {}
    quant_V, qw_Y = {}, {}

    def e_xg(g):
        w = [xg_dma_gate.get(g)]
        xg_dma[g] = P.op("G", lambda e, g=g: e.dma_start(
            out=xg[g % 3][:, :].rearrange("p (a t) -> p a t", a=NCH),
            in_=d["xfm"].rearrange("(a p) t -> p a t", p=128)[:, :, g * G1:(g + 1) * G1]),
            waits=w, inc=chG)

    xg_dma_gate = {}

    def e_stats(g, bankfree):
        for cc in range(NCH):
            wsq = [cast_tok[cc]] if g == 0 else [xg_dma[g]]
            key = (g, cc - 2) if cc >= 2 else (g - 1, cc + 6)
            if key in sq_T:
                wsq += [sq_T[key], sq_S[key]]
            sq_S[(g, cc)] = P.op("S", lambda e, g=g, cc=cc: e.activation(
                out=sqb[cc % 2][:, :], in_=xg[g % 3][:, cc * G1:(cc + 1) * G1],
                func=AF.Square), waits=wsq, inc=chS)
            w = ([cast_tok[cc]] if g == 0 else [xg_dma[g]]) + [t_oc]
            if cc == 0:
                w.append(bankfree)
            P.op("T", lambda e, g=g, cc=cc: e.matmul(
                T0[0:1, :], onescol[:, 0:1], xg[g % 3][:, cc * G1:(cc + 1) * G1],
                start=(cc == 0), stop=(cc == NCH - 1), skip_group_check=True),
                waits=w, inc=chT)
            sq_T[(g, cc)] = P.op("T", lambda e, g=g, cc=cc: e.matmul(
                T1[0:1, :], onescol[:, 0:1], sqb[cc % 2][:, :],
                start=(cc == 0), stop=(cc == NCH - 1), skip_group_check=True),
                waits=[sq_S[(g, cc)]], inc=chT)
        stats_T[g] = chT.tok()

    def e_math(g, rows, prev=()):
        s1, s2, t1_, t2_ = rows
        v0 = P.op("V", lambda e, s1=s1: e.tensor_copy(out=s1, in_=T0[0:1, :]),
                  waits=[stats_T[g]] + list(prev), inc=chV)
        v1 = P.op("V", lambda e, s2=s2: e.tensor_copy(out=s2, in_=T1[0:1, :]), inc=chV)
        v2 = P.op("V", lambda e, t1_=t1_, s1=s1: e.tensor_tensor(
            out=t1_, in0=s1, in1=s1, op=ALU.mult), waits=[v0], inc=chV)
        v3 = P.op("V", lambda e, t1_=t1_, s2=s2: e.scalar_tensor_tensor(
            out=t1_, in0=s2, scalar=float(C), in1=t1_, op0=ALU.mult, op1=ALU.subtract),
            waits=[v1, v2], inc=chV)
        v4 = P.op("V", lambda e, t1_=t1_: e.tensor_scalar(
            out=t1_, in0=t1_, scalar1=1.0 / (C * C), scalar2=EPS,
            op0=ALU.mult, op1=ALU.add), waits=[v3], inc=chV)
        s0 = P.op("S", lambda e, t1_=t1_: e.activation(out=t1_, in_=t1_, func=AF.Ln),
                  waits=[v4], inc=chS)
        s1x = P.op("S", lambda e, t2_=t2_, t1_=t1_: e.activation(
            out=t2_, in_=t1_, func=AF.Exp, scale=-0.5), waits=[s0], inc=chS)
        math_S[g] = s1x
        return P.op("V", lambda e: e.engine_nop(), waits=[s1x], inc=chV)

    def e_bcast(g):
        P.op("T", lambda e: e.matmul(T0[:, :], invCrow[0:1, :], s1r,
                                     start=True, stop=True),
             waits=[math_V[g], stats_T[g], t_ic, t_or], inc=chT)
        bcast_T[g] = P.op("T", lambda e: e.matmul(T1[:, :], onesrow[0:1, :], tr2,
                                                  start=True, stop=True), inc=chT)

    def e_xn(g):
        P.op("V", lambda e: e.tensor_copy(out=mbc_s[:, :], in_=T0[:, :]),
             waits=[bcast_T[g]], inc=chV)
        ev = P.op("V", lambda e: e.tensor_copy(out=rbc_s[:, :], in_=T1[:, :]),
                  inc=chV)
        bcastev_V[g] = ev
        mul_t = {}
        for c0 in range(0, NCH, 2):
            subs = []
            for cc in (c0, c0 + 1):
                w = [ev] if cc < 2 else [mul_t[cc - 2]]
                if g == 0:
                    w.append(cast_tok[cc])
                subs.append(P.op("V", lambda e, g=g, cc=cc: e.tensor_tensor(
                    out=xnt[cc % 2][:, :], in0=xg[g % 3][:, cc * G1:(cc + 1) * G1],
                    in1=mbc_s[:, :], op=ALU.subtract), waits=w, inc=chV))
            for k, cc in enumerate((c0, c0 + 1)):
                mul_t[cc] = P.op("V", lambda e, g=g, cc=cc: e.tensor_tensor(
                    out=xg[g % 3][:, cc * G1:(cc + 1) * G1], in0=xnt[cc % 2][:, :],
                    in1=rbc_s[:, :], op=ALU.mult), waits=[subs[k]], inc=chV)
        xn_V[g] = mul_t[NCH - 1]
        xn_cc[g] = dict(mul_t)

    def e_kproj(g, extra=None):
        ev = {}
        for dc in range(NCH):
            for cc in range(NCH):
                w = [xn_cc[g][cc]] if dc == 0 else []
                if cc == 0:
                    if dc == 0:
                        w.append(wk_cast[0])
                        if g >= 2:
                            w.append(kst_Y[g - 2])
                    if dc == 4:
                        w.append(wk_cast[1])
                    if dc >= 2:
                        w.append(ev[dc - 2])
                    elif g >= 1:
                        w.append(kst_S[g - 1])
                P.op("T", lambda e, g=g, dc=dc, cc=cc: e.matmul(
                    psK[dc % 2][:, :],
                    wkb[:, :].rearrange("p (a x) -> p a x", a=NCH)[:, cc, dc * 128:(dc + 1) * 128],
                    xg[g % 3][:, cc * G1:(cc + 1) * G1],
                    start=(cc == 0), stop=(cc == NCH - 1)), waits=w, inc=chT)
            tmm = chT.tok()
            wk_ev = [tmm] + ([kst_S[g - 2]] if dc == 0 and g >= 2 else [])
            if g <= 1 and dc == 0:
                wk_ev.append(wk_cast[1])
            ev[dc] = P.op("S", lambda e, g=g, dc=dc: e.copy(
                out=kstage[g % 2][:, dc * G1:(dc + 1) * G1], in_=psK[dc % 2][:, :]),
                waits=wk_ev, inc=chS)
            if extra is not None:
                extra(dc)
        kst_S[g] = ev[NCH - 1]
        kst_Y[g] = P.op("Y", lambda e, g=g: e.dma_start(
            out=kT_h[g, :, :], in_=kstage[g % 2][:, :]),
            waits=[ev[NCH - 1]], inc=chY)

    def e_vproj(g):
        evs = {}
        for tt in range(4):
            for h in range(2):
                for cc in range(NCH):
                    w = [xn_cc[g][cc]] if (tt == 0 and h == 0) else []
                    if cc == 0 and h == 0:
                        if tt >= 2:
                            w.append(evs[tt - 2])
                        elif g >= 1:
                            w.append(vmm_S[g - 1])
                        if tt == 0:
                            w += wv_done
                            if g == 2:
                                w.append(v_ab_tok[NCH - 1])
                    P.op("T", lambda e, g=g, tt=tt, h=h, cc=cc: e.matmul(
                        psV[tt % 2][h][:, :],
                        xg[g % 3][:, cc * G1 + tt * 128: cc * G1 + (tt + 1) * 128],
                        wvb[:, :].rearrange("p (a x) -> p a x", a=NCH)[:, cc, h * 512:(h + 1) * 512],
                        start=(cc == 0), stop=(cc == NCH - 1)), waits=w, inc=chT)
            tmm = chT.tok()
            base = (4 * g + tt) * C
            P.op("S", lambda e, tt=tt, base=base: e.copy(
                out=v_res[:, base:base + 512], in_=psV[tt % 2][0][:, :]),
                waits=[tmm], inc=chS)
            evs[tt] = P.op("S", lambda e, tt=tt, base=base: e.copy(
                out=v_res[:, base + 512:base + 1024], in_=psV[tt % 2][1][:, :]), inc=chS)
        vmm_T[g] = chT.tok()
        vmm_S[g] = evs[3]

    def e_qdma(i):
        w = [quant_V[i - 2]] if i >= 2 else [qmax_done[i + 6]]
        qw_Y[i] = P.op("Y", lambda e, i=i: e.dma_start(
            out=w32[i % 2][:, :], in_=d["wker"][i * 128:(i + 1) * 128, :]),
            waits=w, inc=chY)

    def e_quant(i):
        vq = chV.tok()
        for hh in range(2):
            w = [qw_Y[i], vq]
            if i == 0 and hh == 0:
                w += [v_ab_tok[NCH - 1], y_bc[0], y_bc[1]]
            vq = P.op("V", lambda e, i=i, hh=hh: e.tensor_scalar(
                out=lad_u[:, :], in0=w32[i % 2][:, hh * 512:(hh + 1) * 512],
                scalar1=iscol[:, 0:1], scalar2=8.5, op0=ALU.mult, op1=ALU.add),
                waits=w, inc=chV)
            vq = P.op("V", lambda e: e.tensor_scalar(
                out=lad_a[:, :], in0=lad_u[:, :], scalar1=8.0, scalar2=8.0,
                op0=ALU.is_ge, op1=ALU.mult), waits=[vq], inc=chV)
            for st in (4.0, 2.0, 1.0):
                vq = P.op("V", lambda e: e.tensor_tensor(out=lad_b[:, :], in0=lad_u[:, :],
                                                         in1=lad_a[:, :], op=ALU.subtract),
                          waits=[vq], inc=chV)
                vq = P.op("V", lambda e, st=st: e.tensor_scalar(
                    out=lad_b[:, :], in0=lad_b[:, :], scalar1=st, scalar2=st,
                    op0=ALU.is_ge, op1=ALU.mult), waits=[vq], inc=chV)
                vq = P.op("V", lambda e: e.tensor_tensor(out=lad_a[:, :], in0=lad_a[:, :],
                                                         in1=lad_b[:, :], op=ALU.add),
                          waits=[vq], inc=chV)
            vq = P.op("V", lambda e, i=i, hh=hh: e.scalar_tensor_tensor(
                out=weff[:, i * C + hh * 512: i * C + (hh + 1) * 512], in0=lad_a[:, :],
                scalar=scol[:, 0:1], in1=ab_sb[:, i * C + hh * 512: i * C + (hh + 1) * 512],
                op0=ALU.mult, op1=ALU.add), waits=[vq], inc=chV)
        quant_V[i] = vq

    e_stats(0, None)
    math_V[0] = e_math(0, (s1r, s2r, tr1, tr2))
    e_wk_tail()
    e_bcast(0); e_xn(0)
    e_xg(1)
    WQ32R = [None, None]
    wq_h0_Y = [None]
    wq_h1_Y = [None]
    kp7tok = [None]

    def wq_hook(dc):
        if dc == NCH - 1:
            kp7tok[0] = chT.tok()
            wq_h0_Y[0] = P.op("Y", lambda e: e.dma_start(
                out=WQ32R[0][:, :].rearrange("p (a t) -> p a t", a=NCH),
                in_=d["wqT"].rearrange("(a p) x -> p a x", p=128)[:, :, 0:512]),
                waits=[kp7tok[0]], inc=chY)
            wq_h1_Y[0] = P.op("Y", lambda e: e.dma_start(
                out=WQ32R[1][:, :].rearrange("p (a t) -> p a t", a=NCH),
                in_=d["wqT"].rearrange("(a p) x -> p a x", p=128)[:, :, 512:1024]),
                waits=[quant_V[NCH - 1]], inc=chY)

    quant_sched = {2: [0], 3: [1], 4: [2], 5: [3], 6: [4, 5], 7: [6, 7]}
    qdma_sched = {3: [2], 4: [3], 5: [4], 6: [5], 7: [6, 7]}
    e_stats(1, bcastev_V[0])
    math_V[1] = e_math(1, (s1r, s2r, tr1, tr2), prev=[math_V[0], math_S[0]])
    e_bcast(1); e_xn(1)
    wv_done = e_wv_tail()
    e_kproj(0)
    e_scan()
    e_qdma(0); e_qdma(1)
    for g in range(1, NG1 + 1):
        if g + 1 <= NG1 - 1:
            xg_dma_gate[g + 1] = vmm_T[g - 2] if g >= 2 else None
            e_xg(g + 1)
        for qi in qdma_sched.get(g, []):
            e_qdma(qi)
        e_vproj(g - 1)
        if g == 7:
            for qi in quant_sched.get(g, []):
                e_quant(qi)
        if g <= NG1 - 1:
            ex = e_ab_iter if g == 2 else (wq_hook if g == 7 else None)
            e_kproj(g, extra=ex)
            if g + 1 <= NG1 - 1:
                e_stats(g + 1, bcastev_V[g])
                math_V[g + 1] = e_math(g + 1, (s1r, s2r, tr1, tr2),
                                       prev=[math_V[g], math_S[g]])
                e_bcast(g + 1)
                e_xn(g + 1)
        if g < 7:
            for qi in quant_sched.get(g, []):
                e_quant(qi)
    p1_last = {
        "wkb": kp7tok[0], "wvb": vmm_T[NG1 - 1],
        "xg0": vmm_T[6] + xn_V[6], "xg1": vmm_T[7] + xn_V[7],
        "xg2": vmm_T[5] + xn_V[5],
        "kst_even": kst_Y[6] + kst_S[6], "kst_odd": kst_Y[7] + kst_S[7],
        "sqb": stats_T[7] + sq_S[(7, 7)],
        "strow": bcast_T[7] + math_V[7],
        "xnrow": xn_V[7],
        "w32": quant_V[7], "lad": quant_V[7], "ab": quant_V[7],
        "psT01": bcastev_V[7], "psK": kst_S[7], "psV": vmm_S[7],
    }
    bar1 = (chT.tok() + chV.tok() + chS.tok() + chY.snap() + chG.snap() + chQ.snap())
    p1.close()

    p2a = ExitStack()

    def sb2(n, s, t):
        return _rec(n, "p2a", lambda: p2a.enter_context(nc.sbuf_tensor(n, s, t)))

    def ps2(n, s, t=F32):
        return p2a.enter_context(nc.psum_tensor(n, s, t))

    wq32 = sb2("wq32", [128, 4 * C], F32)
    wqb = sb2("wqb", [128, NCH * C], BF16)
    xga = [sb2(f"xga{i}", [128, NCH * G1], BF16) for i in range(3)]
    strow2 = sb2("strow2", [1, 4 * G1], F32)
    mbc2 = sb2("mbc2", [128, G1], F32)
    rbc2 = sb2("rbc2", [128, G1], F32)
    xntmp2 = sb2("xntmp2", [128, G1], F32)
    xntmp2_b = sb2("xntmp2_b", [128, G1], F32)
    sqa = [sb2(f"sqa{i}", [128, G1], BF16) for i in range(2)]
    xnt2 = [xntmp2, xntmp2_b]
    qstage = [sb2(f"qst{i}", [128, NCH * G1], BF16) for i in range(2)]
    ffstage = [sb2(f"ffst{i}", [128, C], F32) for i in range(2)]
    wq32b = sb2("wq32b", [128, 4 * C], F32)
    biasb = sb2("biasb_s", [128, C], F32)
    U0 = ps2("U0", [128, G1])
    U1 = ps2("U1", [128, G1])
    psQ = [ps2(f"psQ{i}", [128, G1]) for i in range(2)]
    psF = [[ps2(f"psF{i}_{h}", [128, 512]) for h in range(2)] for i in range(2)]
    s1a, s2a, ta1, ta2 = (strow2[0:1, i * G1:(i + 1) * G1] for i in range(4))
    WQ32R[0] = wq32
    WQ32R[1] = wq32b
    KRESR = [None]
    QTILER = [None, None]
    MASKTR = [None]
    kres_early = []
    qt_early = {}
    t_mk_early = [None]

    def _rng(n):
        return sb_off[n][0:2]
    assert _rng("wq32") == _rng("wkb"), (sb_off["wq32"], sb_off["wkb"])
    assert _rng("wqb") == _rng("wvb")
    for i in range(3):
        assert _rng(f"xga{i}") == _rng(f"xg{i}")
    assert _rng("strow2") == (_rng("kst01")[0], _rng("kst01")[0] + 8192)
    assert _rng("xntmp2_b")[1] == _rng("kst01")[1]
    assert _rng("sqa0") == _rng("sq0") and _rng("sqa1") == _rng("sq1")
    assert _rng("qst0") == _rng("strow")
    assert (_rng("qst1")[0], _rng("qst1")[1]) == (_rng("mbc_s")[0], _rng("xntmp_b")[1])
    assert _rng("ffst0") == _rng("w32_0") and _rng("ffst1") == _rng("w32_1")
    assert _rng("wq32b")[0] >= _rng("lad_u")[0] and _rng("wq32b")[1] <= _rng("ab_sb")[1]
    assert _rng("biasb_s")[1] <= _rng("ab_sb")[1]

    wq_cast = {}

    def e_wq_casts():
        for h in range(2):
            stg = wq32 if h == 0 else wq32b
            for q in range(2):
                dst = wqb[:, :].rearrange("p (a x) -> p a x", a=NCH)[
                    :, 4 * q:4 * (q + 1), h * 512:(h + 1) * 512]
                srcv = stg[:, :].rearrange("p (a t) -> p a t", a=NCH)[:, 4 * q:4 * (q + 1), :]
                tk = P.op("S", lambda e, dst=dst, srcv=srcv: e.copy(out=dst, in_=srcv),
                          waits=[wq_h0_Y[0] if h == 0 else wq_h1_Y[0],
                                 p1_last["wvb"] if (h == 0 and q == 0) else None], inc=chS)
            wq_cast[h] = tk

    t_bb = P.op("Y", lambda e: e.dma_start(out=biasb[:, :], in_=d["biasb"][:, :]),
                waits=[p1_last["lad"]], inc=chY)

    xga_dma, stats2_T, math2_V, bcast2_T, bcastev2_V, xn2_V = {}, {}, {}, {}, {}, {}
    fmm_T, qst_Y, qst_S, ff_Y, ff_V = {}, {}, {}, {}, {}
    math2_S, ffev, xn2_cc = {}, {}, {}
    sq2_T, sq2_S = {}, {}
    xga_gate = {}

    SL = (0, 2, 1, 0)
    SLGATE = {0: p1_last["xg0"], 2: p1_last["xg2"], 1: p1_last["xg1"]}

    def e_xga(b):
        w = [SLGATE[SL[b]] if b <= 2 else None, xga_gate.get(b)]
        xga_dma[b] = P.op("G", lambda e, b=b: e.dma_start(
            out=xga[SL[b]][:, :].rearrange("p (a t) -> p a t", a=NCH),
            in_=d["xcols"].rearrange("(a p) t -> p a t", p=128)[:, :, b * G1:(b + 1) * G1]),
            waits=w, inc=chG)

    def e_stats2(b, bankfree):
        for cc in range(NCH):
            wsq = [xga_dma[b]]
            if b == 0 and cc <= 1:
                wsq.append(p1_last["sqb"])
            key = (b, cc - 2) if cc >= 2 else (b - 1, cc + 6)
            if key in sq2_T:
                wsq += [sq2_T[key], sq2_S[key]]
            sq2_S[(b, cc)] = P.op("S", lambda e, b=b, cc=cc: e.activation(
                out=sqa[cc % 2][:, :], in_=xga[SL[b]][:, cc * G1:(cc + 1) * G1],
                func=AF.Square), waits=wsq, inc=chS)
            w = [xga_dma[b]]
            if cc == 0:
                w.append(bankfree)
            P.op("T", lambda e, b=b, cc=cc: e.matmul(
                U0[0:1, :], onescol[:, 0:1], xga[SL[b]][:, cc * G1:(cc + 1) * G1],
                start=(cc == 0), stop=(cc == NCH - 1), skip_group_check=True),
                waits=w, inc=chT)
            sq2_T[(b, cc)] = P.op("T", lambda e, b=b, cc=cc: e.matmul(
                U1[0:1, :], onescol[:, 0:1], sqa[cc % 2][:, :],
                start=(cc == 0), stop=(cc == NCH - 1), skip_group_check=True),
                waits=[sq2_S[(b, cc)]] + ([bankfree] if b == 0 and cc == 0 else []),
                inc=chT)
        stats2_T[b] = chT.tok()

    def e_math2(b, prev=()):
        reg = [p1_last["kst_even"]] if b == 0 else []
        v0 = P.op("V", lambda e: e.tensor_copy(out=s1a, in_=U0[0:1, :]),
                  waits=[stats2_T[b]] + list(prev) + reg, inc=chV)
        v1 = P.op("V", lambda e: e.tensor_copy(out=s2a, in_=U1[0:1, :]),
                  waits=reg, inc=chV)
        v2 = P.op("V", lambda e: e.tensor_tensor(
            out=ta1, in0=s1a, in1=s1a, op=ALU.mult), waits=[v0], inc=chV)
        v3 = P.op("V", lambda e: e.scalar_tensor_tensor(
            out=ta1, in0=s2a, scalar=float(C), in1=ta1, op0=ALU.mult, op1=ALU.subtract),
            waits=[v1, v2], inc=chV)
        v4 = P.op("V", lambda e: e.tensor_scalar(
            out=ta1, in0=ta1, scalar1=1.0 / (C * C), scalar2=EPS,
            op0=ALU.mult, op1=ALU.add), waits=[v3], inc=chV)
        s0 = P.op("S", lambda e: e.activation(out=ta1, in_=ta1, func=AF.Ln),
                  waits=[v4], inc=chS)
        s1x = P.op("S", lambda e: e.activation(
            out=ta2, in_=ta1, func=AF.Exp, scale=-0.5), waits=[s0], inc=chS)
        math2_S[b] = s1x
        math2_V[b] = P.op("V", lambda e: e.engine_nop(), waits=[s1x], inc=chV)

    def e_bcast2(b):
        P.op("T", lambda e: e.matmul(U0[:, :], invCrow[0:1, :], s1a,
                                     start=True, stop=True),
             waits=[math2_V[b], stats2_T[b], t_ic], inc=chT)
        bcast2_T[b] = P.op("T", lambda e: e.matmul(U1[:, :], onesrow[0:1, :], ta2,
                                                   start=True, stop=True), inc=chT)

    def e_xn2(b):
        reg = [p1_last["kst_odd"]] if b == 0 else []
        P.op("V", lambda e: e.tensor_copy(out=mbc2[:, :], in_=U0[:, :]),
             waits=[bcast2_T[b]] + reg, inc=chV)
        ev = P.op("V", lambda e: e.tensor_copy(out=rbc2[:, :], in_=U1[:, :]),
                  waits=reg, inc=chV)
        bcastev2_V[b] = ev
        mul_t = {}
        for c0 in range(0, NCH, 2):
            subs = []
            for cc in (c0, c0 + 1):
                w = ([ev] if cc < 2 else [mul_t[cc - 2]]) + (reg if cc <= 1 else [])
                subs.append(P.op("V", lambda e, b=b, cc=cc: e.tensor_tensor(
                    out=xnt2[cc % 2][:, :], in0=xga[SL[b]][:, cc * G1:(cc + 1) * G1],
                    in1=mbc2[:, :], op=ALU.subtract), waits=w, inc=chV))
            for k, cc in enumerate((c0, c0 + 1)):
                mul_t[cc] = P.op("V", lambda e, b=b, cc=cc: e.tensor_tensor(
                    out=xga[SL[b]][:, cc * G1:(cc + 1) * G1], in0=xnt2[cc % 2][:, :],
                    in1=rbc2[:, :], op=ALU.mult), waits=[subs[k]], inc=chV)
        xn2_V[b] = mul_t[NCH - 1]
        xn2_cc[b] = dict(mul_t)

    def e_qproj(b):
        ev = {}
        for dc in range(NCH):
            for cc in range(NCH):
                w = [xn2_cc[b][cc]] if dc == 0 else []
                if cc == 0:
                    if dc == 0:
                        w.append(wq_cast[0])
                        if b == 0:
                            w.append(p1_last["psK"])
                        if b >= 2:
                            w.append(qst_Y[b - 2])
                    if dc == 4:
                        w.append(wq_cast[1])
                    if dc >= 2:
                        w.append(ev[dc - 2])
                    elif b >= 1:
                        w.append(qst_S[b - 1])
                P.op("T", lambda e, b=b, dc=dc, cc=cc: e.matmul(
                    psQ[dc % 2][:, :],
                    wqb[:, :].rearrange("p (a x) -> p a x", a=NCH)[:, cc, dc * 128:(dc + 1) * 128],
                    xga[SL[b]][:, cc * G1:(cc + 1) * G1],
                    start=(cc == 0), stop=(cc == NCH - 1)), waits=w, inc=chT)
            tmm = chT.tok()
            wse = [tmm] + ([qst_S[b - 2]] if dc == 0 and b >= 2 else [])
            if b <= 1 and dc == 0:
                wse.append(p1_last["strow"] if b == 0 else p1_last["xnrow"])
            ev[dc] = P.op("S", lambda e, b=b, dc=dc: e.activation(
                out=qstage[b % 2][:, :].rearrange(
                    "p (o a t) -> p o a t", o=4, a=NCH)[:, :, dc, :],
                in_=psQ[dc % 2][:, :].rearrange("p (o t) -> p o t", o=4),
                func=AF.Copy, scale=SCALE),
                waits=wse, inc=chS)
        qst_S[b] = ev[NCH - 1]
        qst_Y[b] = P.op("Y", lambda e, b=b: e.dma_start(
            out=qT_h[b, :, :], in_=qstage[b % 2][:, :]),
            waits=[ev[NCH - 1]], inc=chY)

    def e_ffn(b):
        evs = {}
        for tt in range(4):
            j = 4 * b + tt
            for h in range(2):
                for cc in range(NCH):
                    w = [xn2_cc[b][cc]] if (tt == 0 and h == 0) else []
                    if cc == 0 and h == 0:
                        if b == 0 and tt == 0:
                            w += [quant_V[NCH - 1], p1_last["psV"]]
                        if tt >= 2:
                            w.append(evs[tt - 2])
                        elif b >= 1:
                            w.append(ff_V[b - 1])
                    P.op("T", lambda e, b=b, tt=tt, h=h, cc=cc: e.matmul(
                        psF[tt % 2][h][:, :],
                        xga[SL[b]][:, cc * G1 + tt * 128: cc * G1 + (tt + 1) * 128],
                        weff[:, :].rearrange("p (a x) -> p a x", a=NCH)[:, cc, h * 512:(h + 1) * 512],
                        start=(cc == 0), stop=(cc == NCH - 1)), waits=w, inc=chT)
            tmm = chT.tok()
            wv_ = [tmm, t_bb]
            if j == 0:
                wv_.append(p1_last["w32"])
            if j >= 2:
                wv_.append(ff_Y[j - 2])
            if j >= 2:
                wv_.append(ffev[j - 2])
            P.op("V", lambda e, tt=tt: e.tensor_tensor(
                out=ffstage[tt % 2][:, 0:512], in0=psF[tt % 2][0][:, :],
                in1=biasb[:, 0:512], op=ALU.add), waits=wv_, inc=chV)
            evs[tt] = P.op("V", lambda e, tt=tt: e.tensor_tensor(
                out=ffstage[tt % 2][:, 512:1024], in0=psF[tt % 2][1][:, :],
                in1=biasb[:, 512:1024], op=ALU.add), inc=chV)
            ffev[j] = evs[tt]
            ff_Y[j] = P.op("G", lambda e, j=j, tt=tt: e.dma_start(
                out=ff_h[j, :, :], in_=ffstage[tt % 2][:, :]),
                waits=[evs[tt]], inc=chG)
        fmm_T[b] = chT.tok()
        ff_V[b] = evs[3]

    e_xga(1)
    e_xga(0)
    e_stats2(0, p1_last["psT01"]); e_math2(0); e_bcast2(0); e_xn2(0)
    e_wq_casts()
    kres_early.append(P.op("Y", lambda e: e.dma_start(
        out=KRESR[0][:, 0:NCH * 512], in_=kT_h[0, :, :]),
        waits=[wq_cast[0], kst_Y[0]], inc=chY))
    kres_early.append(P.op("Y", lambda e: e.dma_start(
        out=KRESR[0][:, NCH * 512:], in_=kT_h[1, :, :]),
        waits=[kst_Y[1]], inc=chY))
    e_stats2(1, bcastev2_V[0]); e_math2(1, prev=[math2_V[0], math2_S[0]])
    for b in range(NB2):
        if b >= 1:
            e_ffn(b - 1)
        e_qproj(b)
        if b == NB2 - 1:
            qp3_T = chT.tok()
            ORD_ = sorted(range(NSLOT), key=lambda j: -slot_kcap(j))
            for si_ in range(2):
                j_ = ORD_[si_]
                qt_early[si_] = P.op("Y", lambda e, si_=si_, j_=j_: e.dma_start(
                    out=QTILER[si_][:, :],
                    in_=qT_h[j_ // 4, :, :][:, (j_ % 4) * 1024:(j_ % 4 + 1) * 1024]),
                    waits=[qp3_T, qst_Y[j_ // 4]], inc=chY)
            t_mk_early[0] = P.op("Y", lambda e: e.dma_start(
                out=MASKTR[0][:, :], in_=d["maskt"][:, :]), waits=[qp3_T], inc=chY)
        if b + 1 <= NB2 - 1:
            e_bcast2(b + 1)
            e_xn2(b + 1)
        if b + 2 <= NB2 - 1:
            xga_gate[b + 2] = fmm_T[b - 1] if b >= 1 else None
            e_xga(b + 2)
            e_stats2(b + 2, bcastev2_V[b + 1])
            e_math2(b + 2, prev=[math2_V[b + 1], math2_S[b + 1]])
    e_ffn(NB2 - 1)
    bar2 = (chT.tok() + chV.tok() + chS.tok() + chY.snap() + chG.snap() + chQ.snap())
    p2a.close()

    pw.close()
    p2b = ExitStack()

    def sb3(n, s, t):
        return _rec(n, "p2b", lambda: p2b.enter_context(nc.sbuf_tensor(n, s, t)))

    def ps3(n, s, t=F32):
        return p2b.enter_context(nc.psum_tensor(n, s, t))

    k_res = sb3("k_res", [128, 2 * NCH * 512], BF16)
    qtile = [sb3(f"qtile{i}", [128, NCH * 128], BF16) for i in range(4)]
    maskt = sb3("maskt_s", [128, 256], F32)
    scores = [[sb3(f"scores{p}_{i}", [128, 4096], BF16) for i in range(2)]
              for p in range(2)]
    pbuf = [sb3(f"pbuf{i}", [128, 4096], BF16) for i in range(2)]
    kbuf = [sb3(f"kbuf{i}", [128, NCH * 512], BF16) for i in range(KRING)]
    fftile = [sb3(f"fft{i}", [128, C], F32) for i in range(3)]
    xrtile = [sb3(f"xrt{i}", [128, C], F32) for i in range(3)]
    outsb = [sb3(f"outsb{i}", [128, C], F32) for i in range(2)]
    ptb = [sb3(f"ptb{i}", [128, 128], BF16) for i in range(3)]
    mxv = sb3("mxv", [128, 1], F32)
    nbias = sb3("nbias", [128, 1], F32)
    lsum = sb3("lsum", [128, 4], F32)
    invl = [sb3(f"invl{i}", [128, 1], F32) for i in range(2)]

    psS = [ps3(f"psS{a}", [128, 512]) for a in range(2)]
    psT_ = [ps3(f"psT{i}", [128, 128], BF16) for i in range(2)]
    psAV = [[ps3(f"psAV{i}_{h}", [128, 512]) for h in range(2)] for i in range(2)]

    ORDER = sorted(range(NSLOT), key=lambda j: -slot_kcap(j))
    NP2 = NSLOT // 2

    def pr_slots(k):
        return ORDER[2 * k], ORDER[2 * k + 1]

    KQ = []
    for k in range(NP2):
        jA, _ = pr_slots(k)
        for ci, (off, wdt) in enumerate(score_chunks(slot_kcap(jA))):
            if off >= 1024:
                KQ.append((k, ci, off, wdt))
    kq_pos = {(k, ci): q for q, (k, ci, _, _) in enumerate(KQ)}
    kdma, kcons = {}, {}
    kq_emit = [0]
    mxcols = [sb3(f"mxcols{i}", [128, 8], F32) for i in range(2)]

    KRESR[0] = k_res
    QTILER[0], QTILER[1] = qtile[0], qtile[1]
    MASKTR[0] = maskt
    assert _rng("k_res") == _rng("wq32")
    assert _rng("qtile0")[0] == _rng("wqb")[0] and _rng("maskt_s")[1] <= _rng("wqb")[1]

    kres_dma = kres_early
    t_mk = t_mk_early[0]

    kbuf_gate = {0: stats2_T[NB2 - 1] + qst_Y[2],
                 1: qst_Y[2] + qst_Y[3],
                 2: qst_Y[3] + ff_Y[14] + ff_Y[15]}

    def pump_k(upto):
        while kq_emit[0] < min(upto, len(KQ)):
            q = kq_emit[0]
            k, ci, off, wdt = KQ[q]
            g = off // 512
            buf = q % KRING
            w = [kbuf_gate[q]] if q < KRING else [kcons[q - KRING]]
            qname, qch = "Y", chY
            if wdt == 512:
                v = P.op(qname, lambda e, g=g, buf=buf: e.dma_start(
                    out=kbuf[buf][:, :], in_=kT_h[g, :, :]), waits=w, inc=qch)
            else:
                v = P.op(qname, lambda e, g=g, buf=buf: e.dma_start(
                    out=kbuf[buf][:, :].rearrange("p (a t) -> p a t", a=NCH)[:, :, 0:256],
                    in_=kT_h[g, :, :].rearrange("p (a t) -> p a t", a=NCH)[:, :, 0:256]),
                    waits=w, inc=qch)
            kdma[q] = (v, buf)
            kq_emit[0] += 1

    qt_dma, ff_dma, xr_dma = {}, {}, {}
    qt_cons, ff_cons, xr_cons = {}, {}, {}
    sc_mx_V, exp_S, inv_V, epi_V, mx_fin, exp_half = {}, {}, {}, {}, {}, {}
    trans_cons, av_T, out_G, sc_fin_T = {}, {}, {}, {}
    sc_prev = {}
    ptbg = {}

    def e_sdma_qt(k):
        for a in range(2):
            j = pr_slots(k)[a]
            si = 2 * k + a
            if si in qt_early:
                qt_dma[si] = qt_early[si]
                continue
            w0 = [qp3_T, qst_Y[j // 4]] if k < 2 else []
            qt_dma[si] = P.op("Y", lambda e, si=si, j=j: e.dma_start(
                out=qtile[si % 4][:, :],
                in_=qT_h[j // 4, :, :][:, (j % 4) * 1024:(j % 4 + 1) * 1024]),
                waits=w0 + ([qt_cons[si - 4]] if si >= 4 else []), inc=chY)

    def e_sdma_fx(k):
        w0 = [bar2] if k < 2 else []
        for a in range(2):
            j = pr_slots(k)[a]
            si = 2 * k + a
            ff_dma[si] = P.op("G", lambda e, si=si, j=j: e.dma_start(
                out=fftile[si % 3][:, :], in_=ff_h[j, :, :]),
                waits=w0 + ([ff_cons[si - 3]] if si >= 3 else []), inc=chG)
            xr_dma[si] = P.op("G", lambda e, si=si, j=j: e.dma_start(
                out=xrtile[si % 3][:, :], in_=d["xrows"][j * 128:(j + 1) * 128, :]),
                waits=w0 + ([xr_cons[si - 3]] if si >= 3 else []), inc=chG)

    def e_scores(k):
        jA, jB = pr_slots(k)
        kcapA, kcapB = slot_kcap(jA), slot_kcap(jB)
        chunks = score_chunks(kcapA)
        nchA = len(chunks)
        nchB = len(score_chunks(kcapB))
        kp = k % 2
        mx_prevA = mx_fin.get(2 * (k - 1))
        mx_prevB = mx_fin.get(2 * (k - 1) + 1)
        for ci, (off, wdt) in enumerate(chunks):
            if off < 1024:
                q = None
                v = kres_dma[0] if off < 512 else (kres_dma[0] + kres_dma[1])
                buf = -1
            else:
                q = kq_pos[(k, ci)]
                pump_k(q + 1)
                v, buf = kdma[q]
            for a, (jx, kcx, mx_prev) in enumerate(
                    [(jA, kcapA, mx_prevA), (jB, kcapB, mx_prevB)]):
                wdx = min(wdt, kcx - off)
                if wdx <= 0:
                    continue
                si = 2 * k + a
                for dc in range(NCH):
                    w = []
                    if dc == 0:
                        w.append(v)
                        prev = sc_prev.get(a)
                        if prev is not None:
                            w.append(prev)
                        if ci == 0:
                            w.append(qt_dma[si])
                            if k == 0:
                                w.append(bcastev2_V[NB2 - 1])
                    if buf < 0:
                        kap = k_res[:, :].rearrange("p (g a t) -> p g a t", g=2, a=NCH)[
                            :, off // 512, dc, 0:wdx]
                    else:
                        kap = kbuf[buf][:, :].rearrange("p (a t) -> p a t", a=NCH)[
                            :, dc, 0:wdx]
                    P.op("T", lambda e, si=si, a=a, dc=dc, wdx=wdx, kap=kap: e.matmul(
                        psS[a][:, 0:wdx],
                        qtile[si % 4][:, dc * 128:(dc + 1) * 128],
                        kap,
                        start=(dc == 0), stop=(dc == NCH - 1)), waits=w, inc=chT)
                if a == 1 and q is not None:
                    kcons[q] = chT.tok()
                wev = [chT.tok()]
                if ci == 0 and k >= 1:
                    wev.append(sc_mx_V.get(2 * (k - 1) + a))
                if ci == 0 and k >= 2:
                    wev.append(exp_S.get(2 * (k - 2) + a))
                nchx = nchA if a == 0 else nchB
                if ci < nchx - 1:
                    ev = P.op("V", lambda e, kp=kp, a=a, off=off, wdx=wdx: e.tensor_copy(
                        out=scores[kp][a][:, off:off + wdx], in_=psS[a][:, 0:wdx]),
                        waits=wev, inc=chV)
                    sc_prev[a] = ev
                    P.op("V", lambda e, kp=kp, a=a, ci=ci, off=off, wdx=wdx: e.reduce_max(
                        mxcols[a][:, ci:ci + 1], scores[kp][a][:, off:off + wdx], axis=AX.X),
                        waits=[ev, mx_prev], inc=chV)
                else:
                    evl = []
                    if wdx > 256:
                        evl.append(P.op("V", lambda e, kp=kp, a=a, off=off, wdx=wdx:
                                        e.tensor_copy(
                            out=scores[kp][a][:, off:off + wdx - 256],
                            in_=psS[a][:, 0:wdx - 256]), waits=wev, inc=chV))
                    evl.append(P.op("V", lambda e, kp=kp, a=a, kcx=kcx, wdx=wdx:
                                    e.tensor_tensor(
                        out=scores[kp][a][:, kcx - 256:kcx],
                        in0=psS[a][:, wdx - 256:wdx],
                        in1=maskt[:, :], op=ALU.add),
                        waits=wev + [t_mk, mx_prev], inc=chV))
                    ev = evl[-1]
                    sc_prev[a] = ev
                    P.op("V", lambda e, kp=kp, a=a, ci=ci, off=off, wdx=wdx: e.reduce_max(
                        mxcols[a][:, ci:ci + 1], scores[kp][a][:, off:off + wdx], axis=AX.X),
                        waits=evl + [mx_prev], inc=chV)
                if a == 0:
                    sc_mx_V[2 * k] = chV.tok()
                else:
                    sc_mx_V[2 * k + 1] = chV.tok()
            if kcapB - off <= 0 and q is not None:
                kcons[q] = chT.tok()
            if q is not None:
                pump_k(q + KRING + 1)
        sc_fin_T[k] = chT.tok()
        qt_cons[2 * k] = sc_fin_T[k]
        qt_cons[2 * k + 1] = sc_fin_T[k]

    def e_sm_pre(si):
        a = si % 2
        kcap = slot_kcap(ORDER[si])
        nch = len(score_chunks(kcap))
        v1 = P.op("V", lambda e, a=a, nch=nch: e.reduce_max(
            nbias[:, :], mxcols[a][:, 0:nch], axis=AX.X, negate=True),
            waits=[sc_mx_V[si], exp_S.get(si - 1)], inc=chV)
        mx_fin[si] = v1
        w = [v1, exp_S.get(si - 1)]
        if si >= 2:
            w.append(trans_cons.get(si - 2))
        half = (kcap // 256) * 128
        kp = (si // 2) % 2
        exp_half[si] = P.op("S", lambda e, kp=kp, a=a, half=half: e.activation(
            out=pbuf[a][:, 0:half], in_=scores[kp][a][:, 0:half], func=AF.Exp,
            bias=nbias[:, 0:1], scale=1.0, accum_out=lsum[:, 2 * a:2 * a + 1]), waits=w, inc=chS)
        exp_S[si] = P.op("S", lambda e, kp=kp, a=a, half=half, kcap=kcap: e.activation(
            out=pbuf[a][:, half:kcap], in_=scores[kp][a][:, half:kcap], func=AF.Exp,
            bias=nbias[:, 0:1], scale=1.0, accum_out=lsum[:, 2 * a + 1:2 * a + 2]), inc=chS)

    def e_sm_post(si):
        a = si % 2
        vsum = P.op("V", lambda e, a=a: e.tensor_tensor(
            out=lsum[:, 2 * a:2 * a + 1], in0=lsum[:, 2 * a:2 * a + 1],
            in1=lsum[:, 2 * a + 1:2 * a + 2], op=ALU.add),
            waits=[exp_S[si], inv_V.get(si - 1)], inc=chV)
        inv_V[si] = P.op("V", lambda e, si=si, a=a: e.reciprocal(
            out=invl[si % 2][:, :], in_=lsum[:, 2 * a:2 * a + 1]),
            waits=[vsum, epi_V.get(si - 2)], inc=chV)

    def e_trans_av(si):
        a = si % 2
        kcap = slot_kcap(ORDER[si])
        nk = kcap // 128
        ptb_ev = {}
        av_lastk = {}

        half_nk = (kcap // 256)
        def em_tr(kc):
            w = []
            if kc == 0:
                w = [exp_half[si], t_idb]
                if si == 0:
                    w.append(qst_S[NB2 - 1])
            elif kc == half_nk:
                w = [exp_S[si]]
            tt = P.op("T", lambda e, a=a, kc=kc: e.transpose(
                psT_[kc % 2][:, :], pbuf[a][:, kc * 128:(kc + 1) * 128], identb[:, :]),
                waits=w, inc=chT)
            wv = [tt, ptbg.get(kc % 3)]
            if kc >= 3:
                wv.append(av_lastk[kc - 3])
            ptb_ev[kc] = P.op("S", lambda e, kc=kc: e.copy(
                out=ptb[kc % 3][:, :], in_=psT_[kc % 2][:, :]), waits=wv, inc=chS)
            ptbg[kc % 3] = ptb_ev[kc]

        def em_av(kc):
            for h in range(2):
                w2 = [ptb_ev[kc]]
                if kc == 0 and h == 0 and si == 0:
                    w2.append(ffev[4 * NB2 - 1])
                if kc == 0 and h == 0 and si >= 2:
                    w2.append(epi_V.get(si - 2))
                av_lastk[kc] = P.op("T", lambda e, si=si, kc=kc, h=h, nk=nk: e.matmul(
                    psAV[si % 2][h][:, :], ptb[kc % 3][:, :],
                    v_res[:, kc * C + h * 512: kc * C + (h + 1) * 512],
                    start=(kc == 0), stop=(kc == nk - 1), skip_group_check=True),
                    waits=w2, inc=chT)

        em_tr(0)
        for kc in range(1, nk):
            em_tr(kc)
            em_av(kc - 1)
        em_av(nk - 1)
        trans_cons[si] = chT.tok()
        av_T[si] = chT.tok()

    def e_epilogue(si):
        j = ORDER[si]
        w = [av_T[si], inv_V[si], ff_dma[si], xr_dma[si]]
        if si >= 2:
            w.append(out_G[si - 2])
        P.op("V", lambda e, si=si: e.scalar_tensor_tensor(
            out=outsb[si % 2][:, 0:512], in0=psAV[si % 2][0][:, :], scalar=invl[si % 2][:, 0:1],
            in1=fftile[si % 3][:, 0:512], op0=ALU.mult, op1=ALU.add), waits=w, inc=chV)
        v2 = P.op("V", lambda e, si=si: e.scalar_tensor_tensor(
            out=outsb[si % 2][:, 512:1024], in0=psAV[si % 2][1][:, :], scalar=invl[si % 2][:, 0:1],
            in1=fftile[si % 3][:, 512:1024], op0=ALU.mult, op1=ALU.add), inc=chV)
        ff_cons[si] = v2
        v3 = P.op("V", lambda e, si=si: e.tensor_tensor(
            out=outsb[si % 2][:, :], in0=outsb[si % 2][:, :], in1=xrtile[si % 3][:, :],
            op=ALU.add), waits=[v2], inc=chV)
        xr_cons[si] = v3
        epi_V[si] = v3
        out_G[si] = P.op("G", lambda e, j=j, si=si: e.dma_start(
            out=out_h[j * 128:(j + 1) * 128, :], in_=outsb[si % 2][:, :]),
            waits=[v3], inc=chG)

    e_sdma_qt(0)
    e_sdma_fx(0)
    e_scores(0)
    for k in range(NP2):
        e_sm_pre(2 * k)
        e_sm_pre(2 * k + 1)
        if k + 1 < NP2:
            e_sdma_qt(k + 1)
            e_scores(k + 1)
        e_sm_post(2 * k)
        e_sm_post(2 * k + 1)
        e_trans_av(2 * k)
        e_epilogue(2 * k)
        if k + 1 < NP2:
            e_sdma_fx(k + 1)
        e_trans_av(2 * k + 1)
        e_epilogue(2 * k + 1)

    fin = chG.snap() + chY.snap() + chQ.snap()
    P.op("G", lambda e: e.nop(), waits=[fin])
    p2b.close()

    sems = {}
    for c in chans:
        for n in c.sem_names():
            sems[n] = es.enter_context(nc.semaphore(n))
    block = es.enter_context(nc.Block())

    @block.tensor
    def _(t):
        P.replay("T", t, sems)

    @block.vector
    def _(v):
        P.replay("V", v, sems)

    @block.scalar
    def _(s):
        P.replay("S", s, sems)

    @block.sync
    def _(sy):
        P.replay("Y", sy, sems)

    @block.gpsimd
    def _(g):
        P.replay("G", g, sems)

    es.close()
    return nc



_NC_CACHE = {}


def _get_nc():
    if "nc" not in _NC_CACHE:
        _NC_CACHE["nc"] = build_nc()
    return _NC_CACHE["nc"]


def kernel(L0_attn_x, L0_attn_Wq, L0_attn_Wk, L0_attn_Wv,
           L0_ffn_kernel, L0_ffn_bias, L0_ffn_a, L0_ffn_b):
    x = np.ascontiguousarray(np.asarray(L0_attn_x), dtype=np.float32)
    Wq = np.asarray(L0_attn_Wq, dtype=np.float32)
    Wk = np.asarray(L0_attn_Wk, dtype=np.float32)
    Wv = np.asarray(L0_attn_Wv, dtype=np.float32)
    Wker = np.ascontiguousarray(np.asarray(L0_ffn_kernel), dtype=np.float32)
    bias = np.asarray(L0_ffn_bias, dtype=np.float32)
    a_m = np.asarray(L0_ffn_a, dtype=np.float32)
    b_m = np.ascontiguousarray(np.asarray(L0_ffn_b), dtype=np.float32)

    wqT = np.ascontiguousarray(Wq.T)
    wkT = np.ascontiguousarray(Wk.T)
    wvT = np.ascontiguousarray(Wv.T)
    aT = np.ascontiguousarray(a_m.T)
    biasb = np.ascontiguousarray(np.broadcast_to(bias[None, :], (128, C)))
    identb = np.eye(128, dtype=ml_dtypes.bfloat16)
    identf = np.eye(128, dtype=np.float32)
    onesrow = np.ones((1, 128), np.float32)
    onescol = np.ones((128, 1), np.float32)

    tri = np.tril(np.ones((128, 128), np.float32))
    diag_m = np.where(tri > 0, 0.0, NEG).astype(np.float32)
    full_m = np.full((128, 128), NEG, np.float32)
    zero_m = np.zeros((128, 128), np.float32)
    mask_par = [np.ascontiguousarray(np.concatenate([diag_m, full_m], axis=1)),
                np.ascontiguousarray(np.concatenate([zero_m, diag_m], axis=1))]

    nc = _get_nc()
    in_maps, rows_per_core = [], []
    for core in range(8):
        b, par = core // 2, core % 2
        tiles = [slot_tile(j, par) for j in range(NSLOT)]
        own = np.concatenate([np.arange(128 * t, 128 * (t + 1)) for t in tiles])
        rows_per_core.append(own)
        xseq = x[b]
        in_maps.append({
            "xfm": np.ascontiguousarray(xseq.T),
            "xcols": np.ascontiguousarray(xseq[own].T),
            "xrows": np.ascontiguousarray(xseq[own]),
            "wqT": wqT, "wkT": wkT, "wvT": wvT,
            "wker": Wker, "aT": aT, "bmat": b_m,
            "biasb": biasb, "maskt": mask_par[par],
            "identb": identb, "identf": identf,
            "onesrow": onesrow, "onescol": onescol,
            "invCrow": np.full((1, 128), 1.0 / C, np.float32),
        })

    res = run_bass_kernel_spmd(nc, in_maps, core_ids=list(range(8)))
    out = np.empty((4, T_TOK, C), np.float32)
    for core in range(8):
        out[core // 2, rows_per_core[core]] = res.results[core]["out"]
    return out


# version 27
# speedup vs baseline: 1.1904x; 1.0229x over previous
import numpy as np
import ml_dtypes
from contextlib import ExitStack

import concourse.bass as bass
import concourse.mybir as mybir
from concourse.bass_utils import run_bass_kernel_spmd

F32 = mybir.dt.float32
BF16 = mybir.dt.bfloat16
AF = mybir.ActivationFunctionType
ALU = mybir.AluOpType
AX = mybir.AxisListType

T_TOK = 4096
C = 1024
NCH = 8
G1 = 512
NG1 = T_TOK // G1
NB2 = 4
NSLOT = 16
NEG = -1.0e30
EPS = 1e-5
SCALE = 1.0 / 32.0
KRING = 3


def slot_kcap(j):
    p, hi = j // 2, j % 2
    return 256 * (16 - p) if hi else 256 * (p + 1)


def slot_tile(j, parity):
    p, hi = j // 2, j % 2
    return (30 - 2 * p + parity) if hi else (2 * p + parity)


def score_chunks(kcap):
    out, o = [], 0
    while o < kcap:
        w = min(512, kcap - o)
        out.append((o, w))
        o += w
    return out


class Ch:

    def __init__(self, name):
        self.name, self.v = name, 0

    def tok(self):
        return [(self.name, self.v)]

    def sem_names(self):
        return [self.name]


class DCh:

    def __init__(self, name, r):
        self.name, self.r = name, r
        self.c = [0] * r
        self.seq = 0

    def snap(self):
        return [(f"{self.name}{i}", self.c[i]) for i in range(self.r) if self.c[i] > 0]

    def sem_names(self):
        return [f"{self.name}{i}" for i in range(self.r)]


class Plan:
    ENGS = ("T", "V", "S", "Y", "G")

    def __init__(self):
        self.items = {e: [] for e in self.ENGS}

    def op(self, eng, fn, waits=(), inc=None):
        ws = []
        for t in waits:
            if t:
                ws.extend(t)
        tok, semname, amt = None, None, 0
        if isinstance(inc, DCh):
            idx = inc.seq % inc.r
            inc.seq += 1
            semname = f"{inc.name}{idx}"
            if inc.c[idx] > 0:
                ws.append((semname, inc.c[idx]))
            inc.c[idx] += 16
            amt = 16
            tok = [(semname, inc.c[idx])]
        elif isinstance(inc, Ch):
            inc.v += 1
            semname, amt = inc.name, 1
            tok = [(inc.name, inc.v)]
        ws = tuple((n, v) for (n, v) in ws if v and v > 0)
        self.items[eng].append((ws, fn, semname, amt))
        return tok

    def replay(self, eng, engobj, sems):
        for ws, fn, semname, amt in self.items[eng]:
            seen = set()
            for n, v in ws:
                key = n
                if key in seen:
                    v = max(v, dict(ws).get(n, v))
                    continue
                seen.add(key)
                vv = max(val for (nn, val) in ws if nn == n)
                engobj.wait_ge(sems[n], vv)
            ins = fn(engobj)
            if semname is not None:
                ins.then_inc(sems[semname], amt)



_DBG = {}


def build_nc():
    nc = bass.Bass("TRN2", target_bir_lowering=False)
    es = ExitStack()

    d = {}
    for nm, shp, dt in [
        ("xfm", [C, T_TOK], F32), ("xcols", [C, 2048], F32), ("xrows", [2048, C], F32),
        ("wqT", [C, C], F32), ("wkT", [C, C], F32), ("wvT", [C, C], F32),
        ("wker", [C, C], F32), ("aT", [16, C], F32), ("bmat", [16, C], F32),
        ("biasb", [128, C], F32), ("maskt", [128, 256], F32),
        ("identb", [128, 128], BF16), ("identf", [128, 128], F32),
        ("onesrow", [1, 128], F32), ("onescol", [128, 1], F32),
        ("invCrow", [1, 128], F32),
    ]:
        d[nm] = nc.dram_tensor(nm, shp, dt, kind="ExternalInput").ap()
    out_h = nc.dram_tensor("out", [2048, C], F32, kind="ExternalOutput").ap()
    kT_h = nc.dram_tensor("kT_h", [NG1, 128, NCH * G1], BF16, kind="Internal").ap()
    qT_h = nc.dram_tensor("qT_h", [NB2, 128, NCH * G1], BF16, kind="Internal").ap()
    ff_h = nc.dram_tensor("ff_h", [NSLOT, 128, C], F32, kind="Internal").ap()
    scr_h = nc.dram_tensor("scr_h", [128, 1], F32, kind="Internal").ap()
    scrb_h = nc.dram_tensor("scrb_h", [1, 4], F32, kind="Internal").ap()

    chT, chV, chS, chP = Ch("sT"), Ch("sV"), Ch("sS"), Ch("sP")
    chY, chG, chQ = DCh("sY", 8), DCh("sG", 8), DCh("sQ", 8)
    chans = [chT, chV, chS, chP, chY, chG, chQ]
    P = Plan()

    sb_off = {}

    def _rec(n, scope, fn):
        b0 = nc.sbuf_base
        h = fn()
        sb_off[n] = (b0, nc.sbuf_base, scope)
        return h

    def sb(n, s, t):
        return _rec(n, "es", lambda: es.enter_context(nc.sbuf_tensor(n, s, t)))

    v_res = sb("v_res", [128, 32 * C], BF16)
    identb = sb("identb_s", [128, 128], BF16)
    identf = sb("identf_s", [128, 128], F32)
    onesrow = sb("onesrow_s", [1, 128], F32)
    invCrow = sb("invCrow_s", [1, 128], F32)
    onescol = sb("onescol_s", [128, 1], BF16)
    aTs = sb("aT_s", [16, C], BF16)
    bms = sb("bm_s", [16, C], BF16)
    scol = sb("scol", [128, 1], F32)
    iscol = sb("iscol", [128, 1], F32)
    n8scol = sb("n8scol", [128, 1], F32)
    pw = ExitStack()
    weff = pw.enter_context(nc.sbuf_tensor("weff", [128, NCH * C], BF16, side="right"))

    t_idb = P.op("Y", lambda e: e.dma_start(out=identb[:, :], in_=d["identb"][:, :]), inc=chY)
    t_idf = P.op("Y", lambda e: e.dma_start(out=identf[:, :], in_=d["identf"][:, :]), inc=chY)
    t_or = P.op("Y", lambda e: e.dma_start(out=onesrow[:, :], in_=d["onesrow"][:, :]), inc=chY)
    t_ic = P.op("Y", lambda e: e.dma_start(out=invCrow[:, :], in_=d["invCrow"][:, :]), inc=chY)
    t_oc = P.op("G", lambda e: e.dma_start(out=onescol[:, :], in_=d["onescol"][:, :]), inc=chG)
    t_aT = P.op("G", lambda e: e.dma_start(out=aTs[:, :], in_=d["aT"][:, :]), inc=chG)
    t_bm = P.op("G", lambda e: e.dma_start(out=bms[:, :], in_=d["bmat"][:, :]), inc=chG)

    p1 = ExitStack()

    def sb1(n, s, t):
        return _rec(n, "p1", lambda: p1.enter_context(nc.sbuf_tensor(n, s, t)))

    wkb = sb1("wkb", [128, NCH * C], BF16)
    wvb = sb1("wvb", [128, NCH * C], BF16)
    xg = [sb1(f"xg{i}", [128, NCH * G1], BF16) for i in range(3)]
    kst01 = sb1("kst01", [128, 2 * NCH * G1], BF16)
    kstage = [kst01[:, i * NCH * G1:(i + 1) * NCH * G1] for i in range(2)]
    x32g0 = kst01.bitcast(F32)
    sqb = [sb1(f"sq{i}", [128, G1], BF16) for i in range(2)]
    strow = sb1("strow", [1, 4 * G1], F32)
    mbc_s = sb1("mbc_s", [128, G1], F32)
    rbc_s = sb1("rbc_s", [128, G1], F32)
    xntmp = sb1("xntmp", [128, G1], F32)
    xntmp_b = sb1("xntmp_b", [128, G1], F32)
    xnt = [xntmp, xntmp_b]
    w32 = [sb1(f"w32_{i}", [128, C], F32) for i in range(2)]
    lad_u = sb1("lad_u", [128, G1], F32)
    lad_a = sb1("lad_a", [128, G1], F32)
    lad_b = sb1("lad_b", [128, G1], F32)
    ab_sb = sb1("ab_sb", [128, NCH * C], BF16)
    gcol = sb1("gcol", [128, 16], F32)
    grow = sb1("grow", [1, 256], F32)
    _DBG["p1_headroom"] = nc.sbuf_top - nc.sbuf_base

    xg0_Y = []
    for h in range(2):
        xg0_Y.append(P.op("Y", lambda e, h=h: e.dma_start(
            out=x32g0[:, h * 4 * G1:(h + 1) * 4 * G1].rearrange("p (a t) -> p a t", a=4),
            in_=d["xfm"].rearrange("(a p) t -> p a t", p=128)[:, 4 * h:4 * (h + 1), 0:G1]),
            inc=chY))
    cast_tok = {}
    for cc in range(NCH):
        if cc % 2 == 0:
            cast_tok[cc] = P.op("V", lambda e, cc=cc: e.tensor_copy(
                out=xg[0][:, cc * G1:(cc + 1) * G1], in_=x32g0[:, cc * G1:(cc + 1) * G1]),
                waits=[xg0_Y[cc // 4]], inc=chV)
        else:
            cast_tok[cc] = P.op("S", lambda e, cc=cc: e.copy(
                out=xg[0][:, cc * G1:(cc + 1) * G1], in_=x32g0[:, cc * G1:(cc + 1) * G1]),
                waits=[xg0_Y[cc // 4]], inc=chS)

    wk_Y, wk_cast = {}, {}
    wk_Y[0] = P.op("Y", lambda e: e.dma_start(
        out=x32g0[:, :].rearrange("p (a t) -> p a t", a=NCH),
        in_=d["wkT"].rearrange("(a p) x -> p a x", p=128)[:, :, 0:512]),
        waits=[cast_tok[cc] for cc in range(NCH)], inc=chY)

    def e_wk_tail():
        for h in range(2):
            if h == 1:
                wk_Y[1] = P.op("Y", lambda e: e.dma_start(
                    out=x32g0[:, :].rearrange("p (a t) -> p a t", a=NCH),
                    in_=d["wkT"].rearrange("(a p) x -> p a x", p=128)[:, :, 512:1024]),
                    waits=[wk_cast[0]], inc=chY)
            wk_cast[h] = P.op("S", lambda e, h=h: e.copy(
                out=wkb[:, :].rearrange("p (a x) -> p a x", a=NCH)[:, :, h * 512:(h + 1) * 512],
                in_=x32g0[:, :].rearrange("p (a t) -> p a t", a=NCH)),
                waits=[wk_Y[h]], inc=chS)

    wv32 = ab_sb.bitcast(F32)
    wv_cast = {}
    wv_Y = {}
    wv_Y[0] = P.op("Y", lambda e: e.dma_start(
        out=wv32[:, :].rearrange("p (a t) -> p a t", a=NCH),
        in_=d["wvT"].rearrange("(a p) x -> p a x", p=128)[:, :, 0:512]), inc=chY)

    def e_wv_tail():
        for h in range(2):
            if h == 1:
                wv_Y[1] = P.op("Y", lambda e: e.dma_start(
                    out=wv32[:, :].rearrange("p (a t) -> p a t", a=NCH),
                    in_=d["wvT"].rearrange("(a p) x -> p a x", p=128)[:, :, 512:1024]),
                    waits=wv_cast[0], inc=chY)
            toks = []
            for q in range(2):
                dst = wvb[:, :].rearrange("p (a x) -> p a x", a=NCH)[
                    :, 4 * q:4 * (q + 1), h * 512:(h + 1) * 512]
                srcv = wv32[:, :].rearrange("p (a t) -> p a t", a=NCH)[:, 4 * q:4 * (q + 1), :]
                toks.append(P.op("V", lambda e, dst=dst, srcv=srcv: e.tensor_copy(
                    out=dst, in_=srcv), waits=[wv_Y[h]], inc=chV))
            wv_cast[h] = toks
        for i in range(6):
            w = [] if i < 2 else list(wv_cast[1])
            qa_y[i] = P.op("Y", lambda e, i=i: e.dma_start(
                out=scan_buf(i), in_=d["wker"][i * 128:(i + 1) * 128, :]),
                waits=w, inc=chY)
        return wv_cast[0] + wv_cast[1]

    def scan_buf(i):
        if i in (0, 1, 6, 7):
            return w32[i % 2][:, :]
        return wv32[:, (i - 2) * C:(i - 1) * C]

    qa_y = {}
    qmax_done = {}
    y_bc = []
    v_ab_tok = {}

    def ps1(n, s, t=F32):
        return p1.enter_context(nc.psum_tensor(n, s, t))

    T0 = ps1("T0", [128, G1])
    T1 = ps1("T1", [128, G1])
    psK = [ps1(f"psK{i}", [128, G1]) for i in range(2)]
    psV = [[ps1(f"psV{i}_{h}", [128, 512]) for h in range(2)] for i in range(2)]
    s1r, s2r, tr1, tr2 = (strow[0:1, i * G1:(i + 1) * G1] for i in range(4))

    def e_scan():
        for i in range(NCH):
            if i >= 6:
                qa_y[i] = P.op("Y", lambda e, i=i: e.dma_start(
                    out=scan_buf(i), in_=d["wker"][i * 128:(i + 1) * 128, :]),
                    waits=[qmax_done[i - 6]], inc=chY)
            qmax_done[i] = P.op("V", lambda e, i=i: e.reduce_max(
                gcol[:, i:i + 1], scan_buf(i), axis=AX.X, apply_absolute_value=True),
                waits=[qa_y[i]], inc=chV)
        v_am = P.op("V", lambda e: e.reduce_max(gcol[:, 8:9], gcol[:, 0:8], axis=AX.X),
                    waits=[qmax_done[NCH - 1]], inc=chV)
        y_sc1 = P.op("Y", lambda e: e.dma_start(out=scr_h[:, :], in_=gcol[:, 8:9]),
                     waits=[v_am], inc=chY)
        y_sc2 = P.op("Y", lambda e: e.dma_start(
            out=grow[0:1, 0:128], in_=scr_h.rearrange("p one -> one p")),
            waits=[y_sc1], inc=chY)
        v_gm = P.op("V", lambda e: e.reduce_max(grow[0:1, 128:129], grow[0:1, 0:128],
                                                axis=AX.X), waits=[y_sc2], inc=chV)
        v_s = P.op("V", lambda e: e.tensor_scalar(out=grow[0:1, 129:130], in0=grow[0:1, 128:129],
                                                  scalar1=1.0 / 7.0, scalar2=None, op0=ALU.mult),
                   waits=[v_gm], inc=chV)
        v_rc = P.op("V", lambda e: e.reciprocal(out=grow[0:1, 130:131], in_=grow[0:1, 128:129]),
                    waits=[v_gm], inc=chV)
        P.op("V", lambda e: e.tensor_scalar(out=grow[0:1, 131:132], in0=grow[0:1, 130:131],
                                            scalar1=7.0, scalar2=None, op0=ALU.mult),
             waits=[v_rc], inc=chV)
        v_n8 = P.op("V", lambda e: e.tensor_scalar(out=grow[0:1, 132:133], in0=grow[0:1, 129:130],
                                                   scalar1=-8.0, scalar2=None, op0=ALU.mult),
                    waits=[v_s], inc=chV)
        y_sc3 = P.op("Y", lambda e: e.dma_start(out=scrb_h[0:1, :], in_=grow[0:1, 129:133]),
                     waits=[v_n8], inc=chY)
        for cidx, dst in ((0, scol), (2, iscol), (3, n8scol)):
            y_bc.append(P.op("Y", lambda e, cidx=cidx, dst=dst: e.dma_start(
                out=dst[:, :], in_=scrb_h[0:1, cidx:cidx + 1].broadcast_to([128, 1])),
                waits=[y_sc3], inc=chY))

    def e_ab_iter(i):
        w1 = [t_aT, t_bm]
        if i == 0:
            w1.append(vmm_S[1])
        if i - 1 in v_ab_tok:
            w1.append(v_ab_tok[i - 1])
        t1 = P.op("T", lambda e, i=i: e.matmul(
            psV[0][0][:, :], aTs[:, i * 128:(i + 1) * 128], bms[:, 0:512],
            start=True, stop=True), waits=w1, inc=chT)
        t2 = P.op("T", lambda e, i=i: e.matmul(
            psV[0][1][:, :], aTs[:, i * 128:(i + 1) * 128], bms[:, 512:1024],
            start=True, stop=True), inc=chT)
        wv_ = [t2, y_bc[2], qmax_done[min(2 + i // 2, 5)]] + wv_cast[1]
        P.op("V", lambda e, i=i: e.tensor_scalar(
            out=ab_sb[:, i * C:i * C + 512], in0=psV[0][0][:, :],
            scalar1=n8scol[:, 0:1], scalar2=None, op0=ALU.add), waits=wv_, inc=chV)
        v_ab_tok[i] = P.op("V", lambda e, i=i: e.tensor_scalar(
            out=ab_sb[:, i * C + 512:(i + 1) * C], in0=psV[0][1][:, :],
            scalar1=n8scol[:, 0:1], scalar2=None, op0=ALU.add), inc=chV)

    xg_dma, stats_T, math_V, bcast_T, bcastev_V, xn_V = {}, {}, {}, {}, {}, {}
    math_S, xn_cc = {}, {}
    vmm_T, kst_Y, kst_S, vmm_S = {}, {}, {}, {}
    sq_T, sq_S = {}, {}
    quant_V, qw_Y = {}, {}

    def e_xg(g):
        w = [xg_dma_gate.get(g)]
        xg_dma[g] = P.op("G", lambda e, g=g: e.dma_start(
            out=xg[g % 3][:, :].rearrange("p (a t) -> p a t", a=NCH),
            in_=d["xfm"].rearrange("(a p) t -> p a t", p=128)[:, :, g * G1:(g + 1) * G1]),
            waits=w, inc=chG)

    xg_dma_gate = {}

    def e_stats(g, bankfree):
        for cc in range(NCH):
            wsq = [cast_tok[cc]] if g == 0 else [xg_dma[g]]
            key = (g, cc - 2) if cc >= 2 else (g - 1, cc + 6)
            if key in sq_T:
                wsq += [sq_T[key], sq_S[key]]
            sq_S[(g, cc)] = P.op("S", lambda e, g=g, cc=cc: e.activation(
                out=sqb[cc % 2][:, :], in_=xg[g % 3][:, cc * G1:(cc + 1) * G1],
                func=AF.Square), waits=wsq, inc=chS)
            w = ([cast_tok[cc]] if g == 0 else [xg_dma[g]]) + [t_oc]
            if cc == 0:
                w.append(bankfree)
            P.op("T", lambda e, g=g, cc=cc: e.matmul(
                T0[0:1, :], onescol[:, 0:1], xg[g % 3][:, cc * G1:(cc + 1) * G1],
                start=(cc == 0), stop=(cc == NCH - 1), skip_group_check=True),
                waits=w, inc=chT)
            sq_T[(g, cc)] = P.op("T", lambda e, g=g, cc=cc: e.matmul(
                T1[0:1, :], onescol[:, 0:1], sqb[cc % 2][:, :],
                start=(cc == 0), stop=(cc == NCH - 1), skip_group_check=True),
                waits=[sq_S[(g, cc)]], inc=chT)
        stats_T[g] = chT.tok()

    def e_math(g, rows, prev=()):
        s1, s2, t1_, t2_ = rows
        v0 = P.op("V", lambda e, s1=s1: e.tensor_copy(out=s1, in_=T0[0:1, :]),
                  waits=[stats_T[g]] + list(prev), inc=chV)
        v1 = P.op("V", lambda e, s2=s2: e.tensor_copy(out=s2, in_=T1[0:1, :]), inc=chV)
        v2 = P.op("V", lambda e, t1_=t1_, s1=s1: e.tensor_tensor(
            out=t1_, in0=s1, in1=s1, op=ALU.mult), waits=[v0], inc=chV)
        v3 = P.op("V", lambda e, t1_=t1_, s2=s2: e.scalar_tensor_tensor(
            out=t1_, in0=s2, scalar=float(C), in1=t1_, op0=ALU.mult, op1=ALU.subtract),
            waits=[v1, v2], inc=chV)
        v4 = P.op("V", lambda e, t1_=t1_: e.tensor_scalar(
            out=t1_, in0=t1_, scalar1=1.0 / (C * C), scalar2=EPS,
            op0=ALU.mult, op1=ALU.add), waits=[v3], inc=chV)
        s0 = P.op("S", lambda e, t1_=t1_: e.activation(out=t1_, in_=t1_, func=AF.Ln),
                  waits=[v4], inc=chS)
        s1x = P.op("S", lambda e, t2_=t2_, t1_=t1_: e.activation(
            out=t2_, in_=t1_, func=AF.Exp, scale=-0.5), waits=[s0], inc=chS)
        math_S[g] = s1x
        return P.op("V", lambda e: e.engine_nop(), waits=[s1x], inc=chV)

    def e_bcast(g):
        P.op("T", lambda e: e.matmul(T0[:, :], invCrow[0:1, :], s1r,
                                     start=True, stop=True),
             waits=[math_V[g], stats_T[g], t_ic, t_or], inc=chT)
        bcast_T[g] = P.op("T", lambda e: e.matmul(T1[:, :], onesrow[0:1, :], tr2,
                                                  start=True, stop=True), inc=chT)

    def e_xn(g):
        P.op("V", lambda e: e.tensor_copy(out=mbc_s[:, :], in_=T0[:, :]),
             waits=[bcast_T[g]], inc=chV)
        ev = P.op("V", lambda e: e.tensor_copy(out=rbc_s[:, :], in_=T1[:, :]),
                  inc=chV)
        bcastev_V[g] = ev
        mul_t = {}
        for c0 in range(0, NCH, 2):
            subs = []
            for cc in (c0, c0 + 1):
                w = [ev] if cc < 2 else [mul_t[cc - 2]]
                if g == 0:
                    w.append(cast_tok[cc])
                subs.append(P.op("V", lambda e, g=g, cc=cc: e.tensor_tensor(
                    out=xnt[cc % 2][:, :], in0=xg[g % 3][:, cc * G1:(cc + 1) * G1],
                    in1=mbc_s[:, :], op=ALU.subtract), waits=w, inc=chV))
            for k, cc in enumerate((c0, c0 + 1)):
                mul_t[cc] = P.op("V", lambda e, g=g, cc=cc: e.tensor_tensor(
                    out=xg[g % 3][:, cc * G1:(cc + 1) * G1], in0=xnt[cc % 2][:, :],
                    in1=rbc_s[:, :], op=ALU.mult), waits=[subs[k]], inc=chV)
        xn_V[g] = mul_t[NCH - 1]
        xn_cc[g] = dict(mul_t)

    def e_kproj(g, extra=None):
        ev = {}
        for dc in range(NCH):
            for cc in range(NCH):
                w = [xn_cc[g][cc]] if dc == 0 else []
                if cc == 0:
                    if dc == 0:
                        w.append(wk_cast[0])
                        if g >= 2:
                            w.append(kst_Y[g - 2])
                    if dc == 4:
                        w.append(wk_cast[1])
                    if dc >= 2:
                        w.append(ev[dc - 2])
                    elif g >= 1:
                        w.append(kst_S[g - 1])
                P.op("T", lambda e, g=g, dc=dc, cc=cc: e.matmul(
                    psK[dc % 2][:, :],
                    wkb[:, :].rearrange("p (a x) -> p a x", a=NCH)[:, cc, dc * 128:(dc + 1) * 128],
                    xg[g % 3][:, cc * G1:(cc + 1) * G1],
                    start=(cc == 0), stop=(cc == NCH - 1)), waits=w, inc=chT)
            tmm = chT.tok()
            wk_ev = [tmm] + ([kst_S[g - 2]] if dc == 0 and g >= 2 else [])
            if g <= 1 and dc == 0:
                wk_ev.append(wk_cast[1])
            ev[dc] = P.op("S", lambda e, g=g, dc=dc: e.copy(
                out=kstage[g % 2][:, dc * G1:(dc + 1) * G1], in_=psK[dc % 2][:, :]),
                waits=wk_ev, inc=chS)
            if extra is not None:
                extra(dc)
        kst_S[g] = ev[NCH - 1]
        kst_Y[g] = P.op("Y", lambda e, g=g: e.dma_start(
            out=kT_h[g, :, :], in_=kstage[g % 2][:, :]),
            waits=[ev[NCH - 1]], inc=chY)

    def e_vproj(g):
        evs = {}
        for tt in range(4):
            for h in range(2):
                for cc in range(NCH):
                    w = [xn_cc[g][cc]] if (tt == 0 and h == 0) else []
                    if cc == 0 and h == 0:
                        if tt >= 2:
                            w.append(evs[tt - 2])
                        elif g >= 1:
                            w.append(vmm_S[g - 1])
                        if tt == 0:
                            w += wv_done
                            if g == 2:
                                w.append(v_ab_tok[NCH - 1])
                    P.op("T", lambda e, g=g, tt=tt, h=h, cc=cc: e.matmul(
                        psV[tt % 2][h][:, :],
                        xg[g % 3][:, cc * G1 + tt * 128: cc * G1 + (tt + 1) * 128],
                        wvb[:, :].rearrange("p (a x) -> p a x", a=NCH)[:, cc, h * 512:(h + 1) * 512],
                        start=(cc == 0), stop=(cc == NCH - 1)), waits=w, inc=chT)
            tmm = chT.tok()
            base = (4 * g + tt) * C
            P.op("S", lambda e, tt=tt, base=base: e.copy(
                out=v_res[:, base:base + 512], in_=psV[tt % 2][0][:, :]),
                waits=[tmm], inc=chS)
            evs[tt] = P.op("S", lambda e, tt=tt, base=base: e.copy(
                out=v_res[:, base + 512:base + 1024], in_=psV[tt % 2][1][:, :]), inc=chS)
        vmm_T[g] = chT.tok()
        vmm_S[g] = evs[3]

    def e_qdma(i):
        w = [quant_V[i - 2]] if i >= 2 else [qmax_done[i + 6]]
        qw_Y[i] = P.op("Y", lambda e, i=i: e.dma_start(
            out=w32[i % 2][:, :], in_=d["wker"][i * 128:(i + 1) * 128, :]),
            waits=w, inc=chY)

    def e_quant(i):
        vq = chV.tok()
        for hh in range(2):
            w = [qw_Y[i], vq]
            if i == 0 and hh == 0:
                w += [v_ab_tok[NCH - 1], y_bc[0], y_bc[1]]
            vq = P.op("V", lambda e, i=i, hh=hh: e.tensor_scalar(
                out=lad_u[:, :], in0=w32[i % 2][:, hh * 512:(hh + 1) * 512],
                scalar1=iscol[:, 0:1], scalar2=8.5, op0=ALU.mult, op1=ALU.add),
                waits=w, inc=chV)
            vq = P.op("V", lambda e: e.tensor_scalar(
                out=lad_a[:, :], in0=lad_u[:, :], scalar1=8.0, scalar2=8.0,
                op0=ALU.is_ge, op1=ALU.mult), waits=[vq], inc=chV)
            for st in (4.0, 2.0, 1.0):
                vq = P.op("V", lambda e: e.tensor_tensor(out=lad_b[:, :], in0=lad_u[:, :],
                                                         in1=lad_a[:, :], op=ALU.subtract),
                          waits=[vq], inc=chV)
                vq = P.op("V", lambda e, st=st: e.tensor_scalar(
                    out=lad_b[:, :], in0=lad_b[:, :], scalar1=st, scalar2=st,
                    op0=ALU.is_ge, op1=ALU.mult), waits=[vq], inc=chV)
                vq = P.op("V", lambda e: e.tensor_tensor(out=lad_a[:, :], in0=lad_a[:, :],
                                                         in1=lad_b[:, :], op=ALU.add),
                          waits=[vq], inc=chV)
            vq = P.op("V", lambda e, i=i, hh=hh: e.scalar_tensor_tensor(
                out=weff[:, i * C + hh * 512: i * C + (hh + 1) * 512], in0=lad_a[:, :],
                scalar=scol[:, 0:1], in1=ab_sb[:, i * C + hh * 512: i * C + (hh + 1) * 512],
                op0=ALU.mult, op1=ALU.add), waits=[vq], inc=chV)
        quant_V[i] = vq

    e_stats(0, None)
    math_V[0] = e_math(0, (s1r, s2r, tr1, tr2))
    e_wk_tail()
    e_bcast(0); e_xn(0)
    e_xg(1)
    WQ32R = [None, None]
    wq_h0_Y = [None]
    wq_h1_Y = [None]
    kp7tok = [None]

    def wq_hook(dc):
        if dc == NCH - 1:
            kp7tok[0] = chT.tok()
            wq_h0_Y[0] = P.op("Y", lambda e: e.dma_start(
                out=WQ32R[0][:, :].rearrange("p (a t) -> p a t", a=NCH),
                in_=d["wqT"].rearrange("(a p) x -> p a x", p=128)[:, :, 0:512]),
                waits=[kp7tok[0]], inc=chY)
            wq_h1_Y[0] = P.op("Y", lambda e: e.dma_start(
                out=WQ32R[1][:, :].rearrange("p (a t) -> p a t", a=NCH),
                in_=d["wqT"].rearrange("(a p) x -> p a x", p=128)[:, :, 512:1024]),
                waits=[quant_V[NCH - 1]], inc=chY)

    quant_sched = {2: [0], 3: [1], 4: [2, 3], 5: [4, 5], 6: [6, 7]}
    qdma_sched = {3: [2], 4: [3], 5: [4, 5], 6: [6, 7]}
    e_stats(1, bcastev_V[0])
    math_V[1] = e_math(1, (s1r, s2r, tr1, tr2), prev=[math_V[0], math_S[0]])
    e_bcast(1); e_xn(1)
    wv_done = e_wv_tail()
    e_kproj(0)
    e_scan()
    e_qdma(0); e_qdma(1)
    for g in range(1, NG1 + 1):
        if g + 1 <= NG1 - 1:
            xg_dma_gate[g + 1] = vmm_T[g - 2] if g >= 2 else None
            e_xg(g + 1)
        for qi in qdma_sched.get(g, []):
            e_qdma(qi)
        e_vproj(g - 1)
        if g == 7:
            for qi in quant_sched.get(g, []):
                e_quant(qi)
        if g <= NG1 - 1:
            ex = e_ab_iter if g == 2 else (wq_hook if g == 7 else None)
            e_kproj(g, extra=ex)
            if g + 1 <= NG1 - 1:
                e_stats(g + 1, bcastev_V[g])
                math_V[g + 1] = e_math(g + 1, (s1r, s2r, tr1, tr2),
                                       prev=[math_V[g], math_S[g]])
                e_bcast(g + 1)
                e_xn(g + 1)
        if g < 7:
            for qi in quant_sched.get(g, []):
                e_quant(qi)
    p1_last = {
        "wkb": kp7tok[0], "wvb": vmm_T[NG1 - 1],
        "xg0": vmm_T[6] + xn_V[6], "xg1": vmm_T[7] + xn_V[7],
        "xg2": vmm_T[5] + xn_V[5],
        "kst_even": kst_Y[6] + kst_S[6], "kst_odd": kst_Y[7] + kst_S[7],
        "sqb": stats_T[7] + sq_S[(7, 7)],
        "strow": bcast_T[7] + math_V[7],
        "xnrow": xn_V[7],
        "w32": quant_V[7], "lad": quant_V[7], "ab": quant_V[7],
        "psT01": bcastev_V[7], "psK": kst_S[7], "psV": vmm_S[7],
    }
    bar1 = (chT.tok() + chV.tok() + chS.tok() + chY.snap() + chG.snap() + chQ.snap())
    p1.close()

    p2a = ExitStack()

    def sb2(n, s, t):
        return _rec(n, "p2a", lambda: p2a.enter_context(nc.sbuf_tensor(n, s, t)))

    def ps2(n, s, t=F32):
        return p2a.enter_context(nc.psum_tensor(n, s, t))

    wq32 = sb2("wq32", [128, 4 * C], F32)
    wqb = sb2("wqb", [128, NCH * C], BF16)
    xga = [sb2(f"xga{i}", [128, NCH * G1], BF16) for i in range(3)]
    strow2 = sb2("strow2", [1, 4 * G1], F32)
    mbc2 = sb2("mbc2", [128, G1], F32)
    rbc2 = sb2("rbc2", [128, G1], F32)
    xntmp2 = sb2("xntmp2", [128, G1], F32)
    xntmp2_b = sb2("xntmp2_b", [128, G1], F32)
    sqa = [sb2(f"sqa{i}", [128, G1], BF16) for i in range(2)]
    xnt2 = [xntmp2, xntmp2_b]
    qstage = [sb2(f"qst{i}", [128, NCH * G1], BF16) for i in range(2)]
    ffstage = [sb2(f"ffst{i}", [128, C], F32) for i in range(2)]
    wq32b = sb2("wq32b", [128, 4 * C], F32)
    biasb = sb2("biasb_s", [128, C], F32)
    U0 = ps2("U0", [128, G1])
    U1 = ps2("U1", [128, G1])
    psQ = [ps2(f"psQ{i}", [128, G1]) for i in range(2)]
    psF = [[ps2(f"psF{i}_{h}", [128, 512]) for h in range(2)] for i in range(2)]
    s1a, s2a, ta1, ta2 = (strow2[0:1, i * G1:(i + 1) * G1] for i in range(4))
    WQ32R[0] = wq32
    WQ32R[1] = wq32b
    KRESR = [None]
    QTILER = [None, None]
    MASKTR = [None]
    kres_early = []
    qt_early = {}
    t_mk_early = [None]

    def _rng(n):
        return sb_off[n][0:2]
    assert _rng("wq32") == _rng("wkb"), (sb_off["wq32"], sb_off["wkb"])
    assert _rng("wqb") == _rng("wvb")
    for i in range(3):
        assert _rng(f"xga{i}") == _rng(f"xg{i}")
    assert _rng("strow2") == (_rng("kst01")[0], _rng("kst01")[0] + 8192)
    assert _rng("xntmp2_b")[1] == _rng("kst01")[1]
    assert _rng("sqa0") == _rng("sq0") and _rng("sqa1") == _rng("sq1")
    assert _rng("qst0") == _rng("strow")
    assert (_rng("qst1")[0], _rng("qst1")[1]) == (_rng("mbc_s")[0], _rng("xntmp_b")[1])
    assert _rng("ffst0") == _rng("w32_0") and _rng("ffst1") == _rng("w32_1")
    assert _rng("wq32b")[0] >= _rng("lad_u")[0] and _rng("wq32b")[1] <= _rng("ab_sb")[1]
    assert _rng("biasb_s")[1] <= _rng("ab_sb")[1]

    wq_cast = {}

    def e_wq_casts():
        for h in range(2):
            stg = wq32 if h == 0 else wq32b
            for q in range(2):
                dst = wqb[:, :].rearrange("p (a x) -> p a x", a=NCH)[
                    :, 4 * q:4 * (q + 1), h * 512:(h + 1) * 512]
                srcv = stg[:, :].rearrange("p (a t) -> p a t", a=NCH)[:, 4 * q:4 * (q + 1), :]
                tk = P.op("S", lambda e, dst=dst, srcv=srcv: e.copy(out=dst, in_=srcv),
                          waits=[wq_h0_Y[0] if h == 0 else wq_h1_Y[0],
                                 p1_last["wvb"] if (h == 0 and q == 0) else None], inc=chS)
            wq_cast[h] = tk

    t_bb = P.op("Y", lambda e: e.dma_start(out=biasb[:, :], in_=d["biasb"][:, :]),
                waits=[p1_last["lad"]], inc=chY)

    xga_dma, stats2_T, math2_V, bcast2_T, bcastev2_V, xn2_V = {}, {}, {}, {}, {}, {}
    fmm_T, qst_Y, qst_S, ff_Y, ff_V = {}, {}, {}, {}, {}
    math2_S, ffev, xn2_cc = {}, {}, {}
    sq2_T, sq2_S = {}, {}
    xga_gate = {}

    SL = (0, 2, 1, 0)
    SLGATE = {0: p1_last["xg0"], 2: p1_last["xg2"], 1: p1_last["xg1"]}

    def e_xga(b):
        w = [SLGATE[SL[b]] if b <= 2 else None, xga_gate.get(b)]
        xga_dma[b] = P.op("G", lambda e, b=b: e.dma_start(
            out=xga[SL[b]][:, :].rearrange("p (a t) -> p a t", a=NCH),
            in_=d["xcols"].rearrange("(a p) t -> p a t", p=128)[:, :, b * G1:(b + 1) * G1]),
            waits=w, inc=chG)

    def e_stats2(b, bankfree):
        for cc in range(NCH):
            wsq = [xga_dma[b]]
            if b == 0 and cc <= 1:
                wsq.append(p1_last["sqb"])
            key = (b, cc - 2) if cc >= 2 else (b - 1, cc + 6)
            if key in sq2_T:
                wsq += [sq2_T[key], sq2_S[key]]
            sq2_S[(b, cc)] = P.op("S", lambda e, b=b, cc=cc: e.activation(
                out=sqa[cc % 2][:, :], in_=xga[SL[b]][:, cc * G1:(cc + 1) * G1],
                func=AF.Square), waits=wsq, inc=chS)
            w = [xga_dma[b]]
            if cc == 0:
                w.append(bankfree)
            P.op("T", lambda e, b=b, cc=cc: e.matmul(
                U0[0:1, :], onescol[:, 0:1], xga[SL[b]][:, cc * G1:(cc + 1) * G1],
                start=(cc == 0), stop=(cc == NCH - 1), skip_group_check=True),
                waits=w, inc=chT)
            sq2_T[(b, cc)] = P.op("T", lambda e, b=b, cc=cc: e.matmul(
                U1[0:1, :], onescol[:, 0:1], sqa[cc % 2][:, :],
                start=(cc == 0), stop=(cc == NCH - 1), skip_group_check=True),
                waits=[sq2_S[(b, cc)]] + ([bankfree] if b == 0 and cc == 0 else []),
                inc=chT)
        stats2_T[b] = chT.tok()

    def e_math2(b, prev=()):
        reg = [p1_last["kst_even"]] if b == 0 else []
        v0 = P.op("V", lambda e: e.tensor_copy(out=s1a, in_=U0[0:1, :]),
                  waits=[stats2_T[b]] + list(prev) + reg, inc=chV)
        v1 = P.op("V", lambda e: e.tensor_copy(out=s2a, in_=U1[0:1, :]),
                  waits=reg, inc=chV)
        v2 = P.op("V", lambda e: e.tensor_tensor(
            out=ta1, in0=s1a, in1=s1a, op=ALU.mult), waits=[v0], inc=chV)
        v3 = P.op("V", lambda e: e.scalar_tensor_tensor(
            out=ta1, in0=s2a, scalar=float(C), in1=ta1, op0=ALU.mult, op1=ALU.subtract),
            waits=[v1, v2], inc=chV)
        v4 = P.op("V", lambda e: e.tensor_scalar(
            out=ta1, in0=ta1, scalar1=1.0 / (C * C), scalar2=EPS,
            op0=ALU.mult, op1=ALU.add), waits=[v3], inc=chV)
        s0 = P.op("S", lambda e: e.activation(out=ta1, in_=ta1, func=AF.Ln),
                  waits=[v4], inc=chS)
        s1x = P.op("S", lambda e: e.activation(
            out=ta2, in_=ta1, func=AF.Exp, scale=-0.5), waits=[s0], inc=chS)
        math2_S[b] = s1x
        math2_V[b] = P.op("V", lambda e: e.engine_nop(), waits=[s1x], inc=chV)

    def e_bcast2(b):
        P.op("T", lambda e: e.matmul(U0[:, :], invCrow[0:1, :], s1a,
                                     start=True, stop=True),
             waits=[math2_V[b], stats2_T[b], t_ic], inc=chT)
        bcast2_T[b] = P.op("T", lambda e: e.matmul(U1[:, :], onesrow[0:1, :], ta2,
                                                   start=True, stop=True), inc=chT)

    def e_xn2(b):
        reg = [p1_last["kst_odd"]] if b == 0 else []
        P.op("V", lambda e: e.tensor_copy(out=mbc2[:, :], in_=U0[:, :]),
             waits=[bcast2_T[b]] + reg, inc=chV)
        ev = P.op("V", lambda e: e.tensor_copy(out=rbc2[:, :], in_=U1[:, :]),
                  waits=reg, inc=chV)
        bcastev2_V[b] = ev
        mul_t = {}
        for c0 in range(0, NCH, 2):
            subs = []
            for cc in (c0, c0 + 1):
                w = ([ev] if cc < 2 else [mul_t[cc - 2]]) + (reg if cc <= 1 else [])
                subs.append(P.op("V", lambda e, b=b, cc=cc: e.tensor_tensor(
                    out=xnt2[cc % 2][:, :], in0=xga[SL[b]][:, cc * G1:(cc + 1) * G1],
                    in1=mbc2[:, :], op=ALU.subtract), waits=w, inc=chV))
            for k, cc in enumerate((c0, c0 + 1)):
                mul_t[cc] = P.op("V", lambda e, b=b, cc=cc: e.tensor_tensor(
                    out=xga[SL[b]][:, cc * G1:(cc + 1) * G1], in0=xnt2[cc % 2][:, :],
                    in1=rbc2[:, :], op=ALU.mult), waits=[subs[k]], inc=chV)
        xn2_V[b] = mul_t[NCH - 1]
        xn2_cc[b] = dict(mul_t)

    def e_qproj(b):
        ev = {}
        for dc in range(NCH):
            for cc in range(NCH):
                w = [xn2_cc[b][cc]] if dc == 0 else []
                if cc == 0:
                    if dc == 0:
                        w.append(wq_cast[0])
                        if b == 0:
                            w.append(p1_last["psK"])
                        if b >= 2:
                            w.append(qst_Y[b - 2])
                    if dc == 4:
                        w.append(wq_cast[1])
                    if dc >= 2:
                        w.append(ev[dc - 2])
                    elif b >= 1:
                        w.append(qst_S[b - 1])
                P.op("T", lambda e, b=b, dc=dc, cc=cc: e.matmul(
                    psQ[dc % 2][:, :],
                    wqb[:, :].rearrange("p (a x) -> p a x", a=NCH)[:, cc, dc * 128:(dc + 1) * 128],
                    xga[SL[b]][:, cc * G1:(cc + 1) * G1],
                    start=(cc == 0), stop=(cc == NCH - 1)), waits=w, inc=chT)
            tmm = chT.tok()
            wse = [tmm] + ([qst_S[b - 2]] if dc == 0 and b >= 2 else [])
            if b <= 1 and dc == 0:
                wse.append(p1_last["strow"] if b == 0 else p1_last["xnrow"])
            ev[dc] = P.op("S", lambda e, b=b, dc=dc: e.activation(
                out=qstage[b % 2][:, :].rearrange(
                    "p (o a t) -> p o a t", o=4, a=NCH)[:, :, dc, :],
                in_=psQ[dc % 2][:, :].rearrange("p (o t) -> p o t", o=4),
                func=AF.Copy, scale=SCALE),
                waits=wse, inc=chS)
        qst_S[b] = ev[NCH - 1]
        qst_Y[b] = P.op("Y", lambda e, b=b: e.dma_start(
            out=qT_h[b, :, :], in_=qstage[b % 2][:, :]),
            waits=[ev[NCH - 1]], inc=chY)

    def e_ffn(b):
        evs = {}
        for tt in range(4):
            j = 4 * b + tt
            for h in range(2):
                for cc in range(NCH):
                    w = [xn2_cc[b][cc]] if (tt == 0 and h == 0) else []
                    if cc == 0 and h == 0:
                        if b == 0 and tt == 0:
                            w += [quant_V[NCH - 1], p1_last["psV"]]
                        if tt >= 2:
                            w.append(evs[tt - 2])
                        elif b >= 1:
                            w.append(ff_V[b - 1])
                    P.op("T", lambda e, b=b, tt=tt, h=h, cc=cc: e.matmul(
                        psF[tt % 2][h][:, :],
                        xga[SL[b]][:, cc * G1 + tt * 128: cc * G1 + (tt + 1) * 128],
                        weff[:, :].rearrange("p (a x) -> p a x", a=NCH)[:, cc, h * 512:(h + 1) * 512],
                        start=(cc == 0), stop=(cc == NCH - 1)), waits=w, inc=chT)
            tmm = chT.tok()
            wv_ = [tmm, t_bb]
            if j == 0:
                wv_.append(p1_last["w32"])
            if j >= 2:
                wv_.append(ff_Y[j - 2])
            if j >= 2:
                wv_.append(ffev[j - 2])
            P.op("V", lambda e, tt=tt: e.tensor_tensor(
                out=ffstage[tt % 2][:, 0:512], in0=psF[tt % 2][0][:, :],
                in1=biasb[:, 0:512], op=ALU.add), waits=wv_, inc=chV)
            evs[tt] = P.op("V", lambda e, tt=tt: e.tensor_tensor(
                out=ffstage[tt % 2][:, 512:1024], in0=psF[tt % 2][1][:, :],
                in1=biasb[:, 512:1024], op=ALU.add), inc=chV)
            ffev[j] = evs[tt]
            ff_Y[j] = P.op("G", lambda e, j=j, tt=tt: e.dma_start(
                out=ff_h[j, :, :], in_=ffstage[tt % 2][:, :]),
                waits=[evs[tt]], inc=chG)
        fmm_T[b] = chT.tok()
        ff_V[b] = evs[3]

    e_xga(1)
    e_xga(0)
    e_stats2(0, p1_last["psT01"]); e_math2(0); e_bcast2(0); e_xn2(0)
    e_wq_casts()
    kres_early.append(P.op("Y", lambda e: e.dma_start(
        out=KRESR[0][:, 0:NCH * 512], in_=kT_h[0, :, :]),
        waits=[wq_cast[0], kst_Y[0]], inc=chY))
    kres_early.append(P.op("Y", lambda e: e.dma_start(
        out=KRESR[0][:, NCH * 512:], in_=kT_h[1, :, :]),
        waits=[kst_Y[1]], inc=chY))
    e_stats2(1, bcastev2_V[0]); e_math2(1, prev=[math2_V[0], math2_S[0]])
    for b in range(NB2):
        if b >= 1:
            e_ffn(b - 1)
        e_qproj(b)
        if b == NB2 - 1:
            qp3_T = chT.tok()
            ORD_ = sorted(range(NSLOT), key=lambda j: -slot_kcap(j))
            for si_ in range(2):
                j_ = ORD_[si_]
                qt_early[si_] = P.op("Y", lambda e, si_=si_, j_=j_: e.dma_start(
                    out=QTILER[si_][:, :],
                    in_=qT_h[j_ // 4, :, :][:, (j_ % 4) * 1024:(j_ % 4 + 1) * 1024]),
                    waits=[qp3_T, qst_Y[j_ // 4]], inc=chY)
            t_mk_early[0] = P.op("Y", lambda e: e.dma_start(
                out=MASKTR[0][:, :], in_=d["maskt"][:, :]), waits=[qp3_T], inc=chY)
        if b + 1 <= NB2 - 1:
            e_bcast2(b + 1)
            e_xn2(b + 1)
        if b + 2 <= NB2 - 1:
            xga_gate[b + 2] = fmm_T[b - 1] if b >= 1 else None
            e_xga(b + 2)
            e_stats2(b + 2, bcastev2_V[b + 1])
            e_math2(b + 2, prev=[math2_V[b + 1], math2_S[b + 1]])
    e_ffn(NB2 - 1)
    bar2 = (chT.tok() + chV.tok() + chS.tok() + chY.snap() + chG.snap() + chQ.snap())
    p2a.close()

    pw.close()
    p2b = ExitStack()

    def sb3(n, s, t):
        return _rec(n, "p2b", lambda: p2b.enter_context(nc.sbuf_tensor(n, s, t)))

    def ps3(n, s, t=F32):
        return p2b.enter_context(nc.psum_tensor(n, s, t))

    k_res = sb3("k_res", [128, 2 * NCH * 512], BF16)
    qtile = [sb3(f"qtile{i}", [128, NCH * 128], BF16) for i in range(4)]
    maskt = sb3("maskt_s", [128, 256], F32)
    scores = [[sb3(f"scores{p}_{i}", [128, 4096], BF16) for i in range(2)]
              for p in range(2)]
    pbuf = [sb3(f"pbuf{i}", [128, 4096], BF16) for i in range(2)]
    kbuf = [sb3(f"kbuf{i}", [128, NCH * 512], BF16) for i in range(KRING)]
    fftile = [sb3(f"fft{i}", [128, C], F32) for i in range(3)]
    xrtile = [sb3(f"xrt{i}", [128, C], F32) for i in range(3)]
    outsb = [sb3(f"outsb{i}", [128, C], F32) for i in range(2)]
    ptb = [sb3(f"ptb{i}", [128, 128], BF16) for i in range(3)]
    mxv = sb3("mxv", [128, 1], F32)
    nbias = sb3("nbias", [128, 1], F32)
    lsum = sb3("lsum", [128, 4], F32)
    invl = [sb3(f"invl{i}", [128, 1], F32) for i in range(2)]

    psS = [ps3(f"psS{a}", [128, 512]) for a in range(2)]
    psT_ = [ps3(f"psT{i}", [128, 128], BF16) for i in range(2)]
    psAV = [[ps3(f"psAV{i}_{h}", [128, 512]) for h in range(2)] for i in range(2)]

    ORDER = sorted(range(NSLOT), key=lambda j: -slot_kcap(j))
    NP2 = NSLOT // 2

    def pr_slots(k):
        return ORDER[2 * k], ORDER[2 * k + 1]

    KQ = []
    for k in range(NP2):
        jA, _ = pr_slots(k)
        for ci, (off, wdt) in enumerate(score_chunks(slot_kcap(jA))):
            if off >= 1024:
                KQ.append((k, ci, off, wdt))
    kq_pos = {(k, ci): q for q, (k, ci, _, _) in enumerate(KQ)}
    kdma, kcons = {}, {}
    kq_emit = [0]
    mxcols = [sb3(f"mxcols{i}", [128, 8], F32) for i in range(2)]

    KRESR[0] = k_res
    QTILER[0], QTILER[1] = qtile[0], qtile[1]
    MASKTR[0] = maskt
    assert _rng("k_res") == _rng("wq32")
    assert _rng("qtile0")[0] == _rng("wqb")[0] and _rng("maskt_s")[1] <= _rng("wqb")[1]

    kres_dma = kres_early
    t_mk = t_mk_early[0]

    kbuf_gate = {0: stats2_T[NB2 - 1] + qst_Y[2],
                 1: qst_Y[2] + qst_Y[3],
                 2: qst_Y[3] + ff_Y[14] + ff_Y[15]}

    def pump_k(upto):
        while kq_emit[0] < min(upto, len(KQ)):
            q = kq_emit[0]
            k, ci, off, wdt = KQ[q]
            g = off // 512
            buf = q % KRING
            w = [kbuf_gate[q]] if q < KRING else [kcons[q - KRING]]
            qname, qch = "Y", chY
            if wdt == 512:
                v = P.op(qname, lambda e, g=g, buf=buf: e.dma_start(
                    out=kbuf[buf][:, :], in_=kT_h[g, :, :]), waits=w, inc=qch)
            else:
                v = P.op(qname, lambda e, g=g, buf=buf: e.dma_start(
                    out=kbuf[buf][:, :].rearrange("p (a t) -> p a t", a=NCH)[:, :, 0:256],
                    in_=kT_h[g, :, :].rearrange("p (a t) -> p a t", a=NCH)[:, :, 0:256]),
                    waits=w, inc=qch)
            kdma[q] = (v, buf)
            kq_emit[0] += 1

    qt_dma, ff_dma, xr_dma = {}, {}, {}
    qt_cons, ff_cons, xr_cons = {}, {}, {}
    sc_mx_V, exp_S, inv_V, epi_V, mx_fin, exp_half = {}, {}, {}, {}, {}, {}
    trans_cons, av_T, out_G, sc_fin_T = {}, {}, {}, {}
    sc_prev = {}
    ptbg = {}

    def e_sdma_qt(k):
        for a in range(2):
            j = pr_slots(k)[a]
            si = 2 * k + a
            if si in qt_early:
                qt_dma[si] = qt_early[si]
                continue
            w0 = [qp3_T, qst_Y[j // 4]] if k < 2 else []
            qt_dma[si] = P.op("Y", lambda e, si=si, j=j: e.dma_start(
                out=qtile[si % 4][:, :],
                in_=qT_h[j // 4, :, :][:, (j % 4) * 1024:(j % 4 + 1) * 1024]),
                waits=w0 + ([qt_cons[si - 4]] if si >= 4 else []), inc=chY)

    def e_sdma_fx(k):
        w0 = [bar2] if k < 2 else []
        for a in range(2):
            j = pr_slots(k)[a]
            si = 2 * k + a
            ff_dma[si] = P.op("G", lambda e, si=si, j=j: e.dma_start(
                out=fftile[si % 3][:, :], in_=ff_h[j, :, :]),
                waits=w0 + ([ff_cons[si - 3]] if si >= 3 else []), inc=chG)
            xr_dma[si] = P.op("G", lambda e, si=si, j=j: e.dma_start(
                out=xrtile[si % 3][:, :], in_=d["xrows"][j * 128:(j + 1) * 128, :]),
                waits=w0 + ([xr_cons[si - 3]] if si >= 3 else []), inc=chG)

    def e_scores(k):
        jA, jB = pr_slots(k)
        kcapA, kcapB = slot_kcap(jA), slot_kcap(jB)
        chunks = score_chunks(kcapA)
        nchA = len(chunks)
        nchB = len(score_chunks(kcapB))
        kp = k % 2
        mx_prevA = mx_fin.get(2 * (k - 1))
        mx_prevB = mx_fin.get(2 * (k - 1) + 1)
        for ci, (off, wdt) in enumerate(chunks):
            if off < 1024:
                q = None
                v = kres_dma[0] if off < 512 else (kres_dma[0] + kres_dma[1])
                buf = -1
            else:
                q = kq_pos[(k, ci)]
                pump_k(q + 1)
                v, buf = kdma[q]
            for a, (jx, kcx, mx_prev) in enumerate(
                    [(jA, kcapA, mx_prevA), (jB, kcapB, mx_prevB)]):
                wdx = min(wdt, kcx - off)
                if wdx <= 0:
                    continue
                si = 2 * k + a
                for dc in range(NCH):
                    w = []
                    if dc == 0:
                        w.append(v)
                        prev = sc_prev.get(a)
                        if prev is not None:
                            w.append(prev)
                        if ci == 0:
                            w.append(qt_dma[si])
                            if k == 0:
                                w.append(bcastev2_V[NB2 - 1])
                    if buf < 0:
                        kap = k_res[:, :].rearrange("p (g a t) -> p g a t", g=2, a=NCH)[
                            :, off // 512, dc, 0:wdx]
                    else:
                        kap = kbuf[buf][:, :].rearrange("p (a t) -> p a t", a=NCH)[
                            :, dc, 0:wdx]
                    P.op("T", lambda e, si=si, a=a, dc=dc, wdx=wdx, kap=kap: e.matmul(
                        psS[a][:, 0:wdx],
                        qtile[si % 4][:, dc * 128:(dc + 1) * 128],
                        kap,
                        start=(dc == 0), stop=(dc == NCH - 1)), waits=w, inc=chT)
                if a == 1 and q is not None:
                    kcons[q] = chT.tok()
                wev = [chT.tok()]
                if ci == 0 and k >= 1:
                    wev.append(sc_mx_V.get(2 * (k - 1) + a))
                if ci == 0 and k >= 2:
                    wev.append(exp_S.get(2 * (k - 2) + a))
                nchx = nchA if a == 0 else nchB
                if ci < nchx - 1:
                    ev = P.op("V", lambda e, kp=kp, a=a, off=off, wdx=wdx: e.tensor_copy(
                        out=scores[kp][a][:, off:off + wdx], in_=psS[a][:, 0:wdx]),
                        waits=wev, inc=chV)
                    sc_prev[a] = ev
                    P.op("V", lambda e, kp=kp, a=a, ci=ci, off=off, wdx=wdx: e.reduce_max(
                        mxcols[a][:, ci:ci + 1], scores[kp][a][:, off:off + wdx], axis=AX.X),
                        waits=[ev, mx_prev], inc=chV)
                else:
                    evl = []
                    if wdx > 256:
                        evl.append(P.op("V", lambda e, kp=kp, a=a, off=off, wdx=wdx:
                                        e.tensor_copy(
                            out=scores[kp][a][:, off:off + wdx - 256],
                            in_=psS[a][:, 0:wdx - 256]), waits=wev, inc=chV))
                    evl.append(P.op("V", lambda e, kp=kp, a=a, kcx=kcx, wdx=wdx:
                                    e.tensor_tensor(
                        out=scores[kp][a][:, kcx - 256:kcx],
                        in0=psS[a][:, wdx - 256:wdx],
                        in1=maskt[:, :], op=ALU.add),
                        waits=wev + [t_mk, mx_prev], inc=chV))
                    ev = evl[-1]
                    sc_prev[a] = ev
                    P.op("V", lambda e, kp=kp, a=a, ci=ci, off=off, wdx=wdx: e.reduce_max(
                        mxcols[a][:, ci:ci + 1], scores[kp][a][:, off:off + wdx], axis=AX.X),
                        waits=evl + [mx_prev], inc=chV)
                if a == 0:
                    sc_mx_V[2 * k] = chV.tok()
                else:
                    sc_mx_V[2 * k + 1] = chV.tok()
            if kcapB - off <= 0 and q is not None:
                kcons[q] = chT.tok()
            if q is not None:
                pump_k(q + KRING + 1)
        sc_fin_T[k] = chT.tok()
        qt_cons[2 * k] = sc_fin_T[k]
        qt_cons[2 * k + 1] = sc_fin_T[k]

    def e_sm_pre(si):
        a = si % 2
        kcap = slot_kcap(ORDER[si])
        nch = len(score_chunks(kcap))
        v1 = P.op("V", lambda e, a=a, nch=nch: e.reduce_max(
            nbias[:, :], mxcols[a][:, 0:nch], axis=AX.X, negate=True),
            waits=[sc_mx_V[si], exp_S.get(si - 1)], inc=chV)
        mx_fin[si] = v1
        w = [v1, exp_S.get(si - 1)]
        if si >= 2:
            w.append(trans_cons.get(si - 2))
        half = (kcap // 256) * 128
        kp = (si // 2) % 2
        exp_half[si] = P.op("S", lambda e, kp=kp, a=a, half=half: e.activation(
            out=pbuf[a][:, 0:half], in_=scores[kp][a][:, 0:half], func=AF.Exp,
            bias=nbias[:, 0:1], scale=1.0, accum_out=lsum[:, 2 * a:2 * a + 1]), waits=w, inc=chS)
        exp_S[si] = P.op("S", lambda e, kp=kp, a=a, half=half, kcap=kcap: e.activation(
            out=pbuf[a][:, half:kcap], in_=scores[kp][a][:, half:kcap], func=AF.Exp,
            bias=nbias[:, 0:1], scale=1.0, accum_out=lsum[:, 2 * a + 1:2 * a + 2]), inc=chS)

    def e_sm_post(si):
        a = si % 2
        vsum = P.op("V", lambda e, a=a: e.tensor_tensor(
            out=lsum[:, 2 * a:2 * a + 1], in0=lsum[:, 2 * a:2 * a + 1],
            in1=lsum[:, 2 * a + 1:2 * a + 2], op=ALU.add),
            waits=[exp_S[si], inv_V.get(si - 1)], inc=chV)
        inv_V[si] = P.op("V", lambda e, si=si, a=a: e.reciprocal(
            out=invl[si % 2][:, :], in_=lsum[:, 2 * a:2 * a + 1]),
            waits=[vsum, epi_V.get(si - 2)], inc=chV)

    def e_trans_av(si):
        a = si % 2
        kcap = slot_kcap(ORDER[si])
        nk = kcap // 128
        ptb_ev = {}
        av_lastk = {}

        half_nk = (kcap // 256)
        def em_tr(kc):
            w = []
            if kc == 0:
                w = [exp_half[si], t_idb]
                if si == 0:
                    w.append(qst_S[NB2 - 1])
            elif kc == half_nk:
                w = [exp_S[si]]
            tt = P.op("T", lambda e, a=a, kc=kc: e.transpose(
                psT_[kc % 2][:, :], pbuf[a][:, kc * 128:(kc + 1) * 128], identb[:, :]),
                waits=w, inc=chT)
            wv = [tt, ptbg.get(kc % 3)]
            if kc >= 3:
                wv.append(av_lastk[kc - 3])
            ptb_ev[kc] = P.op("S", lambda e, kc=kc: e.copy(
                out=ptb[kc % 3][:, :], in_=psT_[kc % 2][:, :]), waits=wv, inc=chS)
            ptbg[kc % 3] = ptb_ev[kc]

        def em_av(kc):
            for h in range(2):
                w2 = [ptb_ev[kc]]
                if kc == 0 and h == 0 and si == 0:
                    w2.append(ffev[4 * NB2 - 1])
                if kc == 0 and h == 0 and si >= 2:
                    w2.append(epi_V.get(si - 2))
                av_lastk[kc] = P.op("T", lambda e, si=si, kc=kc, h=h, nk=nk: e.matmul(
                    psAV[si % 2][h][:, :], ptb[kc % 3][:, :],
                    v_res[:, kc * C + h * 512: kc * C + (h + 1) * 512],
                    start=(kc == 0), stop=(kc == nk - 1), skip_group_check=True),
                    waits=w2, inc=chT)

        em_tr(0)
        for kc in range(1, nk):
            em_tr(kc)
            em_av(kc - 1)
        em_av(nk - 1)
        trans_cons[si] = chT.tok()
        av_T[si] = chT.tok()

    def e_epilogue(si):
        j = ORDER[si]
        w = [av_T[si], inv_V[si], ff_dma[si], xr_dma[si]]
        if si >= 2:
            w.append(out_G[si - 2])
        P.op("V", lambda e, si=si: e.scalar_tensor_tensor(
            out=outsb[si % 2][:, 0:512], in0=psAV[si % 2][0][:, :], scalar=invl[si % 2][:, 0:1],
            in1=fftile[si % 3][:, 0:512], op0=ALU.mult, op1=ALU.add), waits=w, inc=chV)
        v2 = P.op("V", lambda e, si=si: e.scalar_tensor_tensor(
            out=outsb[si % 2][:, 512:1024], in0=psAV[si % 2][1][:, :], scalar=invl[si % 2][:, 0:1],
            in1=fftile[si % 3][:, 512:1024], op0=ALU.mult, op1=ALU.add), inc=chV)
        ff_cons[si] = v2
        v3 = P.op("V", lambda e, si=si: e.tensor_tensor(
            out=outsb[si % 2][:, :], in0=outsb[si % 2][:, :], in1=xrtile[si % 3][:, :],
            op=ALU.add), waits=[v2], inc=chV)
        xr_cons[si] = v3
        epi_V[si] = v3
        out_G[si] = P.op("G", lambda e, j=j, si=si: e.dma_start(
            out=out_h[j * 128:(j + 1) * 128, :], in_=outsb[si % 2][:, :]),
            waits=[v3], inc=chG)

    e_sdma_qt(0)
    e_sdma_fx(0)
    e_scores(0)
    for k in range(NP2):
        e_sm_pre(2 * k)
        e_sm_pre(2 * k + 1)
        if k + 1 < NP2:
            e_sdma_qt(k + 1)
            e_scores(k + 1)
        e_sm_post(2 * k)
        e_sm_post(2 * k + 1)
        e_trans_av(2 * k)
        e_epilogue(2 * k)
        if k + 1 < NP2:
            e_sdma_fx(k + 1)
        e_trans_av(2 * k + 1)
        e_epilogue(2 * k + 1)

    fin = chG.snap() + chY.snap() + chQ.snap()
    P.op("G", lambda e: e.nop(), waits=[fin])
    p2b.close()

    sems = {}
    for c in chans:
        for n in c.sem_names():
            sems[n] = es.enter_context(nc.semaphore(n))
    block = es.enter_context(nc.Block())

    @block.tensor
    def _(t):
        P.replay("T", t, sems)

    @block.vector
    def _(v):
        P.replay("V", v, sems)

    @block.scalar
    def _(s):
        P.replay("S", s, sems)

    @block.sync
    def _(sy):
        P.replay("Y", sy, sems)

    @block.gpsimd
    def _(g):
        P.replay("G", g, sems)

    es.close()
    return nc



_NC_CACHE = {}


def _get_nc():
    if "nc" not in _NC_CACHE:
        _NC_CACHE["nc"] = build_nc()
    return _NC_CACHE["nc"]


def kernel(L0_attn_x, L0_attn_Wq, L0_attn_Wk, L0_attn_Wv,
           L0_ffn_kernel, L0_ffn_bias, L0_ffn_a, L0_ffn_b):
    x = np.ascontiguousarray(np.asarray(L0_attn_x), dtype=np.float32)
    Wq = np.asarray(L0_attn_Wq, dtype=np.float32)
    Wk = np.asarray(L0_attn_Wk, dtype=np.float32)
    Wv = np.asarray(L0_attn_Wv, dtype=np.float32)
    Wker = np.ascontiguousarray(np.asarray(L0_ffn_kernel), dtype=np.float32)
    bias = np.asarray(L0_ffn_bias, dtype=np.float32)
    a_m = np.asarray(L0_ffn_a, dtype=np.float32)
    b_m = np.ascontiguousarray(np.asarray(L0_ffn_b), dtype=np.float32)

    wqT = np.ascontiguousarray(Wq.T)
    wkT = np.ascontiguousarray(Wk.T)
    wvT = np.ascontiguousarray(Wv.T)
    aT = np.ascontiguousarray(a_m.T)
    biasb = np.ascontiguousarray(np.broadcast_to(bias[None, :], (128, C)))
    identb = np.eye(128, dtype=ml_dtypes.bfloat16)
    identf = np.eye(128, dtype=np.float32)
    onesrow = np.ones((1, 128), np.float32)
    onescol = np.ones((128, 1), np.float32)

    tri = np.tril(np.ones((128, 128), np.float32))
    diag_m = np.where(tri > 0, 0.0, NEG).astype(np.float32)
    full_m = np.full((128, 128), NEG, np.float32)
    zero_m = np.zeros((128, 128), np.float32)
    mask_par = [np.ascontiguousarray(np.concatenate([diag_m, full_m], axis=1)),
                np.ascontiguousarray(np.concatenate([zero_m, diag_m], axis=1))]

    nc = _get_nc()
    in_maps, rows_per_core = [], []
    for core in range(8):
        b, par = core // 2, core % 2
        tiles = [slot_tile(j, par) for j in range(NSLOT)]
        own = np.concatenate([np.arange(128 * t, 128 * (t + 1)) for t in tiles])
        rows_per_core.append(own)
        xseq = x[b]
        in_maps.append({
            "xfm": np.ascontiguousarray(xseq.T),
            "xcols": np.ascontiguousarray(xseq[own].T),
            "xrows": np.ascontiguousarray(xseq[own]),
            "wqT": wqT, "wkT": wkT, "wvT": wvT,
            "wker": Wker, "aT": aT, "bmat": b_m,
            "biasb": biasb, "maskt": mask_par[par],
            "identb": identb, "identf": identf,
            "onesrow": onesrow, "onescol": onescol,
            "invCrow": np.full((1, 128), 1.0 / C, np.float32),
        })

    res = run_bass_kernel_spmd(nc, in_maps, core_ids=list(range(8)))
    out = np.empty((4, T_TOK, C), np.float32)
    for core in range(8):
        out[core // 2, rows_per_core[core]] = res.results[core]["out"]
    return out


# version 28 (final)
# speedup vs baseline: 1.2028x; 1.0104x over previous
import numpy as np
import ml_dtypes
from contextlib import ExitStack

import concourse.bass as bass
import concourse.mybir as mybir
from concourse.bass_utils import run_bass_kernel_spmd

F32 = mybir.dt.float32
BF16 = mybir.dt.bfloat16
AF = mybir.ActivationFunctionType
ALU = mybir.AluOpType
AX = mybir.AxisListType

T_TOK = 4096
C = 1024
NCH = 8
G1 = 512
NG1 = T_TOK // G1
NB2 = 4
NSLOT = 16
NEG = -1.0e30
EPS = 1e-5
SCALE = 1.0 / 32.0
KRING = 3


def slot_kcap(j):
    p, hi = j // 2, j % 2
    return 256 * (16 - p) if hi else 256 * (p + 1)


def slot_tile(j, parity):
    p, hi = j // 2, j % 2
    return (30 - 2 * p + parity) if hi else (2 * p + parity)


def score_chunks(kcap):
    out, o = [], 0
    while o < kcap:
        w = min(512, kcap - o)
        out.append((o, w))
        o += w
    return out


class Ch:

    def __init__(self, name):
        self.name, self.v = name, 0

    def tok(self):
        return [(self.name, self.v)]

    def sem_names(self):
        return [self.name]


class DCh:

    def __init__(self, name, r):
        self.name, self.r = name, r
        self.c = [0] * r
        self.seq = 0

    def snap(self):
        return [(f"{self.name}{i}", self.c[i]) for i in range(self.r) if self.c[i] > 0]

    def sem_names(self):
        return [f"{self.name}{i}" for i in range(self.r)]


class Plan:
    ENGS = ("T", "V", "S", "Y", "G")

    def __init__(self):
        self.items = {e: [] for e in self.ENGS}

    def op(self, eng, fn, waits=(), inc=None):
        ws = []
        for t in waits:
            if t:
                ws.extend(t)
        tok, semname, amt = None, None, 0
        if isinstance(inc, DCh):
            idx = inc.seq % inc.r
            inc.seq += 1
            semname = f"{inc.name}{idx}"
            if inc.c[idx] > 0:
                ws.append((semname, inc.c[idx]))
            inc.c[idx] += 16
            amt = 16
            tok = [(semname, inc.c[idx])]
        elif isinstance(inc, Ch):
            inc.v += 1
            semname, amt = inc.name, 1
            tok = [(inc.name, inc.v)]
        ws = tuple((n, v) for (n, v) in ws if v and v > 0)
        self.items[eng].append((ws, fn, semname, amt))
        return tok

    def replay(self, eng, engobj, sems):
        for ws, fn, semname, amt in self.items[eng]:
            seen = set()
            for n, v in ws:
                key = n
                if key in seen:
                    v = max(v, dict(ws).get(n, v))
                    continue
                seen.add(key)
                vv = max(val for (nn, val) in ws if nn == n)
                engobj.wait_ge(sems[n], vv)
            ins = fn(engobj)
            if semname is not None:
                ins.then_inc(sems[semname], amt)



_DBG = {}


def build_nc():
    nc = bass.Bass("TRN2", target_bir_lowering=False)
    es = ExitStack()

    d = {}
    for nm, shp, dt in [
        ("xfm", [C, T_TOK], F32), ("xcols", [C, 2048], F32), ("xrows", [2048, C], F32),
        ("wqT", [C, C], F32), ("wkT", [C, C], F32), ("wvT", [C, C], F32),
        ("wker", [C, C], F32), ("aT", [16, C], F32), ("bmat", [16, C], F32),
        ("biasb", [128, C], F32), ("maskt", [128, 256], F32),
        ("identb", [128, 128], BF16), ("identf", [128, 128], F32),
        ("onesrow", [1, 128], F32), ("onescol", [128, 1], F32),
        ("invCrow", [1, 128], F32),
    ]:
        d[nm] = nc.dram_tensor(nm, shp, dt, kind="ExternalInput").ap()
    out_h = nc.dram_tensor("out", [2048, C], F32, kind="ExternalOutput").ap()
    kT_h = nc.dram_tensor("kT_h", [NG1, 128, NCH * G1], BF16, kind="Internal").ap()
    qT_h = nc.dram_tensor("qT_h", [NB2, 128, NCH * G1], BF16, kind="Internal").ap()
    ff_h = nc.dram_tensor("ff_h", [NSLOT, 128, C], F32, kind="Internal").ap()
    scr_h = nc.dram_tensor("scr_h", [128, 1], F32, kind="Internal").ap()
    scrb_h = nc.dram_tensor("scrb_h", [1, 4], F32, kind="Internal").ap()

    chT, chV, chS, chP = Ch("sT"), Ch("sV"), Ch("sS"), Ch("sP")
    chY, chG, chQ = DCh("sY", 8), DCh("sG", 8), DCh("sQ", 8)
    chans = [chT, chV, chS, chP, chY, chG, chQ]
    P = Plan()

    sb_off = {}

    def _rec(n, scope, fn):
        b0 = nc.sbuf_base
        h = fn()
        sb_off[n] = (b0, nc.sbuf_base, scope)
        return h

    def sb(n, s, t):
        return _rec(n, "es", lambda: es.enter_context(nc.sbuf_tensor(n, s, t)))

    v_res = sb("v_res", [128, 32 * C], BF16)
    identb = sb("identb_s", [128, 128], BF16)
    identf = sb("identf_s", [128, 128], F32)
    onesrow = sb("onesrow_s", [1, 128], F32)
    invCrow = sb("invCrow_s", [1, 128], F32)
    onescol = sb("onescol_s", [128, 1], BF16)
    aTs = sb("aT_s", [16, C], BF16)
    bms = sb("bm_s", [16, C], BF16)
    scol = sb("scol", [128, 1], F32)
    iscol = sb("iscol", [128, 1], F32)
    n8scol = sb("n8scol", [128, 1], F32)
    pw = ExitStack()
    weff = pw.enter_context(nc.sbuf_tensor("weff", [128, NCH * C], BF16, side="right"))

    t_idb = P.op("Y", lambda e: e.dma_start(out=identb[:, :], in_=d["identb"][:, :]), inc=chY)
    t_idf = P.op("Y", lambda e: e.dma_start(out=identf[:, :], in_=d["identf"][:, :]), inc=chY)
    t_or = P.op("Y", lambda e: e.dma_start(out=onesrow[:, :], in_=d["onesrow"][:, :]), inc=chY)
    t_ic = P.op("Y", lambda e: e.dma_start(out=invCrow[:, :], in_=d["invCrow"][:, :]), inc=chY)
    t_oc = P.op("G", lambda e: e.dma_start(out=onescol[:, :], in_=d["onescol"][:, :]), inc=chG)
    t_aT = P.op("G", lambda e: e.dma_start(out=aTs[:, :], in_=d["aT"][:, :]), inc=chG)
    t_bm = P.op("G", lambda e: e.dma_start(out=bms[:, :], in_=d["bmat"][:, :]), inc=chG)

    p1 = ExitStack()

    def sb1(n, s, t):
        return _rec(n, "p1", lambda: p1.enter_context(nc.sbuf_tensor(n, s, t)))

    wkb = sb1("wkb", [128, NCH * C], BF16)
    wvb = sb1("wvb", [128, NCH * C], BF16)
    xg = [sb1(f"xg{i}", [128, NCH * G1], BF16) for i in range(3)]
    kst01 = sb1("kst01", [128, 2 * NCH * G1], BF16)
    kstage = [kst01[:, i * NCH * G1:(i + 1) * NCH * G1] for i in range(2)]
    x32g0 = kst01.bitcast(F32)
    sqb = [sb1(f"sq{i}", [128, G1], BF16) for i in range(2)]
    strow = sb1("strow", [1, 4 * G1], F32)
    mbc_s = sb1("mbc_s", [128, G1], F32)
    rbc_s = sb1("rbc_s", [128, G1], F32)
    xntmp = sb1("xntmp", [128, G1], F32)
    xntmp_b = sb1("xntmp_b", [128, G1], F32)
    xnt = [xntmp, xntmp_b]
    w32 = [sb1(f"w32_{i}", [128, C], F32) for i in range(2)]
    lad_u = sb1("lad_u", [128, G1], F32)
    lad_a = sb1("lad_a", [128, G1], F32)
    lad_b = sb1("lad_b", [128, G1], F32)
    ab_sb = sb1("ab_sb", [128, NCH * C], BF16)
    gcol = sb1("gcol", [128, 16], F32)
    grow = sb1("grow", [1, 256], F32)
    _DBG["p1_headroom"] = nc.sbuf_top - nc.sbuf_base

    xg0_Y = []
    for h in range(2):
        xg0_Y.append(P.op("Y", lambda e, h=h: e.dma_start(
            out=x32g0[:, h * 4 * G1:(h + 1) * 4 * G1].rearrange("p (a t) -> p a t", a=4),
            in_=d["xfm"].rearrange("(a p) t -> p a t", p=128)[:, 4 * h:4 * (h + 1), 0:G1]),
            inc=chY))
    cast_tok = {}
    for cc in range(NCH):
        if cc % 2 == 0:
            cast_tok[cc] = P.op("V", lambda e, cc=cc: e.tensor_copy(
                out=xg[0][:, cc * G1:(cc + 1) * G1], in_=x32g0[:, cc * G1:(cc + 1) * G1]),
                waits=[xg0_Y[cc // 4]], inc=chV)
        else:
            cast_tok[cc] = P.op("S", lambda e, cc=cc: e.copy(
                out=xg[0][:, cc * G1:(cc + 1) * G1], in_=x32g0[:, cc * G1:(cc + 1) * G1]),
                waits=[xg0_Y[cc // 4]], inc=chS)

    wk_Y, wk_cast = {}, {}
    wk_Y[0] = P.op("Y", lambda e: e.dma_start(
        out=x32g0[:, :].rearrange("p (a t) -> p a t", a=NCH),
        in_=d["wkT"].rearrange("(a p) x -> p a x", p=128)[:, :, 0:512]),
        waits=[cast_tok[cc] for cc in range(NCH)], inc=chY)

    def e_wk_tail():
        for h in range(2):
            if h == 1:
                wk_Y[1] = P.op("Y", lambda e: e.dma_start(
                    out=x32g0[:, :].rearrange("p (a t) -> p a t", a=NCH),
                    in_=d["wkT"].rearrange("(a p) x -> p a x", p=128)[:, :, 512:1024]),
                    waits=[wk_cast[0]], inc=chY)
            wk_cast[h] = P.op("S", lambda e, h=h: e.copy(
                out=wkb[:, :].rearrange("p (a x) -> p a x", a=NCH)[:, :, h * 512:(h + 1) * 512],
                in_=x32g0[:, :].rearrange("p (a t) -> p a t", a=NCH)),
                waits=[wk_Y[h]], inc=chS)

    wv32 = ab_sb.bitcast(F32)
    wv_cast = {}
    wv_Y = {}
    wv_Y[0] = P.op("Y", lambda e: e.dma_start(
        out=wv32[:, :].rearrange("p (a t) -> p a t", a=NCH),
        in_=d["wvT"].rearrange("(a p) x -> p a x", p=128)[:, :, 0:512]), inc=chY)

    def e_wv_tail():
        for h in range(2):
            if h == 1:
                wv_Y[1] = P.op("Y", lambda e: e.dma_start(
                    out=wv32[:, :].rearrange("p (a t) -> p a t", a=NCH),
                    in_=d["wvT"].rearrange("(a p) x -> p a x", p=128)[:, :, 512:1024]),
                    waits=wv_cast[0], inc=chY)
            toks = []
            for q in range(2):
                dst = wvb[:, :].rearrange("p (a x) -> p a x", a=NCH)[
                    :, 4 * q:4 * (q + 1), h * 512:(h + 1) * 512]
                srcv = wv32[:, :].rearrange("p (a t) -> p a t", a=NCH)[:, 4 * q:4 * (q + 1), :]
                toks.append(P.op("V", lambda e, dst=dst, srcv=srcv: e.tensor_copy(
                    out=dst, in_=srcv), waits=[wv_Y[h]], inc=chV))
            wv_cast[h] = toks
        for i in range(6):
            w = [] if i < 2 else list(wv_cast[1])
            qa_y[i] = P.op("Y", lambda e, i=i: e.dma_start(
                out=scan_buf(i), in_=d["wker"][i * 128:(i + 1) * 128, :]),
                waits=w, inc=chY)
        return wv_cast[0] + wv_cast[1]

    def scan_buf(i):
        if i in (0, 1, 6, 7):
            return w32[i % 2][:, :]
        return wv32[:, (i - 2) * C:(i - 1) * C]

    qa_y = {}
    qmax_done = {}
    y_bc = []
    v_ab_tok = {}

    def ps1(n, s, t=F32):
        return p1.enter_context(nc.psum_tensor(n, s, t))

    T0 = ps1("T0", [128, G1])
    T1 = ps1("T1", [128, G1])
    psK = [ps1(f"psK{i}", [128, G1]) for i in range(2)]
    psV = [[ps1(f"psV{i}_{h}", [128, 512]) for h in range(2)] for i in range(2)]
    s1r, s2r, tr1, tr2 = (strow[0:1, i * G1:(i + 1) * G1] for i in range(4))

    def e_scan():
        for i in range(NCH):
            if i >= 6:
                qa_y[i] = P.op("Y", lambda e, i=i: e.dma_start(
                    out=scan_buf(i), in_=d["wker"][i * 128:(i + 1) * 128, :]),
                    waits=[qmax_done[i - 6]], inc=chY)
            qmax_done[i] = P.op("V", lambda e, i=i: e.reduce_max(
                gcol[:, i:i + 1], scan_buf(i), axis=AX.X, apply_absolute_value=True),
                waits=[qa_y[i]], inc=chV)
        v_am = P.op("V", lambda e: e.reduce_max(gcol[:, 8:9], gcol[:, 0:8], axis=AX.X),
                    waits=[qmax_done[NCH - 1]], inc=chV)
        y_sc1 = P.op("Y", lambda e: e.dma_start(out=scr_h[:, :], in_=gcol[:, 8:9]),
                     waits=[v_am], inc=chY)
        y_sc2 = P.op("Y", lambda e: e.dma_start(
            out=grow[0:1, 0:128], in_=scr_h.rearrange("p one -> one p")),
            waits=[y_sc1], inc=chY)
        v_gm = P.op("V", lambda e: e.reduce_max(grow[0:1, 128:129], grow[0:1, 0:128],
                                                axis=AX.X), waits=[y_sc2], inc=chV)
        v_s = P.op("V", lambda e: e.tensor_scalar(out=grow[0:1, 129:130], in0=grow[0:1, 128:129],
                                                  scalar1=1.0 / 7.0, scalar2=None, op0=ALU.mult),
                   waits=[v_gm], inc=chV)
        v_rc = P.op("V", lambda e: e.reciprocal(out=grow[0:1, 130:131], in_=grow[0:1, 128:129]),
                    waits=[v_gm], inc=chV)
        P.op("V", lambda e: e.tensor_scalar(out=grow[0:1, 131:132], in0=grow[0:1, 130:131],
                                            scalar1=7.0, scalar2=None, op0=ALU.mult),
             waits=[v_rc], inc=chV)
        v_n8 = P.op("V", lambda e: e.tensor_scalar(out=grow[0:1, 132:133], in0=grow[0:1, 129:130],
                                                   scalar1=-8.0, scalar2=None, op0=ALU.mult),
                    waits=[v_s], inc=chV)
        y_sc3 = P.op("Y", lambda e: e.dma_start(out=scrb_h[0:1, :], in_=grow[0:1, 129:133]),
                     waits=[v_n8], inc=chY)
        for cidx, dst in ((0, scol), (2, iscol), (3, n8scol)):
            y_bc.append(P.op("Y", lambda e, cidx=cidx, dst=dst: e.dma_start(
                out=dst[:, :], in_=scrb_h[0:1, cidx:cidx + 1].broadcast_to([128, 1])),
                waits=[y_sc3], inc=chY))

    def e_ab_iter(i):
        w1 = [t_aT, t_bm]
        if i == 0:
            w1.append(vmm_S[1])
        if i - 1 in v_ab_tok:
            w1.append(v_ab_tok[i - 1])
        t1 = P.op("T", lambda e, i=i: e.matmul(
            psV[0][0][:, :], aTs[:, i * 128:(i + 1) * 128], bms[:, 0:512],
            start=True, stop=True), waits=w1, inc=chT)
        t2 = P.op("T", lambda e, i=i: e.matmul(
            psV[0][1][:, :], aTs[:, i * 128:(i + 1) * 128], bms[:, 512:1024],
            start=True, stop=True), inc=chT)
        wv_ = [t2, y_bc[2], qmax_done[min(2 + i // 2, 5)]] + wv_cast[1]
        P.op("V", lambda e, i=i: e.tensor_scalar(
            out=ab_sb[:, i * C:i * C + 512], in0=psV[0][0][:, :],
            scalar1=n8scol[:, 0:1], scalar2=None, op0=ALU.add), waits=wv_, inc=chV)
        v_ab_tok[i] = P.op("V", lambda e, i=i: e.tensor_scalar(
            out=ab_sb[:, i * C + 512:(i + 1) * C], in0=psV[0][1][:, :],
            scalar1=n8scol[:, 0:1], scalar2=None, op0=ALU.add), inc=chV)

    xg_dma, stats_T, math_V, bcast_T, bcastev_V, xn_V = {}, {}, {}, {}, {}, {}
    math_S, xn_cc = {}, {}
    vmm_T, kst_Y, kst_S, vmm_S = {}, {}, {}, {}
    sq_T, sq_S = {}, {}
    quant_V, qw_Y = {}, {}

    def e_xg(g):
        w = [xg_dma_gate.get(g)]
        xg_dma[g] = P.op("G", lambda e, g=g: e.dma_start(
            out=xg[g % 3][:, :].rearrange("p (a t) -> p a t", a=NCH),
            in_=d["xfm"].rearrange("(a p) t -> p a t", p=128)[:, :, g * G1:(g + 1) * G1]),
            waits=w, inc=chG)

    xg_dma_gate = {}

    def e_stats(g, bankfree):
        for cc in range(NCH):
            wsq = [cast_tok[cc]] if g == 0 else [xg_dma[g]]
            key = (g, cc - 2) if cc >= 2 else (g - 1, cc + 6)
            if key in sq_T:
                wsq += [sq_T[key], sq_S[key]]
            sq_S[(g, cc)] = P.op("S", lambda e, g=g, cc=cc: e.activation(
                out=sqb[cc % 2][:, :], in_=xg[g % 3][:, cc * G1:(cc + 1) * G1],
                func=AF.Square), waits=wsq, inc=chS)
            w = ([cast_tok[cc]] if g == 0 else [xg_dma[g]]) + [t_oc]
            if cc == 0:
                w.append(bankfree)
            P.op("T", lambda e, g=g, cc=cc: e.matmul(
                T0[0:1, :], onescol[:, 0:1], xg[g % 3][:, cc * G1:(cc + 1) * G1],
                start=(cc == 0), stop=(cc == NCH - 1), skip_group_check=True),
                waits=w, inc=chT)
            sq_T[(g, cc)] = P.op("T", lambda e, g=g, cc=cc: e.matmul(
                T1[0:1, :], onescol[:, 0:1], sqb[cc % 2][:, :],
                start=(cc == 0), stop=(cc == NCH - 1), skip_group_check=True),
                waits=[sq_S[(g, cc)]], inc=chT)
        stats_T[g] = chT.tok()

    def e_math(g, rows, prev=()):
        s1, s2, t1_, t2_ = rows
        v0 = P.op("V", lambda e, s1=s1: e.tensor_copy(out=s1, in_=T0[0:1, :]),
                  waits=[stats_T[g]] + list(prev), inc=chV)
        v1 = P.op("V", lambda e, s2=s2: e.tensor_copy(out=s2, in_=T1[0:1, :]), inc=chV)
        v2 = P.op("V", lambda e, t1_=t1_, s1=s1: e.tensor_tensor(
            out=t1_, in0=s1, in1=s1, op=ALU.mult), waits=[v0], inc=chV)
        v3 = P.op("V", lambda e, t1_=t1_, s2=s2: e.scalar_tensor_tensor(
            out=t1_, in0=s2, scalar=float(C), in1=t1_, op0=ALU.mult, op1=ALU.subtract),
            waits=[v1, v2], inc=chV)
        v4 = P.op("V", lambda e, t1_=t1_: e.tensor_scalar(
            out=t1_, in0=t1_, scalar1=1.0 / (C * C), scalar2=EPS,
            op0=ALU.mult, op1=ALU.add), waits=[v3], inc=chV)
        s0 = P.op("S", lambda e, t1_=t1_: e.activation(out=t1_, in_=t1_, func=AF.Ln),
                  waits=[v4], inc=chS)
        s1x = P.op("S", lambda e, t2_=t2_, t1_=t1_: e.activation(
            out=t2_, in_=t1_, func=AF.Exp, scale=-0.5), waits=[s0], inc=chS)
        math_S[g] = s1x
        return P.op("V", lambda e: e.engine_nop(), waits=[s1x], inc=chV)

    def e_bcast(g):
        P.op("T", lambda e: e.matmul(T0[:, :], invCrow[0:1, :], s1r,
                                     start=True, stop=True),
             waits=[math_V[g], stats_T[g], t_ic, t_or], inc=chT)
        bcast_T[g] = P.op("T", lambda e: e.matmul(T1[:, :], onesrow[0:1, :], tr2,
                                                  start=True, stop=True), inc=chT)

    def e_xn(g):
        P.op("V", lambda e: e.tensor_copy(out=mbc_s[:, :], in_=T0[:, :]),
             waits=[bcast_T[g]], inc=chV)
        ev = P.op("V", lambda e: e.tensor_copy(out=rbc_s[:, :], in_=T1[:, :]),
                  inc=chV)
        bcastev_V[g] = ev
        mul_t = {}
        for c0 in range(0, NCH, 2):
            subs = []
            for cc in (c0, c0 + 1):
                w = [ev] if cc < 2 else [mul_t[cc - 2]]
                if g == 0:
                    w.append(cast_tok[cc])
                subs.append(P.op("V", lambda e, g=g, cc=cc: e.tensor_tensor(
                    out=xnt[cc % 2][:, :], in0=xg[g % 3][:, cc * G1:(cc + 1) * G1],
                    in1=mbc_s[:, :], op=ALU.subtract), waits=w, inc=chV))
            for k, cc in enumerate((c0, c0 + 1)):
                mul_t[cc] = P.op("V", lambda e, g=g, cc=cc: e.tensor_tensor(
                    out=xg[g % 3][:, cc * G1:(cc + 1) * G1], in0=xnt[cc % 2][:, :],
                    in1=rbc_s[:, :], op=ALU.mult), waits=[subs[k]], inc=chV)
        xn_V[g] = mul_t[NCH - 1]
        xn_cc[g] = dict(mul_t)

    def e_kproj(g, extra=None):
        ev = {}
        for dc in range(NCH):
            for cc in range(NCH):
                w = [xn_cc[g][cc]] if dc == 0 else []
                if cc == 0:
                    if dc == 0:
                        w.append(wk_cast[0])
                        if g >= 2:
                            w.append(kst_Y[g - 2])
                    if dc == 4:
                        w.append(wk_cast[1])
                    if dc >= 2:
                        w.append(ev[dc - 2])
                    elif g >= 1:
                        w.append(kst_S[g - 1])
                P.op("T", lambda e, g=g, dc=dc, cc=cc: e.matmul(
                    psK[dc % 2][:, :],
                    wkb[:, :].rearrange("p (a x) -> p a x", a=NCH)[:, cc, dc * 128:(dc + 1) * 128],
                    xg[g % 3][:, cc * G1:(cc + 1) * G1],
                    start=(cc == 0), stop=(cc == NCH - 1)), waits=w, inc=chT)
            tmm = chT.tok()
            wk_ev = [tmm] + ([kst_S[g - 2]] if dc == 0 and g >= 2 else [])
            if g <= 1 and dc == 0:
                wk_ev.append(wk_cast[1])
            ev[dc] = P.op("S", lambda e, g=g, dc=dc: e.copy(
                out=kstage[g % 2][:, dc * G1:(dc + 1) * G1], in_=psK[dc % 2][:, :]),
                waits=wk_ev, inc=chS)
            if extra is not None:
                extra(dc)
        kst_S[g] = ev[NCH - 1]
        kst_Y[g] = P.op("Y", lambda e, g=g: e.dma_start(
            out=kT_h[g, :, :], in_=kstage[g % 2][:, :]),
            waits=[ev[NCH - 1]], inc=chY)

    def e_vproj(g):
        evs = {}
        for tt in range(4):
            for h in range(2):
                for cc in range(NCH):
                    w = [xn_cc[g][cc]] if (tt == 0 and h == 0) else []
                    if cc == 0 and h == 0:
                        if tt >= 2:
                            w.append(evs[tt - 2])
                        elif g >= 1:
                            w.append(vmm_S[g - 1])
                        if tt == 0:
                            w += wv_done
                            if g == 2:
                                w.append(v_ab_tok[NCH - 1])
                    P.op("T", lambda e, g=g, tt=tt, h=h, cc=cc: e.matmul(
                        psV[tt % 2][h][:, :],
                        xg[g % 3][:, cc * G1 + tt * 128: cc * G1 + (tt + 1) * 128],
                        wvb[:, :].rearrange("p (a x) -> p a x", a=NCH)[:, cc, h * 512:(h + 1) * 512],
                        start=(cc == 0), stop=(cc == NCH - 1)), waits=w, inc=chT)
            tmm = chT.tok()
            base = (4 * g + tt) * C
            P.op("S", lambda e, tt=tt, base=base: e.copy(
                out=v_res[:, base:base + 512], in_=psV[tt % 2][0][:, :]),
                waits=[tmm], inc=chS)
            evs[tt] = P.op("S", lambda e, tt=tt, base=base: e.copy(
                out=v_res[:, base + 512:base + 1024], in_=psV[tt % 2][1][:, :]), inc=chS)
        vmm_T[g] = chT.tok()
        vmm_S[g] = evs[3]

    def e_qdma(i):
        w = [quant_V[i - 2]] if i >= 2 else [qmax_done[i + 6]]
        qw_Y[i] = P.op("Y", lambda e, i=i: e.dma_start(
            out=w32[i % 2][:, :], in_=d["wker"][i * 128:(i + 1) * 128, :]),
            waits=w, inc=chY)

    def e_quant(i):
        vq = chV.tok()
        for hh in range(2):
            w = [qw_Y[i], vq]
            if i == 0 and hh == 0:
                w += [v_ab_tok[NCH - 1], y_bc[0], y_bc[1]]
            vq = P.op("V", lambda e, i=i, hh=hh: e.tensor_scalar(
                out=lad_u[:, :], in0=w32[i % 2][:, hh * 512:(hh + 1) * 512],
                scalar1=iscol[:, 0:1], scalar2=8.5, op0=ALU.mult, op1=ALU.add),
                waits=w, inc=chV)
            vq = P.op("V", lambda e: e.tensor_scalar(
                out=lad_a[:, :], in0=lad_u[:, :], scalar1=8.0, scalar2=8.0,
                op0=ALU.is_ge, op1=ALU.mult), waits=[vq], inc=chV)
            for st in (4.0, 2.0, 1.0):
                vq = P.op("V", lambda e: e.tensor_tensor(out=lad_b[:, :], in0=lad_u[:, :],
                                                         in1=lad_a[:, :], op=ALU.subtract),
                          waits=[vq], inc=chV)
                vq = P.op("V", lambda e, st=st: e.tensor_scalar(
                    out=lad_b[:, :], in0=lad_b[:, :], scalar1=st, scalar2=st,
                    op0=ALU.is_ge, op1=ALU.mult), waits=[vq], inc=chV)
                vq = P.op("V", lambda e: e.tensor_tensor(out=lad_a[:, :], in0=lad_a[:, :],
                                                         in1=lad_b[:, :], op=ALU.add),
                          waits=[vq], inc=chV)
            vq = P.op("V", lambda e, i=i, hh=hh: e.scalar_tensor_tensor(
                out=weff[:, i * C + hh * 512: i * C + (hh + 1) * 512], in0=lad_a[:, :],
                scalar=scol[:, 0:1], in1=ab_sb[:, i * C + hh * 512: i * C + (hh + 1) * 512],
                op0=ALU.mult, op1=ALU.add), waits=[vq], inc=chV)
        quant_V[i] = vq

    e_stats(0, None)
    math_V[0] = e_math(0, (s1r, s2r, tr1, tr2))
    e_wk_tail()
    e_bcast(0); e_xn(0)
    e_xg(1)
    WQ32R = [None, None]
    wq_h0_Y = [None]
    wq_h1_Y = [None]
    kp7tok = [None]

    def wq_hook(dc):
        if dc == NCH - 1:
            kp7tok[0] = chT.tok()
            wq_h0_Y[0] = P.op("Y", lambda e: e.dma_start(
                out=WQ32R[0][:, :].rearrange("p (a t) -> p a t", a=NCH),
                in_=d["wqT"].rearrange("(a p) x -> p a x", p=128)[:, :, 0:512]),
                waits=[kp7tok[0]], inc=chY)
            wq_h1_Y[0] = P.op("Y", lambda e: e.dma_start(
                out=WQ32R[1][:, :].rearrange("p (a t) -> p a t", a=NCH),
                in_=d["wqT"].rearrange("(a p) x -> p a x", p=128)[:, :, 512:1024]),
                waits=[quant_V[NCH - 1]], inc=chY)

    quant_sched = {2: [0], 3: [1], 4: [2, 3], 5: [4, 5], 6: [6, 7]}
    qdma_sched = {3: [2], 4: [3], 5: [4, 5], 6: [6, 7]}
    e_stats(1, bcastev_V[0])
    math_V[1] = e_math(1, (s1r, s2r, tr1, tr2), prev=[math_V[0], math_S[0]])
    e_bcast(1); e_xn(1)
    wv_done = e_wv_tail()
    e_kproj(0)
    e_scan()
    e_qdma(0); e_qdma(1)
    e_xg(2)
    e_stats(2, bcastev_V[1])
    math_V[2] = e_math(2, (s1r, s2r, tr1, tr2), prev=[math_V[1], math_S[1]])
    for g in range(1, NG1 + 1):
        for qi in qdma_sched.get(g, []):
            e_qdma(qi)
        e_vproj(g - 1)
        if g + 2 <= NG1 - 1:
            xg_dma_gate[g + 2] = vmm_T[g - 1]
            e_xg(g + 2)
        if g <= NG1 - 1:
            ex = e_ab_iter if g == 2 else (wq_hook if g == 7 else None)
            e_kproj(g, extra=ex)
            if g + 1 <= NG1 - 1:
                e_bcast(g + 1)
                e_xn(g + 1)
            if g + 2 <= NG1 - 1:
                e_stats(g + 2, bcastev_V[g + 1])
                math_V[g + 2] = e_math(g + 2, (s1r, s2r, tr1, tr2),
                                       prev=[math_V[g + 1], math_S[g + 1]])
        for qi in quant_sched.get(g, []):
            e_quant(qi)
    p1_last = {
        "wkb": kp7tok[0], "wvb": vmm_T[NG1 - 1],
        "xg0": vmm_T[6] + xn_V[6], "xg1": vmm_T[7] + xn_V[7],
        "xg2": vmm_T[5] + xn_V[5],
        "kst_even": kst_Y[6] + kst_S[6], "kst_odd": kst_Y[7] + kst_S[7],
        "sqb": stats_T[7] + sq_S[(7, 7)],
        "strow": bcast_T[7] + math_V[7],
        "xnrow": xn_V[7],
        "w32": quant_V[7], "lad": quant_V[7], "ab": quant_V[7],
        "psT01": bcastev_V[7], "psK": kst_S[7], "psV": vmm_S[7],
    }
    bar1 = (chT.tok() + chV.tok() + chS.tok() + chY.snap() + chG.snap() + chQ.snap())
    p1.close()

    p2a = ExitStack()

    def sb2(n, s, t):
        return _rec(n, "p2a", lambda: p2a.enter_context(nc.sbuf_tensor(n, s, t)))

    def ps2(n, s, t=F32):
        return p2a.enter_context(nc.psum_tensor(n, s, t))

    wq32 = sb2("wq32", [128, 4 * C], F32)
    wqb = sb2("wqb", [128, NCH * C], BF16)
    xga = [sb2(f"xga{i}", [128, NCH * G1], BF16) for i in range(3)]
    strow2 = sb2("strow2", [1, 4 * G1], F32)
    mbc2 = sb2("mbc2", [128, G1], F32)
    rbc2 = sb2("rbc2", [128, G1], F32)
    xntmp2 = sb2("xntmp2", [128, G1], F32)
    xntmp2_b = sb2("xntmp2_b", [128, G1], F32)
    sqa = [sb2(f"sqa{i}", [128, G1], BF16) for i in range(2)]
    xnt2 = [xntmp2, xntmp2_b]
    qstage = [sb2(f"qst{i}", [128, NCH * G1], BF16) for i in range(2)]
    ffstage = [sb2(f"ffst{i}", [128, C], F32) for i in range(2)]
    wq32b = sb2("wq32b", [128, 4 * C], F32)
    biasb = sb2("biasb_s", [128, C], F32)
    U0 = ps2("U0", [128, G1])
    U1 = ps2("U1", [128, G1])
    psQ = [ps2(f"psQ{i}", [128, G1]) for i in range(2)]
    psF = [[ps2(f"psF{i}_{h}", [128, 512]) for h in range(2)] for i in range(2)]
    s1a, s2a, ta1, ta2 = (strow2[0:1, i * G1:(i + 1) * G1] for i in range(4))
    WQ32R[0] = wq32
    WQ32R[1] = wq32b
    KRESR = [None]
    QTILER = [None, None]
    MASKTR = [None]
    kres_early = []
    qt_early = {}
    t_mk_early = [None]

    def _rng(n):
        return sb_off[n][0:2]
    assert _rng("wq32") == _rng("wkb"), (sb_off["wq32"], sb_off["wkb"])
    assert _rng("wqb") == _rng("wvb")
    for i in range(3):
        assert _rng(f"xga{i}") == _rng(f"xg{i}")
    assert _rng("strow2") == (_rng("kst01")[0], _rng("kst01")[0] + 8192)
    assert _rng("xntmp2_b")[1] == _rng("kst01")[1]
    assert _rng("sqa0") == _rng("sq0") and _rng("sqa1") == _rng("sq1")
    assert _rng("qst0") == _rng("strow")
    assert (_rng("qst1")[0], _rng("qst1")[1]) == (_rng("mbc_s")[0], _rng("xntmp_b")[1])
    assert _rng("ffst0") == _rng("w32_0") and _rng("ffst1") == _rng("w32_1")
    assert _rng("wq32b")[0] >= _rng("lad_u")[0] and _rng("wq32b")[1] <= _rng("ab_sb")[1]
    assert _rng("biasb_s")[1] <= _rng("ab_sb")[1]

    wq_cast = {}

    def e_wq_casts():
        for h in range(2):
            stg = wq32 if h == 0 else wq32b
            for q in range(2):
                dst = wqb[:, :].rearrange("p (a x) -> p a x", a=NCH)[
                    :, 4 * q:4 * (q + 1), h * 512:(h + 1) * 512]
                srcv = stg[:, :].rearrange("p (a t) -> p a t", a=NCH)[:, 4 * q:4 * (q + 1), :]
                tk = P.op("S", lambda e, dst=dst, srcv=srcv: e.copy(out=dst, in_=srcv),
                          waits=[wq_h0_Y[0] if h == 0 else wq_h1_Y[0],
                                 p1_last["wvb"] if (h == 0 and q == 0) else None], inc=chS)
            wq_cast[h] = tk

    t_bb = P.op("Y", lambda e: e.dma_start(out=biasb[:, :], in_=d["biasb"][:, :]),
                waits=[p1_last["lad"]], inc=chY)

    xga_dma, stats2_T, math2_V, bcast2_T, bcastev2_V, xn2_V = {}, {}, {}, {}, {}, {}
    fmm_T, qst_Y, qst_S, ff_Y, ff_V = {}, {}, {}, {}, {}
    math2_S, ffev, xn2_cc = {}, {}, {}
    sq2_T, sq2_S = {}, {}
    xga_gate = {}

    SL = (0, 2, 1, 0)
    SLGATE = {0: p1_last["xg0"], 2: p1_last["xg2"], 1: p1_last["xg1"]}

    def e_xga(b):
        w = [SLGATE[SL[b]] if b <= 2 else None, xga_gate.get(b)]
        xga_dma[b] = P.op("G", lambda e, b=b: e.dma_start(
            out=xga[SL[b]][:, :].rearrange("p (a t) -> p a t", a=NCH),
            in_=d["xcols"].rearrange("(a p) t -> p a t", p=128)[:, :, b * G1:(b + 1) * G1]),
            waits=w, inc=chG)

    def e_stats2(b, bankfree):
        for cc in range(NCH):
            wsq = [xga_dma[b]]
            if b == 0 and cc <= 1:
                wsq.append(p1_last["sqb"])
            key = (b, cc - 2) if cc >= 2 else (b - 1, cc + 6)
            if key in sq2_T:
                wsq += [sq2_T[key], sq2_S[key]]
            sq2_S[(b, cc)] = P.op("S", lambda e, b=b, cc=cc: e.activation(
                out=sqa[cc % 2][:, :], in_=xga[SL[b]][:, cc * G1:(cc + 1) * G1],
                func=AF.Square), waits=wsq, inc=chS)
            w = [xga_dma[b]]
            if cc == 0:
                w.append(bankfree)
            P.op("T", lambda e, b=b, cc=cc: e.matmul(
                U0[0:1, :], onescol[:, 0:1], xga[SL[b]][:, cc * G1:(cc + 1) * G1],
                start=(cc == 0), stop=(cc == NCH - 1), skip_group_check=True),
                waits=w, inc=chT)
            sq2_T[(b, cc)] = P.op("T", lambda e, b=b, cc=cc: e.matmul(
                U1[0:1, :], onescol[:, 0:1], sqa[cc % 2][:, :],
                start=(cc == 0), stop=(cc == NCH - 1), skip_group_check=True),
                waits=[sq2_S[(b, cc)]] + ([bankfree] if b == 0 and cc == 0 else []),
                inc=chT)
        stats2_T[b] = chT.tok()

    def e_math2(b, prev=()):
        reg = [p1_last["kst_even"]] if b == 0 else []
        v0 = P.op("V", lambda e: e.tensor_copy(out=s1a, in_=U0[0:1, :]),
                  waits=[stats2_T[b]] + list(prev) + reg, inc=chV)
        v1 = P.op("V", lambda e: e.tensor_copy(out=s2a, in_=U1[0:1, :]),
                  waits=reg, inc=chV)
        v2 = P.op("V", lambda e: e.tensor_tensor(
            out=ta1, in0=s1a, in1=s1a, op=ALU.mult), waits=[v0], inc=chV)
        v3 = P.op("V", lambda e: e.scalar_tensor_tensor(
            out=ta1, in0=s2a, scalar=float(C), in1=ta1, op0=ALU.mult, op1=ALU.subtract),
            waits=[v1, v2], inc=chV)
        v4 = P.op("V", lambda e: e.tensor_scalar(
            out=ta1, in0=ta1, scalar1=1.0 / (C * C), scalar2=EPS,
            op0=ALU.mult, op1=ALU.add), waits=[v3], inc=chV)
        s0 = P.op("S", lambda e: e.activation(out=ta1, in_=ta1, func=AF.Ln),
                  waits=[v4], inc=chS)
        s1x = P.op("S", lambda e: e.activation(
            out=ta2, in_=ta1, func=AF.Exp, scale=-0.5), waits=[s0], inc=chS)
        math2_S[b] = s1x
        math2_V[b] = P.op("V", lambda e: e.engine_nop(), waits=[s1x], inc=chV)

    def e_bcast2(b):
        P.op("T", lambda e: e.matmul(U0[:, :], invCrow[0:1, :], s1a,
                                     start=True, stop=True),
             waits=[math2_V[b], stats2_T[b], t_ic], inc=chT)
        bcast2_T[b] = P.op("T", lambda e: e.matmul(U1[:, :], onesrow[0:1, :], ta2,
                                                   start=True, stop=True), inc=chT)

    def e_xn2(b):
        reg = [p1_last["kst_odd"]] if b == 0 else []
        P.op("V", lambda e: e.tensor_copy(out=mbc2[:, :], in_=U0[:, :]),
             waits=[bcast2_T[b]] + reg, inc=chV)
        ev = P.op("V", lambda e: e.tensor_copy(out=rbc2[:, :], in_=U1[:, :]),
                  waits=reg, inc=chV)
        bcastev2_V[b] = ev
        mul_t = {}
        for c0 in range(0, NCH, 2):
            subs = []
            for cc in (c0, c0 + 1):
                w = ([ev] if cc < 2 else [mul_t[cc - 2]]) + (reg if cc <= 1 else [])
                subs.append(P.op("V", lambda e, b=b, cc=cc: e.tensor_tensor(
                    out=xnt2[cc % 2][:, :], in0=xga[SL[b]][:, cc * G1:(cc + 1) * G1],
                    in1=mbc2[:, :], op=ALU.subtract), waits=w, inc=chV))
            for k, cc in enumerate((c0, c0 + 1)):
                mul_t[cc] = P.op("V", lambda e, b=b, cc=cc: e.tensor_tensor(
                    out=xga[SL[b]][:, cc * G1:(cc + 1) * G1], in0=xnt2[cc % 2][:, :],
                    in1=rbc2[:, :], op=ALU.mult), waits=[subs[k]], inc=chV)
        xn2_V[b] = mul_t[NCH - 1]
        xn2_cc[b] = dict(mul_t)

    def e_qproj(b):
        ev = {}
        for dc in range(NCH):
            for cc in range(NCH):
                w = [xn2_cc[b][cc]] if dc == 0 else []
                if cc == 0:
                    if dc == 0:
                        w.append(wq_cast[0])
                        if b == 0:
                            w.append(p1_last["psK"])
                        if b >= 2:
                            w.append(qst_Y[b - 2])
                    if dc == 4:
                        w.append(wq_cast[1])
                    if dc >= 2:
                        w.append(ev[dc - 2])
                    elif b >= 1:
                        w.append(qst_S[b - 1])
                P.op("T", lambda e, b=b, dc=dc, cc=cc: e.matmul(
                    psQ[dc % 2][:, :],
                    wqb[:, :].rearrange("p (a x) -> p a x", a=NCH)[:, cc, dc * 128:(dc + 1) * 128],
                    xga[SL[b]][:, cc * G1:(cc + 1) * G1],
                    start=(cc == 0), stop=(cc == NCH - 1)), waits=w, inc=chT)
            tmm = chT.tok()
            wse = [tmm] + ([qst_S[b - 2]] if dc == 0 and b >= 2 else [])
            if b <= 1 and dc == 0:
                wse.append(p1_last["strow"] if b == 0 else p1_last["xnrow"])
            ev[dc] = P.op("S", lambda e, b=b, dc=dc: e.activation(
                out=qstage[b % 2][:, :].rearrange(
                    "p (o a t) -> p o a t", o=4, a=NCH)[:, :, dc, :],
                in_=psQ[dc % 2][:, :].rearrange("p (o t) -> p o t", o=4),
                func=AF.Copy, scale=SCALE),
                waits=wse, inc=chS)
        qst_S[b] = ev[NCH - 1]
        qst_Y[b] = P.op("Y", lambda e, b=b: e.dma_start(
            out=qT_h[b, :, :], in_=qstage[b % 2][:, :]),
            waits=[ev[NCH - 1]], inc=chY)

    def e_ffn(b):
        evs = {}
        for tt in range(4):
            j = 4 * b + tt
            for h in range(2):
                for cc in range(NCH):
                    w = [xn2_cc[b][cc]] if (tt == 0 and h == 0) else []
                    if cc == 0 and h == 0:
                        if b == 0 and tt == 0:
                            w += [quant_V[NCH - 1], p1_last["psV"]]
                        if tt >= 2:
                            w.append(evs[tt - 2])
                        elif b >= 1:
                            w.append(ff_V[b - 1])
                    P.op("T", lambda e, b=b, tt=tt, h=h, cc=cc: e.matmul(
                        psF[tt % 2][h][:, :],
                        xga[SL[b]][:, cc * G1 + tt * 128: cc * G1 + (tt + 1) * 128],
                        weff[:, :].rearrange("p (a x) -> p a x", a=NCH)[:, cc, h * 512:(h + 1) * 512],
                        start=(cc == 0), stop=(cc == NCH - 1)), waits=w, inc=chT)
            tmm = chT.tok()
            wv_ = [tmm, t_bb]
            if j == 0:
                wv_.append(p1_last["w32"])
            if j >= 2:
                wv_.append(ff_Y[j - 2])
            if j >= 2:
                wv_.append(ffev[j - 2])
            P.op("V", lambda e, tt=tt: e.tensor_tensor(
                out=ffstage[tt % 2][:, 0:512], in0=psF[tt % 2][0][:, :],
                in1=biasb[:, 0:512], op=ALU.add), waits=wv_, inc=chV)
            evs[tt] = P.op("V", lambda e, tt=tt: e.tensor_tensor(
                out=ffstage[tt % 2][:, 512:1024], in0=psF[tt % 2][1][:, :],
                in1=biasb[:, 512:1024], op=ALU.add), inc=chV)
            ffev[j] = evs[tt]
            ff_Y[j] = P.op("G", lambda e, j=j, tt=tt: e.dma_start(
                out=ff_h[j, :, :], in_=ffstage[tt % 2][:, :]),
                waits=[evs[tt]], inc=chG)
        fmm_T[b] = chT.tok()
        ff_V[b] = evs[3]

    e_xga(1)
    e_xga(0)
    e_stats2(0, p1_last["psT01"]); e_math2(0); e_bcast2(0); e_xn2(0)
    e_wq_casts()
    kres_early.append(P.op("Y", lambda e: e.dma_start(
        out=KRESR[0][:, 0:NCH * 512], in_=kT_h[0, :, :]),
        waits=[wq_cast[0], kst_Y[0]], inc=chY))
    kres_early.append(P.op("Y", lambda e: e.dma_start(
        out=KRESR[0][:, NCH * 512:], in_=kT_h[1, :, :]),
        waits=[kst_Y[1]], inc=chY))
    e_stats2(1, bcastev2_V[0]); e_math2(1, prev=[math2_V[0], math2_S[0]])
    for b in range(NB2):
        if b >= 1:
            e_ffn(b - 1)
        e_qproj(b)
        if b == NB2 - 1:
            qp3_T = chT.tok()
            ORD_ = sorted(range(NSLOT), key=lambda j: -slot_kcap(j))
            for si_ in range(2):
                j_ = ORD_[si_]
                qt_early[si_] = P.op("Y", lambda e, si_=si_, j_=j_: e.dma_start(
                    out=QTILER[si_][:, :],
                    in_=qT_h[j_ // 4, :, :][:, (j_ % 4) * 1024:(j_ % 4 + 1) * 1024]),
                    waits=[qp3_T, qst_Y[j_ // 4]], inc=chY)
            t_mk_early[0] = P.op("Y", lambda e: e.dma_start(
                out=MASKTR[0][:, :], in_=d["maskt"][:, :]), waits=[qp3_T], inc=chY)
        if b + 1 <= NB2 - 1:
            e_bcast2(b + 1)
            e_xn2(b + 1)
        if b + 2 <= NB2 - 1:
            xga_gate[b + 2] = fmm_T[b - 1] if b >= 1 else None
            e_xga(b + 2)
            e_stats2(b + 2, bcastev2_V[b + 1])
            e_math2(b + 2, prev=[math2_V[b + 1], math2_S[b + 1]])
    e_ffn(NB2 - 1)
    bar2 = (chT.tok() + chV.tok() + chS.tok() + chY.snap() + chG.snap() + chQ.snap())
    p2a.close()

    pw.close()
    p2b = ExitStack()

    def sb3(n, s, t):
        return _rec(n, "p2b", lambda: p2b.enter_context(nc.sbuf_tensor(n, s, t)))

    def ps3(n, s, t=F32):
        return p2b.enter_context(nc.psum_tensor(n, s, t))

    k_res = sb3("k_res", [128, 2 * NCH * 512], BF16)
    qtile = [sb3(f"qtile{i}", [128, NCH * 128], BF16) for i in range(4)]
    maskt = sb3("maskt_s", [128, 256], F32)
    scores = [[sb3(f"scores{p}_{i}", [128, 4096], BF16) for i in range(2)]
              for p in range(2)]
    pbuf = [sb3(f"pbuf{i}", [128, 4096], BF16) for i in range(2)]
    kbuf = [sb3(f"kbuf{i}", [128, NCH * 512], BF16) for i in range(KRING)]
    fftile = [sb3(f"fft{i}", [128, C], F32) for i in range(3)]
    xrtile = [sb3(f"xrt{i}", [128, C], F32) for i in range(3)]
    outsb = [sb3(f"outsb{i}", [128, C], F32) for i in range(2)]
    ptb = [sb3(f"ptb{i}", [128, 128], BF16) for i in range(3)]
    mxv = sb3("mxv", [128, 1], F32)
    nbias = sb3("nbias", [128, 1], F32)
    lsum = sb3("lsum", [128, 4], F32)
    invl = [sb3(f"invl{i}", [128, 1], F32) for i in range(2)]

    psS = [ps3(f"psS{a}", [128, 512]) for a in range(2)]
    psT_ = [ps3(f"psT{i}", [128, 128], BF16) for i in range(2)]
    psAV = [[ps3(f"psAV{i}_{h}", [128, 512]) for h in range(2)] for i in range(2)]

    ORDER = sorted(range(NSLOT), key=lambda j: -slot_kcap(j))
    NP2 = NSLOT // 2

    def pr_slots(k):
        return ORDER[2 * k], ORDER[2 * k + 1]

    KQ = []
    for k in range(NP2):
        jA, _ = pr_slots(k)
        for ci, (off, wdt) in enumerate(score_chunks(slot_kcap(jA))):
            if off >= 1024:
                KQ.append((k, ci, off, wdt))
    kq_pos = {(k, ci): q for q, (k, ci, _, _) in enumerate(KQ)}
    kdma, kcons = {}, {}
    kq_emit = [0]
    mxcols = [sb3(f"mxcols{i}", [128, 8], F32) for i in range(2)]

    KRESR[0] = k_res
    QTILER[0], QTILER[1] = qtile[0], qtile[1]
    MASKTR[0] = maskt
    assert _rng("k_res") == _rng("wq32")
    assert _rng("qtile0")[0] == _rng("wqb")[0] and _rng("maskt_s")[1] <= _rng("wqb")[1]

    kres_dma = kres_early
    t_mk = t_mk_early[0]

    kbuf_gate = {0: stats2_T[NB2 - 1] + qst_Y[2],
                 1: qst_Y[2] + qst_Y[3],
                 2: qst_Y[3] + ff_Y[14] + ff_Y[15]}

    def pump_k(upto):
        while kq_emit[0] < min(upto, len(KQ)):
            q = kq_emit[0]
            k, ci, off, wdt = KQ[q]
            g = off // 512
            buf = q % KRING
            w = [kbuf_gate[q]] if q < KRING else [kcons[q - KRING]]
            qname, qch = "Y", chY
            if wdt == 512:
                v = P.op(qname, lambda e, g=g, buf=buf: e.dma_start(
                    out=kbuf[buf][:, :], in_=kT_h[g, :, :]), waits=w, inc=qch)
            else:
                v = P.op(qname, lambda e, g=g, buf=buf: e.dma_start(
                    out=kbuf[buf][:, :].rearrange("p (a t) -> p a t", a=NCH)[:, :, 0:256],
                    in_=kT_h[g, :, :].rearrange("p (a t) -> p a t", a=NCH)[:, :, 0:256]),
                    waits=w, inc=qch)
            kdma[q] = (v, buf)
            kq_emit[0] += 1

    qt_dma, ff_dma, xr_dma = {}, {}, {}
    qt_cons, ff_cons, xr_cons = {}, {}, {}
    sc_mx_V, exp_S, inv_V, epi_V, mx_fin, exp_half = {}, {}, {}, {}, {}, {}
    trans_cons, av_T, out_G, sc_fin_T = {}, {}, {}, {}
    sc_prev = {}
    ptbg = {}

    def e_sdma_qt(k):
        for a in range(2):
            j = pr_slots(k)[a]
            si = 2 * k + a
            if si in qt_early:
                qt_dma[si] = qt_early[si]
                continue
            w0 = [qp3_T, qst_Y[j // 4]] if k < 2 else []
            qt_dma[si] = P.op("Y", lambda e, si=si, j=j: e.dma_start(
                out=qtile[si % 4][:, :],
                in_=qT_h[j // 4, :, :][:, (j % 4) * 1024:(j % 4 + 1) * 1024]),
                waits=w0 + ([qt_cons[si - 4]] if si >= 4 else []), inc=chY)

    def e_sdma_fx(k):
        w0 = [bar2] if k < 2 else []
        for a in range(2):
            j = pr_slots(k)[a]
            si = 2 * k + a
            ff_dma[si] = P.op("G", lambda e, si=si, j=j: e.dma_start(
                out=fftile[si % 3][:, :], in_=ff_h[j, :, :]),
                waits=w0 + ([ff_cons[si - 3]] if si >= 3 else []), inc=chG)
            xr_dma[si] = P.op("G", lambda e, si=si, j=j: e.dma_start(
                out=xrtile[si % 3][:, :], in_=d["xrows"][j * 128:(j + 1) * 128, :]),
                waits=w0 + ([xr_cons[si - 3]] if si >= 3 else []), inc=chG)

    def e_scores(k):
        jA, jB = pr_slots(k)
        kcapA, kcapB = slot_kcap(jA), slot_kcap(jB)
        chunks = score_chunks(kcapA)
        nchA = len(chunks)
        nchB = len(score_chunks(kcapB))
        kp = k % 2
        mx_prevA = mx_fin.get(2 * (k - 1))
        mx_prevB = mx_fin.get(2 * (k - 1) + 1)
        for ci, (off, wdt) in enumerate(chunks):
            if off < 1024:
                q = None
                v = kres_dma[0] if off < 512 else (kres_dma[0] + kres_dma[1])
                buf = -1
            else:
                q = kq_pos[(k, ci)]
                pump_k(q + 1)
                v, buf = kdma[q]
            for a, (jx, kcx, mx_prev) in enumerate(
                    [(jA, kcapA, mx_prevA), (jB, kcapB, mx_prevB)]):
                wdx = min(wdt, kcx - off)
                if wdx <= 0:
                    continue
                si = 2 * k + a
                for dc in range(NCH):
                    w = []
                    if dc == 0:
                        w.append(v)
                        prev = sc_prev.get(a)
                        if prev is not None:
                            w.append(prev)
                        if ci == 0:
                            w.append(qt_dma[si])
                            if k == 0:
                                w.append(bcastev2_V[NB2 - 1])
                    if buf < 0:
                        kap = k_res[:, :].rearrange("p (g a t) -> p g a t", g=2, a=NCH)[
                            :, off // 512, dc, 0:wdx]
                    else:
                        kap = kbuf[buf][:, :].rearrange("p (a t) -> p a t", a=NCH)[
                            :, dc, 0:wdx]
                    P.op("T", lambda e, si=si, a=a, dc=dc, wdx=wdx, kap=kap: e.matmul(
                        psS[a][:, 0:wdx],
                        qtile[si % 4][:, dc * 128:(dc + 1) * 128],
                        kap,
                        start=(dc == 0), stop=(dc == NCH - 1)), waits=w, inc=chT)
                if a == 1 and q is not None:
                    kcons[q] = chT.tok()
                wev = [chT.tok()]
                if ci == 0 and k >= 1:
                    wev.append(sc_mx_V.get(2 * (k - 1) + a))
                if ci == 0 and k >= 2:
                    wev.append(exp_S.get(2 * (k - 2) + a))
                nchx = nchA if a == 0 else nchB
                if ci < nchx - 1:
                    ev = P.op("V", lambda e, kp=kp, a=a, off=off, wdx=wdx: e.tensor_copy(
                        out=scores[kp][a][:, off:off + wdx], in_=psS[a][:, 0:wdx]),
                        waits=wev, inc=chV)
                    sc_prev[a] = ev
                    P.op("V", lambda e, kp=kp, a=a, ci=ci, off=off, wdx=wdx: e.reduce_max(
                        mxcols[a][:, ci:ci + 1], scores[kp][a][:, off:off + wdx], axis=AX.X),
                        waits=[ev, mx_prev], inc=chV)
                else:
                    evl = []
                    if wdx > 256:
                        evl.append(P.op("V", lambda e, kp=kp, a=a, off=off, wdx=wdx:
                                        e.tensor_copy(
                            out=scores[kp][a][:, off:off + wdx - 256],
                            in_=psS[a][:, 0:wdx - 256]), waits=wev, inc=chV))
                    evl.append(P.op("V", lambda e, kp=kp, a=a, kcx=kcx, wdx=wdx:
                                    e.tensor_tensor(
                        out=scores[kp][a][:, kcx - 256:kcx],
                        in0=psS[a][:, wdx - 256:wdx],
                        in1=maskt[:, :], op=ALU.add),
                        waits=wev + [t_mk, mx_prev], inc=chV))
                    ev = evl[-1]
                    sc_prev[a] = ev
                    P.op("V", lambda e, kp=kp, a=a, ci=ci, off=off, wdx=wdx: e.reduce_max(
                        mxcols[a][:, ci:ci + 1], scores[kp][a][:, off:off + wdx], axis=AX.X),
                        waits=evl + [mx_prev], inc=chV)
                if a == 0:
                    sc_mx_V[2 * k] = chV.tok()
                else:
                    sc_mx_V[2 * k + 1] = chV.tok()
            if kcapB - off <= 0 and q is not None:
                kcons[q] = chT.tok()
            if q is not None:
                pump_k(q + KRING + 1)
        sc_fin_T[k] = chT.tok()
        qt_cons[2 * k] = sc_fin_T[k]
        qt_cons[2 * k + 1] = sc_fin_T[k]

    def e_sm_pre(si):
        a = si % 2
        kcap = slot_kcap(ORDER[si])
        nch = len(score_chunks(kcap))
        v1 = P.op("V", lambda e, a=a, nch=nch: e.reduce_max(
            nbias[:, :], mxcols[a][:, 0:nch], axis=AX.X, negate=True),
            waits=[sc_mx_V[si], exp_S.get(si - 1)], inc=chV)
        mx_fin[si] = v1
        w = [v1, exp_S.get(si - 1)]
        if si >= 2:
            w.append(trans_cons.get(si - 2))
        half = (kcap // 256) * 128
        kp = (si // 2) % 2
        exp_half[si] = P.op("S", lambda e, kp=kp, a=a, half=half: e.activation(
            out=pbuf[a][:, 0:half], in_=scores[kp][a][:, 0:half], func=AF.Exp,
            bias=nbias[:, 0:1], scale=1.0, accum_out=lsum[:, 2 * a:2 * a + 1]), waits=w, inc=chS)
        exp_S[si] = P.op("S", lambda e, kp=kp, a=a, half=half, kcap=kcap: e.activation(
            out=pbuf[a][:, half:kcap], in_=scores[kp][a][:, half:kcap], func=AF.Exp,
            bias=nbias[:, 0:1], scale=1.0, accum_out=lsum[:, 2 * a + 1:2 * a + 2]), inc=chS)

    def e_sm_post(si):
        a = si % 2
        vsum = P.op("V", lambda e, a=a: e.tensor_tensor(
            out=lsum[:, 2 * a:2 * a + 1], in0=lsum[:, 2 * a:2 * a + 1],
            in1=lsum[:, 2 * a + 1:2 * a + 2], op=ALU.add),
            waits=[exp_S[si], inv_V.get(si - 1)], inc=chV)
        inv_V[si] = P.op("V", lambda e, si=si, a=a: e.reciprocal(
            out=invl[si % 2][:, :], in_=lsum[:, 2 * a:2 * a + 1]),
            waits=[vsum, epi_V.get(si - 2)], inc=chV)

    def e_trans_av(si):
        a = si % 2
        kcap = slot_kcap(ORDER[si])
        nk = kcap // 128
        ptb_ev = {}
        av_lastk = {}

        half_nk = (kcap // 256)
        def em_tr(kc):
            w = []
            if kc == 0:
                w = [exp_half[si], t_idb]
                if si == 0:
                    w.append(qst_S[NB2 - 1])
            elif kc == half_nk:
                w = [exp_S[si]]
            tt = P.op("T", lambda e, a=a, kc=kc: e.transpose(
                psT_[kc % 2][:, :], pbuf[a][:, kc * 128:(kc + 1) * 128], identb[:, :]),
                waits=w, inc=chT)
            wv = [tt, ptbg.get(kc % 3)]
            if kc >= 3:
                wv.append(av_lastk[kc - 3])
            ptb_ev[kc] = P.op("S", lambda e, kc=kc: e.copy(
                out=ptb[kc % 3][:, :], in_=psT_[kc % 2][:, :]), waits=wv, inc=chS)
            ptbg[kc % 3] = ptb_ev[kc]

        def em_av(kc):
            for h in range(2):
                w2 = [ptb_ev[kc]]
                if kc == 0 and h == 0 and si == 0:
                    w2.append(ffev[4 * NB2 - 1])
                if kc == 0 and h == 0 and si >= 2:
                    w2.append(epi_V.get(si - 2))
                av_lastk[kc] = P.op("T", lambda e, si=si, kc=kc, h=h, nk=nk: e.matmul(
                    psAV[si % 2][h][:, :], ptb[kc % 3][:, :],
                    v_res[:, kc * C + h * 512: kc * C + (h + 1) * 512],
                    start=(kc == 0), stop=(kc == nk - 1), skip_group_check=True),
                    waits=w2, inc=chT)

        em_tr(0)
        for kc in range(1, nk):
            em_tr(kc)
            em_av(kc - 1)
        em_av(nk - 1)
        trans_cons[si] = chT.tok()
        av_T[si] = chT.tok()

    def e_epilogue(si):
        j = ORDER[si]
        w = [av_T[si], inv_V[si], ff_dma[si], xr_dma[si]]
        if si >= 2:
            w.append(out_G[si - 2])
        P.op("V", lambda e, si=si: e.scalar_tensor_tensor(
            out=outsb[si % 2][:, 0:512], in0=psAV[si % 2][0][:, :], scalar=invl[si % 2][:, 0:1],
            in1=fftile[si % 3][:, 0:512], op0=ALU.mult, op1=ALU.add), waits=w, inc=chV)
        v2 = P.op("V", lambda e, si=si: e.scalar_tensor_tensor(
            out=outsb[si % 2][:, 512:1024], in0=psAV[si % 2][1][:, :], scalar=invl[si % 2][:, 0:1],
            in1=fftile[si % 3][:, 512:1024], op0=ALU.mult, op1=ALU.add), inc=chV)
        ff_cons[si] = v2
        v3 = P.op("V", lambda e, si=si: e.tensor_tensor(
            out=outsb[si % 2][:, :], in0=outsb[si % 2][:, :], in1=xrtile[si % 3][:, :],
            op=ALU.add), waits=[v2], inc=chV)
        xr_cons[si] = v3
        epi_V[si] = v3
        out_G[si] = P.op("G", lambda e, j=j, si=si: e.dma_start(
            out=out_h[j * 128:(j + 1) * 128, :], in_=outsb[si % 2][:, :]),
            waits=[v3], inc=chG)

    e_sdma_qt(0)
    e_sdma_fx(0)
    e_scores(0)
    for k in range(NP2):
        e_sm_pre(2 * k)
        e_sm_pre(2 * k + 1)
        if k + 1 < NP2:
            e_sdma_qt(k + 1)
            e_scores(k + 1)
        e_sm_post(2 * k)
        e_sm_post(2 * k + 1)
        e_trans_av(2 * k)
        e_epilogue(2 * k)
        if k + 1 < NP2:
            e_sdma_fx(k + 1)
        e_trans_av(2 * k + 1)
        e_epilogue(2 * k + 1)

    fin = chG.snap() + chY.snap() + chQ.snap()
    P.op("G", lambda e: e.nop(), waits=[fin])
    p2b.close()

    sems = {}
    for c in chans:
        for n in c.sem_names():
            sems[n] = es.enter_context(nc.semaphore(n))
    block = es.enter_context(nc.Block())

    @block.tensor
    def _(t):
        P.replay("T", t, sems)

    @block.vector
    def _(v):
        P.replay("V", v, sems)

    @block.scalar
    def _(s):
        P.replay("S", s, sems)

    @block.sync
    def _(sy):
        P.replay("Y", sy, sems)

    @block.gpsimd
    def _(g):
        P.replay("G", g, sems)

    es.close()
    return nc



_NC_CACHE = {}


def _get_nc():
    if "nc" not in _NC_CACHE:
        _NC_CACHE["nc"] = build_nc()
    return _NC_CACHE["nc"]


def kernel(L0_attn_x, L0_attn_Wq, L0_attn_Wk, L0_attn_Wv,
           L0_ffn_kernel, L0_ffn_bias, L0_ffn_a, L0_ffn_b):
    x = np.ascontiguousarray(np.asarray(L0_attn_x), dtype=np.float32)
    Wq = np.asarray(L0_attn_Wq, dtype=np.float32)
    Wk = np.asarray(L0_attn_Wk, dtype=np.float32)
    Wv = np.asarray(L0_attn_Wv, dtype=np.float32)
    Wker = np.ascontiguousarray(np.asarray(L0_ffn_kernel), dtype=np.float32)
    bias = np.asarray(L0_ffn_bias, dtype=np.float32)
    a_m = np.asarray(L0_ffn_a, dtype=np.float32)
    b_m = np.ascontiguousarray(np.asarray(L0_ffn_b), dtype=np.float32)

    wqT = np.ascontiguousarray(Wq.T)
    wkT = np.ascontiguousarray(Wk.T)
    wvT = np.ascontiguousarray(Wv.T)
    aT = np.ascontiguousarray(a_m.T)
    biasb = np.ascontiguousarray(np.broadcast_to(bias[None, :], (128, C)))
    identb = np.eye(128, dtype=ml_dtypes.bfloat16)
    identf = np.eye(128, dtype=np.float32)
    onesrow = np.ones((1, 128), np.float32)
    onescol = np.ones((128, 1), np.float32)

    tri = np.tril(np.ones((128, 128), np.float32))
    diag_m = np.where(tri > 0, 0.0, NEG).astype(np.float32)
    full_m = np.full((128, 128), NEG, np.float32)
    zero_m = np.zeros((128, 128), np.float32)
    mask_par = [np.ascontiguousarray(np.concatenate([diag_m, full_m], axis=1)),
                np.ascontiguousarray(np.concatenate([zero_m, diag_m], axis=1))]

    nc = _get_nc()
    in_maps, rows_per_core = [], []
    for core in range(8):
        b, par = core // 2, core % 2
        tiles = [slot_tile(j, par) for j in range(NSLOT)]
        own = np.concatenate([np.arange(128 * t, 128 * (t + 1)) for t in tiles])
        rows_per_core.append(own)
        xseq = x[b]
        in_maps.append({
            "xfm": np.ascontiguousarray(xseq.T),
            "xcols": np.ascontiguousarray(xseq[own].T),
            "xrows": np.ascontiguousarray(xseq[own]),
            "wqT": wqT, "wkT": wkT, "wvT": wvT,
            "wker": Wker, "aT": aT, "bmat": b_m,
            "biasb": biasb, "maskt": mask_par[par],
            "identb": identb, "identf": identf,
            "onesrow": onesrow, "onescol": onescol,
            "invCrow": np.full((1, 128), 1.0 / C, np.float32),
        })

    res = run_bass_kernel_spmd(nc, in_maps, core_ids=list(range(8)))
    out = np.empty((4, T_TOK, C), np.float32)
    for core in range(8):
        out[core // 2, rows_per_core[core]] = res.results[core]["out"]
    return out
